# version 06
# baseline (speedup 1.0000x reference)
import sys
sys.path.insert(0, "/opt/trn_rl_repo")
import numpy as np
import ml_dtypes
from contextlib import ExitStack
import concourse.bass as bass
import concourse.mybir as mybir
from concourse.bass_core_rust import IndirectOffsetOnAxis
from concourse.bass_utils import run_bass_kernel_spmd

F32 = mybir.dt.float32
BF16 = mybir.dt.bfloat16
U32 = mybir.dt.uint32
AF = mybir.ActivationFunctionType
ALU = mybir.AluOpType
AX = mybir.AxisListType.X

NC = 8
NTOK = 4096
NT = NTOK // 128
DIM = 1024
HID = 4096
HEADS, HD = 16, 64
LR = 0.1
NS_ITERS, TTT_STEPS = 5, 2
NDTOT = 32768 * DIM
E_OWN, CAP, SINK_IT, BIS_IT = 2, 256, 6, 26
RG = [list(range(NC))]
NS_A, NS_B, NS_C = 3.4445, -4.775, 2.0315


class Ch:
    def __init__(self, nc, ctx, name):
        self.sem = ctx.enter_context(nc.semaphore(name))
        self.n = 0

    def inc(self, instr, k=1):
        instr.then_inc(self.sem, k)
        self.n += k
        return self.n


class B:
    def __init__(self, dbg=(), race=True):
        self.dbg = set(dbg)
        self.nc = bass.Bass("TRN2", target_bir_lowering=False, num_devices=NC,
                            detect_race_conditions=race)
        self.ctx = ExitStack()
        self.dram = {}

    def din(self, name, shape, dt=F32):
        self.dram[name] = self.nc.dram_tensor(name, list(shape), dt, kind="ExternalInput")

    def dout(self, name, shape, dt=F32):
        self.dram[name] = self.nc.dram_tensor(name, list(shape), dt, kind="ExternalOutput")

    def dint(self, name, shape, dt=F32):
        self.dram[name] = self.nc.dram_tensor(name, list(shape), dt, kind="Internal")

    _uid = 0

    def _nm(self, name):
        B._uid += 1
        return f"{name}_{B._uid}"

    def sb(self, name, shape, dt=F32, stack=None):
        return (stack or self.ctx).enter_context(
            self.nc.sbuf_tensor(self._nm(name), list(shape), dt))

    def psum(self, name, shape, dt=F32, stack=None):
        return (stack or self.ctx).enter_context(
            self.nc.psum_tensor(self._nm(name), list(shape), dt))

    def ap(self, name):
        return self.dram[name].ap()

    def tm(self, name):
        return self.ap(name).rearrange("(t p) f -> p t f", p=128)

    def declare_io(self):
        d = self.din
        d("x", (NTOK, DIM))
        d("qkv_w", (DIM, 3 * DIM))
        d("qkv_b", (128, 3 * DIM))
        d("qn_scale", (128, DIM))
        d("qn_bias", (128, DIM))
        d("kn_scale", (128, DIM))
        d("kn_bias", (128, DIM))
        d("ttt_gate", (DIM, HID))
        d("ttt_val", (DIM, HID))
        d("ttt_projT", (DIM, HID))
        d("proj_w", (DIM, DIM))
        d("proj_b", (128, DIM))
        d("gate_queryT", (DIM, 16))
        d("log_temp", (1, 1))
        d("w1", (E_OWN, DIM, HID))
        d("b1c", (E_OWN, 128, HID // 128))
        d("w2", (E_OWN, HID, DIM))
        d("b2t", (E_OWN, 128, DIM))
        d("cconst", (1, 8))
        d("gate_sh", (DIM, 512))
        d("val_sh", (DIM, 512))
        d("proj_sh", (DIM, 512))
        self.dout("out", (NTOK, DIM))

        di = self.dint
        di("ln0b", (NTOK, DIM), BF16)
        di("q_tok", (NTOK, DIM), BF16)
        di("k_tok", (NTOK, DIM), BF16)
        di("vc_tok", (NTOK, DIM), BF16)
        di("h_tok", (NTOK, HID), BF16)
        di("sw_tok", (NTOK, HID), BF16)
        di("uswp_tok", (NTOK, HID), BF16)
        di("R_tok", (NTOK, DIM), BF16)
        di("dV_tok", (NTOK, HID), BF16)
        di("dG_tok", (NTOK, HID), BF16)
        di("predq_tok", (NTOK, DIM), BF16)
        di("G_work", (8, DIM, 512), BF16)
        di("V_work", (8, DIM, 512), BF16)
        di("PT_work", (8, DIM, 512), BF16)
        di("G_m", (DIM, 512))
        di("V_m", (DIM, 512))
        di("PT_m", (DIM, 512))
        for g in ("gP", "gV", "gG"):
            di(g + "_part", (HID, DIM), BF16)
            di(g + "_rs", (HID // NC, DIM), BF16)
        for p in range(3):
            di(f"nsAp_{p}", (DIM, DIM), BF16)
            di(f"nsA_{p}", (DIM, DIM), BF16)
            di(f"nsXsp_{p}", (DIM, 512), BF16)
        di("norm_part", (1, 3))
        di("norm_out", (1, 3))
        di("x1_tok", (NTOK, DIM))
        di("ln1_tok", (NTOK, DIM), BF16)
        di("ln1_all", (NC * NTOK, DIM), BF16)
        di("logitsT_own", (16, NTOK))
        di("logitsT_all", (NC * 16, NTOK))
        di("trank_own", (16, 8192))
        di("idx_own", (8, CAP), U32)
        di("idx_all", (128, CAP), U32)
        di("gates_own", (8, CAP))
        di("gates_all", (NC * 8, CAP))
        di("eo_own", (8, CAP, DIM), BF16)
        di("eo_all", (NC * 8, CAP, DIM), BF16)
        di("acc", (NTOK + 256, DIM))
        di("sink_dump", (64, 8192))
        di("rank_dump", (64, 8192))

    def barrier(self, block):
        chans = [self.cT, self.cV, self.cA, self.cG, self.dY, self.dS, self.dG, self.cC]
        vals = [c.n for c in chans]

        def mk(ch):
            def f(eng):
                eng.drain()
                ch.inc(eng.nop())
                for c, v in zip(chans, vals):
                    if v > 0:
                        eng.wait_ge(c.sem, v)
            return f
        block.tensor(mk(self.cT))
        block.vector(mk(self.cV))
        block.scalar(mk(self.cA))
        block.gpsimd(mk(self.cG))

        def fy(eng):
            for c, v in zip(chans, vals):
                if v > 0:
                    eng.wait_ge(c.sem, v)
        block.sync(fy)

    def expect(self, ch, val, tag):
        assert ch.n == val, f"{tag}: predicted {val} != actual {ch.n}"

    def build(self):
        nc = self.nc
        self.declare_io()
        ctx = self.ctx
        self.dY = Ch(nc, ctx, "dY")
        self.dS = Ch(nc, ctx, "dS")
        self.dG = Ch(nc, ctx, "dG")
        self.cT = Ch(nc, ctx, "cT")
        self.cV = Ch(nc, ctx, "cV")
        self.cA = Ch(nc, ctx, "cA")
        self.cG = Ch(nc, ctx, "cG")
        self.cC = Ch(nc, ctx, "cC")

        self.konst = self.sb("konst", (128, 4))
        self.ident = self.sb("ident", (128, 128), BF16)
        self.identf = self.sb("identf", (128, 128), F32)
        self.iota_cap = self.sb("iota_cap", (128, 256))
        self.iota_np = self.sb("iota_np", (128, 64))

        with nc.Block() as block:
            @block.vector
            def _(v):
                v.memset(self.konst[:, 0:1], 1e-5)
                v.memset(self.konst[:, 1:2], 1e-6)
                v.memset(self.konst[:, 2:4], 0.0)
                v.memset(self.ident[:], 0.0)
                v.memset(self.identf[:], 0.0)
                self.cV.inc(v.drain())

            @block.gpsimd
            def _(g):
                g.wait_ge(self.cV.sem, self.cV.n)
                for t in (self.ident, self.identf):
                    g.affine_select(out=t[:], in_=t[:],
                                    compare_op=ALU.not_equal, fill=1.0, base=0,
                                    pattern=[[-1, 128]], channel_multiplier=1)
                g.iota(self.iota_cap[:], pattern=[[1, 256]], base=1,
                       channel_multiplier=0, allow_small_or_imprecise_dtypes=True)
                g.iota(self.iota_np[:], pattern=[[128, 64]], base=0,
                       channel_multiplier=1, allow_small_or_imprecise_dtypes=True)
                for src, dst in (("ttt_gate", "G_work"), ("ttt_val", "V_work"),
                                 ("ttt_projT", "PT_work")):
                    for b_ in range(8):
                        ins = g.dma_start(out=self.ap(dst)[b_],
                                          in_=self.ap(src)[:, b_ * 512:(b_ + 1) * 512])
                        self.dG.inc(ins, 16)
                self.cG.inc(g.nop())
                self.init_done = self.cG.n

            with ExitStack() as ps:
                self.phase_a1(block, ps)
                self.barrier(block)
            with ExitStack() as ps:
                self.phase_a2(block, ps)
                self.barrier(block)
            for step in range(TTT_STEPS):
                with ExitStack() as ps:
                    self.ttt_F(block, ps, step)
                    self.barrier(block)
                with ExitStack() as ps:
                    self.ttt_P(block, ps, step)
                    self.barrier(block)
                with ExitStack() as ps:
                    self.ttt_D(block, ps, step)
                    self.barrier(block)
                for xn, rn, on in (("h_tok", "R_tok", "gP_part"),
                                   ("dV_tok", "k_tok", "gV_part"),
                                   ("dG_tok", "k_tok", "gG_part")):
                    with ExitStack() as ps:
                        self.ttt_G1(block, ps, xn, rn, on)
                        self.barrier(block)
                self.ttt_RS(block)
                self.barrier(block)
                with ExitStack() as ps:
                    self.ttt_NS(block, ps, step)
                    self.barrier(block)
            with ExitStack() as ps:
                self.ttt_F(block, ps, 0, query=True)
                self.barrier(block)
            with ExitStack() as ps:
                self.ttt_P(block, ps, 0, query=True)
                self.barrier(block)
            with ExitStack() as ps:
                self.ttt_O(block, ps)
                self.barrier(block)
            with ExitStack() as ps:
                self.ttt_L(block, ps)
                self.barrier(block)
            if "skipffn" not in self.dbg:
                self.run_ffn(block)
            self.emit_debug(block)
        return nc

    def run_ffn(self, block):
        if True:
            with ExitStack() as ps:
                self.ffn_R(block, ps)
                self.barrier(block)
            with ExitStack() as ps:
                self.ffn_E(block, ps)
                self.barrier(block)
            with ExitStack() as ps:
                self.ffn_C(block, ps)
                self.barrier(block)

    def phase_a1(self, block, ps):
        nc = self.nc
        dY, dS, dG = self.dY, self.dS, self.dG
        cT, cV, cA, cG = self.cT, self.cV, self.cA, self.cG

        xin = self.sb("pa_x", (128, 2, DIM), stack=ps)
        ln0t = self.sb("pa_ln0", (128, 2, DIM), BF16, stack=ps)
        stat = self.sb("pa_stat", (128, 2, 16), stack=ps)
        sq = self.sb("pa_sq", (128, DIM), stack=ps)
        xc = self.sb("pa_xc", (128, 2, DIM), stack=ps)

        x_ld = []
        v1b, a1b, ds1b = cV.n, cA.n, dS.n

        @block.sync
        def _(y):
            xv = self.tm("x")
            for t in range(NT):
                if t >= 2:
                    y.wait_ge(cV.sem, v1b + (t - 1))
                    y.wait_ge(dS.sem, ds1b + 16 * (t - 1))
                ins = y.dma_start(out=xin[:, t % 2, :], in_=xv[:, t, :])
                x_ld.append(dY.inc(ins, 16))

        @block.vector
        def _(v):
            for t in range(NT):
                v.wait_ge(dY.sem, x_ld[t])
                if t >= 2:
                    v.wait_ge(cA.sem, a1b + (t - 1))
                st = stat[:, t % 2, :]
                xt = xin[:, t % 2, :]
                v.tensor_tensor(sq[:], xt, xt, ALU.mult)
                v.reduce_sum(st[:, 0:1], xt, axis=AX)
                v.drain()
                v.reduce_sum(st[:, 1:2], sq[:], axis=AX)
                v.tensor_scalar(st[:, 2:3], st[:, 0:1], 1.0 / DIM, None, ALU.mult)
                v.drain()
                v.tensor_scalar(st[:, 3:4], st[:, 1:2], 1.0 / DIM, None, ALU.mult)
                v.tensor_tensor(st[:, 4:5], st[:, 2:3], st[:, 2:3], ALU.mult)
                v.tensor_scalar(xc[:, t % 2, :], xt, st[:, 2:3], None, ALU.subtract)
                v.drain()
                v.tensor_tensor(st[:, 5:6], st[:, 3:4], st[:, 4:5], ALU.subtract)
                cV.inc(v.drain())

        @block.scalar
        def _(a):
            a.wait_ge(cG.sem, self.init_done)
            for t in range(NT):
                a.wait_ge(cV.sem, v1b + (t + 1))
                if t >= 2:
                    a.wait_ge(dS.sem, ds1b + 16 * (t - 1))
                st = stat[:, t % 2, :]
                a.activation(out=st[:, 6:7], in_=st[:, 5:6], func=AF.Ln, bias=self.konst[:, 0:1])
                a.drain()
                a.activation(out=st[:, 7:8], in_=st[:, 6:7], func=AF.Exp, scale=-0.5)
                a.drain()
                a.activation(out=ln0t[:, t % 2, :], in_=xc[:, t % 2, :],
                             func=AF.Copy, scale=st[:, 7:8])
                cA.inc(a.drain())
                ins = a.dma_start(out=self.tm("ln0b")[:, t, :], in_=ln0t[:, t % 2, :])
                dS.inc(ins, 16)
        self.expect(cV, v1b + NT, "pa1 V")
        self.expect(cA, a1b + NT, "pa1 A")
        self.expect(dS, ds1b + 16 * NT, "pa1 dS")

    def phase_a2(self, block, ps):
        nc = self.nc
        dY, dS, dG = self.dY, self.dS, self.dG
        cT, cV, cA, cG = self.cT, self.cV, self.cA, self.cG
        ln0T = self.sb("pa_ln0T", (128, 8, NTOK), BF16, stack=ps)
        wq = self.sb("pa_wq", (128, 8, 3 * DIM), BF16, stack=ps)
        cst = self.sb("pa_cst", (128, 7 * DIM), stack=ps)
        pqkv = self.psum("pa_pqkv", (128, 3 * DIM), stack=ps)
        hs = self.sb("pa_hs", (128, 128), stack=ps)
        qn1 = self.sb("pa_qn1", (128, DIM), stack=ps)
        qn2 = self.sb("pa_qn2", (128, DIM), stack=ps)
        ob = self.sb("pa_ob", (128, 2, 3, DIM), BF16, stack=ps)

        xbar_ld = []
        all_sp1 = dS.n
        t2b, v2b, a2b, ds2b, dg2b = cT.n, cV.n, cA.n, dS.n, dG.n
        del nc

        @block.sync
        def _(y):
            for i, nm in enumerate(("qn_scale", "qn_bias", "kn_scale", "kn_bias")):
                ins = y.dma_start(out=cst[:, i * DIM:(i + 1) * DIM], in_=self.ap(nm))
                dY.inc(ins, 16)
            ins = y.dma_start(out=cst[:, 4 * DIM:7 * DIM], in_=self.ap("qkv_b"))
            cst_ld = dY.inc(ins, 16)
            y.wait_ge(dS.sem, all_sp1)
            for k in range(8):
                ins = y.dma_start(out=ln0T[:, k, :],
                                  in_=self.ap("ln0b")[:, k * 128:(k + 1) * 128],
                                  transpose=True)
                xbar_ld.append(dY.inc(ins, 16))
            self.pa_cst_ld = cst_ld

        @block.gpsimd
        def _(g):
            ins = g.dma_start(out=wq[:],
                              in_=self.ap("qkv_w").rearrange("(k p) n -> p k n", p=128))
            self.pa_wq_ld = dG.inc(ins, 16)

        @block.tensor
        def _(t_):
            t_.wait_ge(dY.sem, xbar_ld[-1])
            t_.wait_ge(dG.sem, self.pa_wq_ld)
            for t in range(NT):
                if t >= 1:
                    t_.wait_ge(cV.sem, v2b + 3 * (t - 1) + 1)
                for k in range(8):
                    lhsT = ln0T[:, k, t * 128:(t + 1) * 128]
                    for n in range(6):
                        ins = t_.matmul(pqkv[:, n * 512:(n + 1) * 512], lhsT,
                                        wq[:, k, n * 512:(n + 1) * 512],
                                        start=(k == 0), stop=(k == 7))
                cT.inc(ins)

        sqq = self.sb("pa_sqq", (128, DIM), stack=ps)
        sqk = self.sb("pa_sqk", (128, DIM), stack=ps)
        qkvb = self.sb("pa_qkvb", (128, 2, 3 * DIM), stack=ps)

        @block.vector
        def _(v):
            v.wait_ge(dY.sem, self.pa_cst_ld)
            for t in range(NT):
                v.wait_ge(cT.sem, t2b + t + 1)
                if t >= 2:
                    v.wait_ge(cA.sem, a2b + 2 * t - 3)
                    v.wait_ge(dG.sem, dg2b + 32 * (t - 1))
                v.tensor_tensor(qkvb[:, t % 2, :], pqkv[:],
                                cst[:, 4 * DIM:7 * DIM], ALU.add)
                cV.inc(v.drain())
                qb = qkvb[:, t % 2, :]
                v.wait_ge(cA.sem, a2b + 2 * t + 1)
                h3q = hs[:, 0:16].rearrange("p (h o) -> p h o", o=1)
                h3k = hs[:, 64:80].rearrange("p (h o) -> p h o", o=1)
                v.reduce_sum(h3q, sqq[:].rearrange("p (h j) -> p h j", j=HD), axis=AX)
                v.reduce_sum(h3k, sqk[:].rearrange("p (h j) -> p h j", j=HD), axis=AX)
                cV.inc(v.drain())
                v.wait_ge(cA.sem, a2b + 2 * t + 2)
                for qi, (sqcol, s_off, b_off, psl) in enumerate((
                        (16, 0, DIM, slice(0, DIM)),
                        (80, 2 * DIM, 3 * DIM, slice(DIM, 2 * DIM)))):
                    v.tensor_scalar(hs[:, sqcol + 16:sqcol + 32],
                                    hs[:, sqcol:sqcol + 16], 1e-6, None, ALU.add)
                    v.drain()
                    v.reciprocal(hs[:, sqcol + 32:sqcol + 48],
                                 hs[:, sqcol + 16:sqcol + 32])
                    v.drain()
                    f3 = hs[:, sqcol + 32:sqcol + 48].rearrange("p (h o) -> p h o", o=1)
                    q3 = qb[:, psl].rearrange("p (h j) -> p h j", j=HD)
                    o3 = qn1[:].rearrange("p (h j) -> p h j", j=HD)
                    v.tensor_tensor(o3, q3, f3.to_broadcast([128, HEADS, HD]), ALU.mult)
                    v.drain()
                    v.tensor_tensor(qn2[:], qn1[:], cst[:, s_off:s_off + DIM], ALU.mult)
                    v.drain()
                    v.tensor_tensor(ob[:, t % 2, qi, :], qn2[:],
                                    cst[:, b_off:b_off + DIM], ALU.add)
                cV.inc(v.drain())

        @block.scalar
        def _(a):
            for t in range(NT):
                a.wait_ge(cV.sem, v2b + 3 * t + 1)
                if t >= 2:
                    a.wait_ge(dS.sem, ds2b + 16 * (t - 1))
                qb = qkvb[:, t % 2, :]
                a.activation(out=sqq[:], in_=qb[:, 0:DIM], func=AF.Square)
                a.activation(out=sqk[:], in_=qb[:, DIM:2 * DIM], func=AF.Square)
                a.activation(out=ob[:, t % 2, 2, :], in_=qb[:, 2 * DIM:3 * DIM],
                             func=AF.Copy, scale=2.0 / NDTOT)
                cA.inc(a.drain())
                ins = a.dma_start(out=self.tm("vc_tok")[:, t, :], in_=ob[:, t % 2, 2, :])
                dS.inc(ins, 16)
                a.wait_ge(cV.sem, v2b + 3 * t + 2)
                a.activation(out=hs[:, 16:32], in_=hs[:, 0:16], func=AF.Sqrt)
                a.activation(out=hs[:, 80:96], in_=hs[:, 64:80], func=AF.Sqrt)
                cA.inc(a.drain())

        @block.gpsimd
        def _(g):
            for t in range(NT):
                g.wait_ge(cV.sem, v2b + 3 * t + 3)
                ins = g.dma_start(out=self.tm("q_tok")[:, t, :], in_=ob[:, t % 2, 0, :])
                dG.inc(ins, 16)
                ins = g.dma_start(out=self.tm("k_tok")[:, t, :], in_=ob[:, t % 2, 1, :])
                dG.inc(ins, 16)
        self.expect(cT, t2b + NT, "pa2 T")
        self.expect(cV, v2b + 3 * NT, "pa2 V")
        self.expect(cA, a2b + 2 * NT, "pa2 A")
        self.expect(dG, dg2b + 32 * NT + 16, "pa2 dG")

    def ttt_F(self, block, ps, step, query=False):
        dY, dS, dG = self.dY, self.dS, self.dG
        cT, cV, cA, cG = self.cT, self.cV, self.cA, self.cG
        src_tok = "q_tok" if query else "k_tok"

        kT = self.sb("f_kT", (128, 8, NTOK), BF16, stack=ps)
        gh = self.sb("f_gh", (128, 8, 4, 512), BF16, stack=ps)
        vh = self.sb("f_vh", (128, 8, 4, 512), BF16, stack=ps)
        pa = self.psum("f_pa", (128, 2048), stack=ps)
        pu = self.psum("f_pu", (128, 2048), stack=ps)
        swt = self.sb("f_swt", (128, 2, 2048), BF16, stack=ps)
        swpt = self.sb("f_swpt", (128, 2, 2048), BF16, stack=ps)
        ubt = self.sb("f_ubt", (128, 2, 2048), BF16, stack=ps)
        ht = self.sb("f_ht", (128, 2, 2048), BF16, stack=ps)
        uswpt = self.sb("f_uswpt", (128, 2, 2048), BF16, stack=ps)

        kT_ld = []
        gv_ld = []
        t0 = dict(T=cT.n, V=cV.n, A=cA.n, dG=dG.n, dS=dS.n, dY=dY.n)
        NSP = 32 if not query else 16

        @block.sync
        def _(y):
            for k in range(8):
                ins = y.dma_start(out=kT[:, k, :],
                                  in_=self.ap(src_tok)[:, k * 128:(k + 1) * 128],
                                  transpose=True)
                kT_ld.append(dY.inc(ins, 16))
            for i in range(2 * NT):
                h, t = divmod(i, NT)
                y.wait_ge(cV.sem, t0["V"] + i + 1)
                ins = y.dma_start(out=self.tm("h_tok")[:, t, h * 2048:(h + 1) * 2048],
                                  in_=ht[:, i % 2, :])
                dY.inc(ins, 16)
                if not query:
                    ins = y.dma_start(
                        out=self.tm("uswp_tok")[:, t, h * 2048:(h + 1) * 2048],
                        in_=uswpt[:, i % 2, :])
                    dY.inc(ins, 16)

        @block.gpsimd
        def _(g):
            for h in range(2):
                if h == 1:
                    g.wait_ge(cT.sem, t0["T"] + NT)
                for nm, dst in (("G_work", gh), ("V_work", vh)):
                    for b_ in range(4):
                        ins = g.dma_start(
                            out=dst[:, :, b_, :],
                            in_=self.ap(nm)[4 * h + b_].rearrange(
                                "(k p) n -> p k n", p=128))
                        dG.inc(ins, 16)
                    gv_ld.append(dG.n)

        @block.tensor
        def _(t_):
            t_.wait_ge(dY.sem, kT_ld[-1])
            for h in range(2):
                t_.wait_ge(dG.sem, gv_ld[2 * h + 1])
                for t in range(NT):
                    i = h * NT + t
                    if i >= 1:
                        t_.wait_ge(cA.sem, t0["A"] + i)
                    for k in range(8):
                        lhsT = kT[:, k, t * 128:(t + 1) * 128]
                        for b_ in range(4):
                            t_.matmul(pa[:, b_ * 512:(b_ + 1) * 512], lhsT,
                                      gh[:, k, b_, :], start=(k == 0), stop=(k == 7))
                        for b_ in range(4):
                            ins = t_.matmul(pu[:, b_ * 512:(b_ + 1) * 512], lhsT,
                                            vh[:, k, b_, :], start=(k == 0), stop=(k == 7))
                    cT.inc(ins)

        @block.scalar
        def _(a):
            for i in range(2 * NT):
                a.wait_ge(cT.sem, t0["T"] + i + 1)
                if i >= 2:
                    a.wait_ge(cV.sem, t0["V"] + i - 1)
                    if not query:
                        a.wait_ge(dS.sem, t0["dS"] + 16 * (i - 1))
                a.activation(out=swt[:, i % 2, :], in_=pa[:], func=AF.Silu)
                if not query:
                    a.activation(out=swpt[:, i % 2, :], in_=pa[:], func=AF.Derivative_silu)
                a.activation(out=ubt[:, i % 2, :], in_=pu[:], func=AF.Copy)
                cA.inc(a.drain())
                if not query:
                    h, t = divmod(i, NT)
                    ins = a.dma_start(
                        out=self.tm("sw_tok")[:, t, h * 2048:(h + 1) * 2048],
                        in_=swt[:, i % 2, :])
                    dS.inc(ins, 16)

        @block.vector
        def _(v):
            for i in range(2 * NT):
                v.wait_ge(cA.sem, t0["A"] + i + 1)
                if i >= 2:
                    v.wait_ge(dY.sem, t0["dY"] + 128 + NSP * (i - 1))
                v.tensor_tensor(ht[:, i % 2, :], swt[:, i % 2, :],
                                ubt[:, i % 2, :], ALU.mult)
                if not query:
                    v.tensor_tensor(uswpt[:, i % 2, :], ubt[:, i % 2, :],
                                    swpt[:, i % 2, :], ALU.mult)
                cV.inc(v.drain())

        self.expect(cT, t0["T"] + 2 * NT, "F T")
        self.expect(cV, t0["V"] + 2 * NT, "F V")

    def ttt_P(self, block, ps, step, query=False):
        dY, dS, dG = self.dY, self.dS, self.dG
        cT, cV, cA = self.cT, self.cV, self.cA

        hT = self.sb("p_hT", (128, 2, 32, 1024), BF16, stack=ps)
        Pb = self.sb("p_Pb", (128, 32, 1024), BF16, stack=ps)
        vct = self.sb("p_vct", (128, 2, DIM), BF16, stack=ps)
        Rt = self.sb("p_Rt", (128, 2, DIM), BF16, stack=ps)
        pp = self.psum("p_pp", (128, 2, DIM), stack=ps)
        NSC, TSC = 4, 8
        S = lambda e, c, f: self.seq(block, e, c, f)

        def s_pb(a):
            for b_ in range(8):
                for j in range(4):
                    dS.inc(a.dma_start(
                        out=Pb[:, b_ * 4 + j, :],
                        in_=self.ap("PT_work")[b_][:, j * 128:(j + 1) * 128],
                        transpose=True), 16)
        S("scalar", None, s_pb)

        def v_z(v):
            v.memset(vct[:, 0, :], 0.0)
            v.memset(vct[:, 1, :], 0.0)
        S("vector", cV, v_z)

        for sc in range(NSC):
            def y_h(y, sc=sc):
                for j in range(32):
                    dY.inc(y.dma_start(
                        out=hT[:, sc % 2, j, :],
                        in_=self.ap("h_tok")[sc * 1024:(sc + 1) * 1024,
                                             j * 128:(j + 1) * 128],
                        transpose=True), 16)
            S("sync", None, y_h)
            for t in range(TSC):
                gt = sc * TSC + t

                def t_mm(t_, sc=sc, t=t, gt=gt):
                    for k in range(32):
                        lhsT = hT[:, sc % 2, k, t * 128:(t + 1) * 128]
                        t_.matmul(pp[:, gt % 2, 0:512], lhsT, Pb[:, k, 0:512],
                                  start=(k == 0), stop=(k == 31))
                        t_.matmul(pp[:, gt % 2, 512:1024], lhsT, Pb[:, k, 512:1024],
                                  start=(k == 0), stop=(k == 31))
                S("tensor", cT, t_mm)

                if not query:
                    def y_vc(y, gt=gt):
                        dY.inc(y.dma_start(out=vct[:, gt % 2, :],
                                           in_=self.tm("vc_tok")[:, gt, :]), 16)
                    S("sync", None, y_vc)

                def v_r(v, gt=gt):
                    v.scalar_tensor_tensor(Rt[:, gt % 2, :], pp[:, gt % 2, :],
                                           1.0 if query else 2.0 / NDTOT,
                                           vct[:, gt % 2, :], ALU.mult, ALU.subtract)
                S("vector", cV, v_r)

                def g_sp(g, gt=gt):
                    dst = "predq_tok" if query else "R_tok"
                    dG.inc(g.dma_start(out=self.tm(dst)[:, gt, :],
                                       in_=Rt[:, gt % 2, :]), 16)
                S("gpsimd", None, g_sp)
        S("gpsimd", None, lambda g: g.wait_ge(dG.sem, dG.n))

    def ttt_D(self, block, ps, step):
        dY, dS, dG = self.dY, self.dS, self.dG
        cT, cV, cA = self.cT, self.cV, self.cA

        RT = self.sb("d_RT", (128, 8, NTOK), BF16, stack=ps)
        PTh = self.sb("d_PTh", (128, 8, 4, 512), BF16, stack=ps)
        swt = self.sb("d_swt", (128, 2, 2048), BF16, stack=ps)
        uswpt = self.sb("d_uswpt", (128, 2, 2048), BF16, stack=ps)
        dvt = self.sb("d_dvt", (128, 2, 2048), BF16, stack=ps)
        dgt = self.sb("d_dgt", (128, 2, 2048), BF16, stack=ps)
        pdh = self.psum("d_pdh", (128, 2, 2048), stack=ps)

        t0 = dict(T=cT.n, V=cV.n, dY=dY.n, dS=dS.n, dG=dG.n)
        RT_ld = []
        pt_ld = []
        s_ld = []

        @block.sync
        def _(y):
            for k in range(8):
                ins = y.dma_start(out=RT[:, k, :],
                                  in_=self.ap("R_tok")[:, k * 128:(k + 1) * 128],
                                  transpose=True)
                RT_ld.append(dY.inc(ins, 16))
            for i in range(2 * NT):
                h, t = divmod(i, NT)
                if i >= 2:
                    y.wait_ge(cV.sem, t0["V"] + i - 1)
                ins = y.dma_start(out=swt[:, i % 2, :],
                                  in_=self.tm("sw_tok")[:, t, h * 2048:(h + 1) * 2048])
                dY.inc(ins, 16)
                ins = y.dma_start(out=uswpt[:, i % 2, :],
                                  in_=self.tm("uswp_tok")[:, t, h * 2048:(h + 1) * 2048])
                s_ld.append(dY.inc(ins, 16))

        @block.scalar
        def _(a):
            for h in range(2):
                if h == 1:
                    a.wait_ge(cT.sem, t0["T"] + NT)
                for b_ in range(4):
                    ins = a.dma_start(
                        out=PTh[:, :, b_, :],
                        in_=self.ap("PT_work")[4 * h + b_].rearrange(
                            "(k p) n -> p k n", p=128))
                    dS.inc(ins, 16)
                pt_ld.append(dS.n)

        @block.tensor
        def _(t_):
            t_.wait_ge(dY.sem, RT_ld[-1])
            for h in range(2):
                t_.wait_ge(dS.sem, pt_ld[h])
                for t in range(NT):
                    i = h * NT + t
                    if i >= 2:
                        t_.wait_ge(cV.sem, t0["V"] + i - 1)
                    for k in range(8):
                        lhsT = RT[:, k, t * 128:(t + 1) * 128]
                        for b_ in range(4):
                            ins = t_.matmul(pdh[:, i % 2, b_ * 512:(b_ + 1) * 512],
                                            lhsT, PTh[:, k, b_, :],
                                            start=(k == 0), stop=(k == 7))
                    cT.inc(ins)

        @block.vector
        def _(v):
            for i in range(2 * NT):
                v.wait_ge(cT.sem, t0["T"] + i + 1)
                v.wait_ge(dY.sem, s_ld[i])
                if i >= 2:
                    v.wait_ge(dG.sem, t0["dG"] + 32 * (i - 1))
                v.tensor_tensor(dvt[:, i % 2, :], pdh[:, i % 2, :],
                                swt[:, i % 2, :], ALU.mult)
                v.tensor_tensor(dgt[:, i % 2, :], pdh[:, i % 2, :],
                                uswpt[:, i % 2, :], ALU.mult)
                cV.inc(v.drain())

        @block.gpsimd
        def _(g):
            for i in range(2 * NT):
                g.wait_ge(cV.sem, t0["V"] + i + 1)
                h, t = divmod(i, NT)
                ins = g.dma_start(out=self.tm("dV_tok")[:, t, h * 2048:(h + 1) * 2048],
                                  in_=dvt[:, i % 2, :])
                dG.inc(ins, 16)
                ins = g.dma_start(out=self.tm("dG_tok")[:, t, h * 2048:(h + 1) * 2048],
                                  in_=dgt[:, i % 2, :])
                dG.inc(ins, 16)
        self.expect(cT, t0["T"] + 2 * NT, "D T")

    def ttt_G1(self, block, ps, xname, rhsname, outname):
        dY, dS, dG = self.dY, self.dS, self.dG
        cT, cV = self.cT, self.cV

        rhs = self.sb("g_rhs", (128, 32, 1024), BF16, stack=ps)
        strip = self.sb("g_strip", (128, 2, 32, 128), BF16, stack=ps)
        ev = self.sb("g_ev", (128, 2, DIM), BF16, stack=ps)
        pg = self.psum("g_pg", (128, 2, DIM), stack=ps)

        t0 = dict(T=cT.n, V=cV.n, dY=dY.n, dS=dS.n, dG=dG.n)
        rhs_ld = []
        strip_ld = []

        @block.scalar
        def _(a):
            ins = a.dma_start(out=rhs[:],
                              in_=self.ap(rhsname).rearrange("(t p) f -> p t f", p=128))
            rhs_ld.append(dS.inc(ins, 16))

        @block.sync
        def _(y):
            xv = self.ap(xname).rearrange("(t p) f -> p t f", p=128)
            for m in range(32):
                if m >= 2:
                    y.wait_ge(cT.sem, t0["T"] + m - 1)
                ins = y.dma_start(out=strip[:, m % 2, :, :],
                                  in_=xv[:, :, m * 128:(m + 1) * 128])
                strip_ld.append(dY.inc(ins, 16))

        @block.tensor
        def _(t_):
            t_.wait_ge(dS.sem, rhs_ld[0])
            for m in range(32):
                t_.wait_ge(dY.sem, strip_ld[m])
                if m >= 2:
                    t_.wait_ge(cV.sem, t0["V"] + m - 1)
                for kt in range(32):
                    lhsT = strip[:, m % 2, kt, :]
                    t_.matmul(pg[:, m % 2, 0:512], lhsT, rhs[:, kt, 0:512],
                              start=(kt == 0), stop=(kt == 31))
                    ins = t_.matmul(pg[:, m % 2, 512:1024], lhsT, rhs[:, kt, 512:1024],
                                    start=(kt == 0), stop=(kt == 31))
                cT.inc(ins)

        @block.vector
        def _(v):
            for m in range(32):
                v.wait_ge(cT.sem, t0["T"] + m + 1)
                if m >= 2:
                    v.wait_ge(dG.sem, t0["dG"] + 16 * (m - 1))
                v.tensor_copy(ev[:, m % 2, :], pg[:, m % 2, :])
                cV.inc(v.drain())

        @block.gpsimd
        def _(g):
            for m in range(32):
                g.wait_ge(cV.sem, t0["V"] + m + 1)
                ins = g.dma_start(out=self.ap(outname)[m * 128:(m + 1) * 128, :],
                                  in_=ev[:, m % 2, :])
                dG.inc(ins, 16)
        self.expect(cT, t0["T"] + 32, "G1 T")

    def ttt_RS(self, block):
        cC, dG = self.cC, self.dG

        @block.gpsimd
        def _(g):
            g.wait_ge(dG.sem, dG.n)
            for nm in ("gP", "gV", "gG"):
                ins = g.collective_compute(
                    "ReduceScatter", ALU.add, replica_groups=RG,
                    ins=[self.ap(nm + "_part").opt()],
                    outs=[self.ap(nm + "_rs").opt()])
                cC.inc(ins)
            g.wait_ge(cC.sem, cC.n)

    def ttt_NS(self, block, ps, step):
        dY, dS, dG = self.dY, self.dS, self.dG
        cT, cV, cA, cG, cC = self.cT, self.cV, self.cA, self.cG, self.cC

        Xb = self.sb("ns_Xb", (128, 3, 8, 512), BF16, stack=ps)
        X = self.sb("ns_X", (128, 3, 8, 512), F32, stack=ps)
        XT = self.sb("ns_XT", (128, 4, 1024), BF16, stack=ps)
        Asb = self.sb("ns_Asb", (128, 8, 1024), BF16, stack=ps)
        Y32 = self.sb("ns_Y32", (128, 8, 512), F32, stack=ps)
        Yb = self.sb("ns_Yb", (128, 8, 512), BF16, stack=ps)
        evA = self.sb("ns_evA", (128, 2, 1024), BF16, stack=ps)
        mst = self.sb("ns_mst", (128, 8, 512), F32, stack=ps)
        sq_s = self.sb("ns_sq", (128, 4096), F32, stack=ps)
        sc = self.sb("ns_sc", (128, 8), F32, stack=ps)
        ones = self.sb("ns_ones", (128, 1), BF16, stack=ps)
        ones_row = self.sb("ns_onesr", (1, 128), BF16, stack=ps)
        scb = self.sb("ns_scb", (128, 4), BF16, stack=ps)
        inv = self.sb("ns_inv", (128, 4), F32, stack=ps)
        psA = self.psum("ns_psA", (128, 2, 1024), stack=ps)
        psY = self.psum("ns_psY", (128, 2, 512), stack=ps)
        psZ = self.psum("ns_psZ", (128, 2, 512), stack=ps)

        P_RS = ("gG_rs", "gV_rs", "gP_rs")
        P_M = ("G_m", "V_m", "PT_m")
        P_WK = ("G_work", "V_work", "PT_work")
        P_SH = ("gate_sh", "val_sh", "proj_sh")

        b0 = dict(T=cT.n, V=cV.n, A=cA.n, dY=dY.n, dS=dS.n, dG=dG.n, C=cC.n)

        xb_ld = [b0["dY"] + 128 * (pi + 1) for pi in range(3)]

        @block.sync
        def _(y):
            y.wait_ge(cC.sem, b0["C"])
            for pi in range(3):
                for j in range(8):
                    ins = y.dma_start(out=Xb[:, pi, j, :],
                                      in_=self.ap(P_RS[pi])[:, j * 128:(j + 1) * 128],
                                      transpose=True)
                    dY.inc(ins, 16)

        @block.scalar
        def _(a):
            for pi in range(3):
                a.wait_ge(dY.sem, xb_ld[pi])
                a.activation(out=sq_s[:], in_=Xb[:, pi].rearrange("p a b -> p (a b)"),
                             func=AF.Square, accum_out=sc[:, pi:pi + 1])
                cA.inc(a.drain())

        @block.vector
        def _(v):
            v.memset(ones[:], 1.0)
            v.memset(ones_row[:], 1.0)
            for pi in range(3):
                v.wait_ge(cA.sem, b0["A"] + pi + 1)
                v.tensor_copy(scb[:, pi:pi + 1], sc[:, pi:pi + 1])
            cV.inc(v.drain())

        @block.tensor
        def _(t_):
            t_.wait_ge(cV.sem, b0["V"] + 1)
            for pi in range(3):
                ins = t_.matmul(psZ[0:1, 0, pi:pi + 1], ones[:], scb[:, pi:pi + 1],
                                start=True, stop=True)
            cT.inc(ins)

        @block.vector
        def _(v):
            v.wait_ge(cT.sem, b0["T"] + 1)
            v.tensor_copy(sc[0:1, 0:3], psZ[0:1, 0, 0:3])
            cV.inc(v.drain())

        @block.gpsimd
        def _(g):
            g.wait_ge(cV.sem, b0["V"] + 2)
            ins = g.dma_start(out=self.ap("norm_part"), in_=sc[0:1, 0:3])
            dG.inc(ins, 16)
            g.wait_ge(dG.sem, dG.n)
            ins = g.collective_compute(
                "AllReduce", ALU.add, replica_groups=RG,
                ins=[self.ap("norm_part").opt()], outs=[self.ap("norm_out").opt()])
            cC.inc(ins)
            g.wait_ge(cC.sem, cC.n)
            ins = g.dma_start(out=sc[0:1, 4:7], in_=self.ap("norm_out"))
            dG.inc(ins, 16)
        nrm_ld = dG.n

        @block.scalar
        def _(a):
            a.wait_ge(dG.sem, nrm_ld)
            a.activation(out=sc[0:1, 0:3], in_=sc[0:1, 4:7], func=AF.Sqrt)
            cA.inc(a.drain())

        @block.vector
        def _(v):
            v.wait_ge(cA.sem, cA.n)
            v.tensor_scalar(sc[0:1, 0:3], sc[0:1, 0:3], 1e-7, None, ALU.add)
            v.drain()
            v.reciprocal(sc[0:1, 0:3], sc[0:1, 0:3])
            v.drain()
            v.tensor_copy(scb[0:1, 0:3], sc[0:1, 0:3])
            cV.inc(v.drain())

        @block.tensor
        def _(t_):
            t_.wait_ge(cV.sem, cV.n)
            ins = t_.matmul(psZ[:, 0, 0:3], ones_row[0:1, :], scb[0:1, 0:3],
                            start=True, stop=True)
            cT.inc(ins)

        @block.vector
        def _(v):
            v.wait_ge(cT.sem, cT.n)
            v.tensor_copy(inv[:, 0:3], psZ[:, 0, 0:3])
            v.drain()
            for pi in range(3):
                v.tensor_scalar(X[:, pi].rearrange("p a b -> p (a b)"),
                                Xb[:, pi].rearrange("p a b -> p (a b)"),
                                inv[:, pi:pi + 1], None, ALU.mult)
            cV.inc(v.drain())

        nb = dict(T=cT.n, V=cV.n, dY=dY.n, dG=dG.n, C=cC.n)
        CV_R, CT_R, DY_R, DG_R = 25, 24, 80, 144
        NR = 3 * NS_ITERS

        def vb(r):
            return nb["V"] + r * CV_R

        def tb(r):
            return nb["T"] + r * CT_R

        @block.vector
        def _(v):
            for r in range(NR):
                pi = r // NS_ITERS
                if r >= 1:
                    v.wait_ge(cT.sem, tb(r - 1) + 16)
                    v.wait_ge(dG.sem, nb["dG"] + (r - 1) * DG_R + 16)
                v.tensor_copy(Xb[:, pi].rearrange("p a b -> p (a b)"),
                              X[:, pi].rearrange("p a b -> p (a b)"))
                cV.inc(v.drain())
                for m in range(8):
                    v.wait_ge(cT.sem, tb(r) + m + 1)
                    v.tensor_copy(evA[:, m % 2, :], psA[:, m % 2, :])
                    cV.inc(v.drain())
                for m in range(8):
                    v.wait_ge(cT.sem, tb(r) + 8 + m + 1)
                    v.tensor_copy(Y32[:, m, :], psY[:, m % 2, :])
                    v.tensor_copy(Yb[:, m, :], psY[:, m % 2, :])
                    cV.inc(v.drain())
                for m in range(8):
                    v.wait_ge(cT.sem, tb(r) + 16 + m + 1)
                    v.tensor_scalar(X[:, pi, m, :], X[:, pi, m, :], NS_A, None, ALU.mult)
                    v.drain()
                    v.scalar_tensor_tensor(X[:, pi, m, :], Y32[:, m, :], NS_B,
                                           X[:, pi, m, :], ALU.mult, ALU.add)
                    v.drain()
                    v.scalar_tensor_tensor(X[:, pi, m, :], psZ[:, m % 2, :], NS_C,
                                           X[:, pi, m, :], ALU.mult, ALU.add)
                    cV.inc(v.drain())

        @block.gpsimd
        def _(g):
            for r in range(NR):
                pi = r // NS_ITERS
                g.wait_ge(cV.sem, vb(r) + 1)
                ins = g.dma_start(
                    out=self.ap(f"nsXsp_{pi}").rearrange("(k p) n -> p k n", p=128),
                    in_=Xb[:, pi].rearrange("p a b -> p a b"))
                dG.inc(ins, 16)
                for m in range(8):
                    g.wait_ge(cV.sem, vb(r) + 1 + m + 1)
                    ins = g.dma_start(
                        out=self.ap(f"nsAp_{pi}")[m * 128:(m + 1) * 128, :]
                        .rearrange("(o p) f -> p o f", p=128)[:, 0, :],
                        in_=evA[:, m % 2, :])
                    dG.inc(ins, 16)
                g.wait_ge(dG.sem, nb["dG"] + r * DG_R + DG_R)
                ins = g.collective_compute(
                    "AllReduce", ALU.add, replica_groups=RG,
                    ins=[self.ap(f"nsAp_{pi}").opt()],
                    outs=[self.ap(f"nsA_{pi}").opt()])
                cC.inc(ins)

        @block.sync
        def _(y):
            for r in range(NR):
                pi = r // NS_ITERS
                y.wait_ge(dG.sem, nb["dG"] + r * DG_R + 16)
                for c in range(4):
                    ins = y.dma_start(
                        out=XT[:, c, :],
                        in_=self.ap(f"nsXsp_{pi}")[:, c * 128:(c + 1) * 128],
                        transpose=True)
                    dY.inc(ins, 16)
                y.wait_ge(cC.sem, nb["C"] + r + 1)
                ins = y.dma_start(out=Asb[:],
                                  in_=self.ap(f"nsA_{pi}").rearrange(
                                      "(k p) n -> p k n", p=128))
                dY.inc(ins, 16)

        @block.tensor
        def _(t_):
            for r in range(NR):
                pi = r // NS_ITERS
                t_.wait_ge(dY.sem, nb["dY"] + r * DY_R + 64)
                for m in range(8):
                    if m >= 2:
                        t_.wait_ge(cV.sem, vb(r) + 1 + m - 1)
                    for kc in range(4):
                        lhsT = XT[:, kc, m * 128:(m + 1) * 128]
                        t_.matmul(psA[:, m % 2, 0:512], lhsT, XT[:, kc, 0:512],
                                  start=(kc == 0), stop=(kc == 3))
                        ins = t_.matmul(psA[:, m % 2, 512:1024], lhsT,
                                        XT[:, kc, 512:1024],
                                        start=(kc == 0), stop=(kc == 3))
                    cT.inc(ins)
                t_.wait_ge(dY.sem, nb["dY"] + r * DY_R + 80)
                for m in range(8):
                    if m >= 2:
                        t_.wait_ge(cV.sem, vb(r) + 9 + m - 1)
                    for k in range(8):
                        ins = t_.matmul(psY[:, m % 2, :],
                                        Asb[:, k, m * 128:(m + 1) * 128],
                                        Xb[:, pi, k, :], start=(k == 0), stop=(k == 7))
                    cT.inc(ins)
                t_.wait_ge(cV.sem, vb(r) + 17)
                for m in range(8):
                    if m >= 2:
                        t_.wait_ge(cV.sem, vb(r) + 17 + m - 1)
                    for k in range(8):
                        ins = t_.matmul(psZ[:, m % 2, :],
                                        Asb[:, k, m * 128:(m + 1) * 128],
                                        Yb[:, k, :], start=(k == 0), stop=(k == 7))
                    cT.inc(ins)

        ib = dict(T=cT.n, V=cV.n, dY=dY.n, dG=dG.n, C=cC.n)
        self.expect(cT, nb["T"] + NR * CT_R, "NS T")
        self.expect(cV, nb["V"] + NR * CV_R, "NS V")

        @block.sync
        def _(y):
            for pi in range(3):
                srcn = P_SH[pi] if step == 0 else P_M[pi]
                y.wait_ge(cV.sem, ib["V"] + pi)
                if pi >= 1:
                    y.wait_ge(dG.sem, ib["dG"] + 32 * (pi - 1) + 16)
                ins = y.dma_start(out=mst[:],
                                  in_=self.ap(srcn).rearrange("(k p) n -> p k n", p=128))
                dY.inc(ins, 16)

        @block.vector
        def _(v):
            for pi in range(3):
                v.wait_ge(dY.sem, ib["dY"] + 16 * (pi + 1))
                v.scalar_tensor_tensor(mst[:].rearrange("p a b -> p (a b)"),
                                       X[:, pi].rearrange("p a b -> p (a b)"), -LR,
                                       mst[:].rearrange("p a b -> p (a b)"),
                                       ALU.mult, ALU.add)
                v.drain()
                v.tensor_copy(Xb[:, pi].rearrange("p a b -> p (a b)"),
                              mst[:].rearrange("p a b -> p (a b)"))
                cV.inc(v.drain())

        @block.gpsimd
        def _(g):
            for pi in range(3):
                g.wait_ge(cV.sem, ib["V"] + pi + 1)
                ins = g.dma_start(out=self.ap(P_M[pi]).rearrange("(k p) n -> p k n", p=128),
                                  in_=mst[:])
                dG.inc(ins, 16)
                ins = g.dma_start(out=self.ap(f"nsXsp_{pi}").rearrange(
                    "(k p) n -> p k n", p=128), in_=Xb[:, pi].rearrange("p a b -> p a b"))
                dG.inc(ins, 16)
                g.wait_ge(dG.sem, dG.n)
                ins = g.collective_compute(
                    "AllGather", ALU.bypass, replica_groups=RG,
                    ins=[self.ap(f"nsXsp_{pi}").opt()],
                    outs=[self.ap(P_WK[pi]).rearrange("b k n -> (b k) n").opt()])
                cC.inc(ins)
            g.wait_ge(cC.sem, cC.n)

    def ttt_O(self, block, ps):
        dY, dS, dG = self.dY, self.dS, self.dG
        cT, cV, cA = self.cT, self.cV, self.cA

        pqT = self.sb("o_pqT", (128, 8, NTOK), BF16, stack=ps)
        pw = self.sb("o_pw", (128, 8, DIM), BF16, stack=ps)
        pjb = self.sb("o_pjb", (128, DIM), F32, stack=ps)
        xin = self.sb("o_x", (128, 2, DIM), F32, stack=ps)
        ob = self.sb("o_ob", (128, 2, DIM), F32, stack=ps)
        sq = self.sb("o_sq", (128, DIM), F32, stack=ps)
        stat = self.sb("o_stat", (128, 2, 16), F32, stack=ps)
        xco = self.sb("o_xco", (128, 2, DIM), F32, stack=ps)
        lno = self.sb("o_lno", (128, 2, DIM), F32, stack=ps)
        x1b = self.sb("o_x1", (128, 2, DIM), F32, stack=ps)
        xc1 = self.sb("o_xc1", (128, 2, DIM), F32, stack=ps)
        ln1t = self.sb("o_ln1", (128, 2, DIM), BF16, stack=ps)
        po = self.psum("o_po", (128, 2, DIM), stack=ps)

        t0 = dict(T=cT.n, V=cV.n, A=cA.n, dY=dY.n, dS=dS.n, dG=dG.n)
        pq_ld = []
        x_ld = []
        pw_ld = []

        @block.sync
        def _(y):
            for k in range(8):
                ins = y.dma_start(out=pqT[:, k, :],
                                  in_=self.ap("predq_tok")[:, k * 128:(k + 1) * 128],
                                  transpose=True)
                pq_ld.append(dY.inc(ins, 16))
            xv = self.tm("x")
            for t in range(NT):
                if t >= 2:
                    y.wait_ge(cV.sem, t0["V"] + 2 * (t - 1))
                ins = y.dma_start(out=xin[:, t % 2, :], in_=xv[:, t, :])
                x_ld.append(dY.inc(ins, 16))

        @block.gpsimd
        def _(g):
            ins = g.dma_start(out=pw[:],
                              in_=self.ap("proj_w").rearrange("(k p) n -> p k n", p=128))
            dG.inc(ins, 16)
            ins = g.dma_start(out=pjb[:], in_=self.ap("proj_b"))
            pw_ld.append(dG.inc(ins, 16))
            for t in range(NT):
                g.wait_ge(cV.sem, t0["V"] + 2 * t + 2)
                ins = g.dma_start(out=self.tm("x1_tok")[:, t, :], in_=x1b[:, t % 2, :])
                dG.inc(ins, 16)

        @block.tensor
        def _(t_):
            t_.wait_ge(dY.sem, pq_ld[-1])
            t_.wait_ge(dG.sem, pw_ld[0])
            for t in range(NT):
                if t >= 1:
                    t_.wait_ge(cV.sem, t0["V"] + 2 * t - 1)
                for k in range(8):
                    lhsT = pqT[:, k, t * 128:(t + 1) * 128]
                    t_.matmul(po[:, t % 2, 0:512], lhsT, pw[:, k, 0:512],
                              start=(k == 0), stop=(k == 7))
                    ins = t_.matmul(po[:, t % 2, 512:1024], lhsT, pw[:, k, 512:1024],
                                    start=(k == 0), stop=(k == 7))
                cT.inc(ins)

        @block.vector
        def _(v):
            for t in range(NT):
                v.wait_ge(cT.sem, t0["T"] + t + 1)
                if t >= 2:
                    v.wait_ge(cA.sem, t0["A"] + 2 * t - 3)
                st = stat[:, t % 2, :]
                v.tensor_tensor(ob[:, t % 2, :], po[:, t % 2, :], pjb[:], ALU.add)
                v.drain()
                v.tensor_tensor(sq[:], ob[:, t % 2, :], ob[:, t % 2, :], ALU.mult)
                v.reduce_sum(st[:, 0:1], ob[:, t % 2, :], axis=AX)
                v.drain()
                v.reduce_sum(st[:, 1:2], sq[:], axis=AX)
                v.tensor_scalar(st[:, 2:3], st[:, 0:1], 1.0 / DIM, None, ALU.mult)
                v.drain()
                v.tensor_scalar(st[:, 3:4], st[:, 1:2], 1.0 / DIM, None, ALU.mult)
                v.tensor_tensor(st[:, 4:5], st[:, 2:3], st[:, 2:3], ALU.mult)
                v.tensor_scalar(xco[:, t % 2, :], ob[:, t % 2, :], st[:, 2:3],
                                None, ALU.subtract)
                v.drain()
                v.tensor_tensor(st[:, 5:6], st[:, 3:4], st[:, 4:5], ALU.subtract)
                cV.inc(v.drain())
                v.wait_ge(cA.sem, t0["A"] + 2 * t + 1)
                v.wait_ge(dY.sem, x_ld[t])
                if t >= 2:
                    v.wait_ge(dG.sem, t0["dG"] + 32 + 16 * (t - 1))
                v.tensor_tensor(x1b[:, t % 2, :], xin[:, t % 2, :],
                                lno[:, t % 2, :], ALU.add)
                v.drain()
                v.tensor_tensor(sq[:], x1b[:, t % 2, :], x1b[:, t % 2, :], ALU.mult)
                v.reduce_sum(st[:, 8:9], x1b[:, t % 2, :], axis=AX)
                v.drain()
                v.reduce_sum(st[:, 9:10], sq[:], axis=AX)
                v.tensor_scalar(st[:, 10:11], st[:, 8:9], 1.0 / DIM, None, ALU.mult)
                v.drain()
                v.tensor_scalar(st[:, 11:12], st[:, 9:10], 1.0 / DIM, None, ALU.mult)
                v.tensor_tensor(st[:, 12:13], st[:, 10:11], st[:, 10:11], ALU.mult)
                v.tensor_scalar(xc1[:, t % 2, :], x1b[:, t % 2, :], st[:, 10:11],
                                None, ALU.subtract)
                v.drain()
                v.tensor_tensor(st[:, 13:14], st[:, 11:12], st[:, 12:13], ALU.subtract)
                cV.inc(v.drain())

        @block.scalar
        def _(a):
            for t in range(NT):
                a.wait_ge(cV.sem, t0["V"] + 2 * t + 1)
                if t >= 2:
                    a.wait_ge(dS.sem, t0["dS"] + 16 * (t - 1))
                st = stat[:, t % 2, :]
                a.activation(out=st[:, 6:7], in_=st[:, 5:6], func=AF.Ln,
                             bias=self.konst[:, 0:1])
                a.drain()
                a.activation(out=st[:, 7:8], in_=st[:, 6:7], func=AF.Exp, scale=-0.5)
                a.drain()
                a.activation(out=lno[:, t % 2, :], in_=xco[:, t % 2, :],
                             func=AF.Copy, scale=st[:, 7:8])
                cA.inc(a.drain())
                a.wait_ge(cV.sem, t0["V"] + 2 * t + 2)
                a.activation(out=st[:, 14:15], in_=st[:, 13:14], func=AF.Ln,
                             bias=self.konst[:, 0:1])
                a.drain()
                a.activation(out=st[:, 15:16], in_=st[:, 14:15], func=AF.Exp, scale=-0.5)
                a.drain()
                a.activation(out=ln1t[:, t % 2, :], in_=xc1[:, t % 2, :],
                             func=AF.Copy, scale=st[:, 15:16])
                cA.inc(a.drain())
                ins = a.dma_start(out=self.tm("ln1_tok")[:, t, :], in_=ln1t[:, t % 2, :])
                dS.inc(ins, 16)
        self.expect(cT, t0["T"] + NT, "O T")
        self.expect(cV, t0["V"] + 2 * NT, "O V")
        self.expect(cA, t0["A"] + 2 * NT, "O A")

    def ttt_L(self, block, ps):
        dY, dS, dG = self.dY, self.dS, self.dG
        cT, cV, cA, cC = self.cT, self.cV, self.cA, self.cC

        ln1T = self.sb("l_ln1T", (128, 8, NTOK), BF16, stack=ps)
        gq = self.sb("l_gq", (128, 8, 16), BF16, stack=ps)
        lt = self.sb("l_lt", (1, 1), F32, stack=ps)
        ltb = self.sb("l_ltb", (16, 1), F32, stack=ps)
        Lsb = self.sb("l_L", (16, NTOK), F32, stack=ps)
        pL = self.psum("l_pL", (16, 2, 512), stack=ps)
        pB = self.psum("l_pB", (16, 1), stack=ps)
        onesr = self.sb("l_onesr", (1, 16), BF16, stack=ps)
        ltb_b = self.sb("l_ltbb", (1, 1), BF16, stack=ps)

        t0 = dict(T=cT.n, V=cV.n, A=cA.n, dY=dY.n, dS=dS.n, dG=dG.n)
        ld = []

        @block.sync
        def _(y):
            for k in range(8):
                ins = y.dma_start(out=ln1T[:, k, :],
                                  in_=self.ap("ln1_tok")[:, k * 128:(k + 1) * 128],
                                  transpose=True)
                ld.append(dY.inc(ins, 16))

        @block.gpsimd
        def _(g):
            ins = g.dma_start(out=gq[:],
                              in_=self.ap("gate_queryT").rearrange("(k p) n -> p k n", p=128))
            dG.inc(ins, 16)
            ins = g.dma_start(out=lt[:], in_=self.ap("log_temp"))
            ld.append(dG.inc(ins, 16))

        @block.scalar
        def _(a):
            a.wait_ge(dG.sem, ld[-1])
            a.activation(out=lt[:], in_=lt[:], func=AF.Exp, scale=-1.0)
            cA.inc(a.drain())

        @block.vector
        def _(v):
            v.memset(onesr[:], 1.0)
            v.wait_ge(cA.sem, t0["A"] + 1)
            v.tensor_copy(ltb_b[:], lt[:])
            cV.inc(v.drain())

        @block.tensor
        def _(t_):
            v0 = t0["V"] + 1
            t_.wait_ge(cV.sem, v0)
            ins = t_.matmul(pB[:], onesr[0:1, :], ltb_b[0:1, :], start=True, stop=True)
            cT.inc(ins)
            t_.wait_ge(dY.sem, t0["dY"] + 128)
            t_.wait_ge(dG.sem, t0["dG"] + 16)
            for n in range(8):
                if n >= 2:
                    t_.wait_ge(cV.sem, v0 + n - 1)
                for k in range(8):
                    ins = t_.matmul(pL[:, n % 2, :], gq[:, k, :],
                                    ln1T[:, k, n * 512:(n + 1) * 512],
                                    start=(k == 0), stop=(k == 7))
                cT.inc(ins)

        @block.vector
        def _(v):
            v.wait_ge(cT.sem, t0["T"] + 1)
            v.tensor_copy(ltb[:], pB[:])
            cV.inc(v.drain())
            for n in range(8):
                v.wait_ge(cT.sem, t0["T"] + n + 2)
                v.tensor_scalar(Lsb[:, n * 512:(n + 1) * 512], pL[:, n % 2, :],
                                ltb[:, 0:1], None, ALU.mult)
                cV.inc(v.drain())

        @block.gpsimd
        def _(g):
            g.wait_ge(cV.sem, cV.n)
            ins = g.dma_start(out=self.ap("logitsT_own"), in_=Lsb[:])
            dG.inc(ins, 16)
            g.wait_ge(dG.sem, dG.n)
            ins = g.collective_compute(
                "AllGather", ALU.bypass, replica_groups=RG,
                ins=[self.ap("ln1_tok").opt()], outs=[self.ap("ln1_all").opt()])
            cC.inc(ins)
            ins = g.collective_compute(
                "AllGather", ALU.bypass, replica_groups=RG,
                ins=[self.ap("logitsT_own").opt()], outs=[self.ap("logitsT_all").opt()])
            cC.inc(ins)
            g.wait_ge(cC.sem, cC.n)

    def seq(self, block, engname, ch, fn):
        chans = [self.cT, self.cV, self.cA, self.cG, self.dY, self.dS, self.dG, self.cC]
        vals = [c.n for c in chans]

        def body(eng):
            for c, v in zip(chans, vals):
                if v > 0:
                    eng.wait_ge(c.sem, v)
            fn(eng)
            if ch is not None:
                ch.inc(eng.drain())
        getattr(block, engname)(body)

    def ffn_R(self, block, ps):
        dY, dS, dG = self.dY, self.dS, self.dG
        cT, cV, cA, cG, cC = self.cT, self.cV, self.cA, self.cG, self.cC
        N2 = 8192

        sink = self.sb("r_sink", (64, N2), F32, stack=ps)
        rank = self.sb("r_rank", (64, N2), F32, stack=ps)
        sm = self.sb("r_sm", (64, 24), F32, stack=ps)
        bmap = self.sb("r_bmap", (64, 4), F32, stack=ps)
        bmapT = self.sb("r_bmapT", (4, 64), F32, stack=ps)
        onesel = self.sb("r_onesel", (64, 8), F32, stack=ps)
        i8t = self.sb("r_i8t", (64, 8), F32, stack=ps)
        tstg = self.sb("r_tstg", (128, 64, 16), F32, stack=ps)
        tbhi = self.sb("r_tbhi", (128, 64, 8), BF16, stack=ps)
        tblo = self.sb("r_tblo", (128, 64, 8), BF16, stack=ps)
        tres = self.sb("r_tres", (128, 64, 8), F32, stack=ps)
        iota_blk = self.sb("r_iblk", (128, 64), BF16, stack=ps)
        iota_pc = self.sb("r_ipc", (128, 64), BF16, stack=ps)
        excol = self.sb("r_excol", (128, 16), F32, stack=ps)
        gcol = self.sb("r_gcol", (128, 16), F32, stack=ps)
        excb = self.sb("r_excb", (128, 16), U32, stack=ps)
        cstt = self.sb("r_cst", (1, 8), F32, stack=ps)
        ccb = self.sb("r_ccb", (1, 1), BF16, stack=ps)
        smu = self.sb("r_smu", (64, 2), U32, stack=ps)
        xev = self.sb("r_xev", (128, 4), F32, stack=ps)
        onesr64 = self.sb("r_o64", (1, 64), BF16, stack=ps)
        ps8 = self.psum("r_ps8", (64, 4, 512), stack=ps)
        psT = self.psum("r_psT", (128, 8, 16), stack=ps)
        psX = self.psum("r_psX", (128, 2, 4), stack=ps)
        scope1 = ExitStack()
        exps = self.sb("r_exps", (64, N2), F32, stack=scope1)
        lse4 = self.sb("r_lse4", (4, N2), F32, stack=scope1)

        S = lambda e, c, f: self.seq(block, e, c, f)

        def st_(y):
            ag3 = self.ap("logitsT_all").rearrange("(c e) t -> c e t", e=16)
            for e in range(16):
                for half in range(2):
                    dY.inc(y.dma_start(
                        out=sink[e * 4:(e + 1) * 4, half * 4096:(half + 1) * 4096],
                        in_=ag3[half::2, e]), 16)
            dY.inc(y.dma_start(out=cstt[:], in_=self.ap("cconst")), 16)
        S("sync", None, st_)

        def g_iota(g):
            g.iota(sm[:, 0:1], pattern=[[0, 1]], base=0, channel_multiplier=1,
                   allow_small_or_imprecise_dtypes=True)
            g.iota(i8t[:], pattern=[[1, 8]], base=0, channel_multiplier=0,
                   allow_small_or_imprecise_dtypes=True)
            g.iota(iota_blk[:], pattern=[[1, 64]], base=0, channel_multiplier=0,
                   allow_small_or_imprecise_dtypes=True)
            g.iota(iota_pc[:], pattern=[[0, 64]], base=0, channel_multiplier=1,
                   allow_small_or_imprecise_dtypes=True)
        S("gpsimd", cG, g_iota)

        def g_bT(g):
            g.iota(lse4[0:4, 0:64], pattern=[[0, 16], [1, 4]], base=0,
                   channel_multiplier=0, allow_small_or_imprecise_dtypes=True)
            g.iota(lse4[0:4, 64:65], pattern=[[0, 1]], base=0,
                   channel_multiplier=1, allow_small_or_imprecise_dtypes=True)
        S("gpsimd", cG, g_bT)

        def v_bT(v):
            v.memset(onesr64[:], 1.0)
            v.tensor_scalar(bmapT[:], lse4[0:4, 0:64], lse4[0:4, 64:65], None,
                            ALU.is_equal)
        S("vector", cV, v_bT)
        S("tensor", cT, lambda t_: t_.transpose(
            ps8[0:64, 0, 0:4], bmapT[:], self.identf[0:4, 0:4]))
        S("vector", cV, lambda v: v.tensor_copy(bmap[:], ps8[0:64, 0, 0:4]))

        for it in range(SINK_IT):
            S("scalar", cA, lambda a: a.activation(out=exps[:], in_=sink[:], func=AF.Exp))
            for qq in range(4):
                def t_es(t_, qq=qq):
                    for c in range(4):
                        t_.matmul(ps8[0:4, c, :], bmap[:],
                                  exps[:, (qq * 4 + c) * 512:(qq * 4 + c + 1) * 512],
                                  start=True, stop=True)
                S("tensor", cT, t_es)

                def v_es(v, qq=qq):
                    v.tensor_copy(lse4[:, qq * 2048:(qq + 1) * 2048],
                                  ps8[0:4, :, :].rearrange("p a b -> p (a b)"))
                S("vector", cV, v_es)
            S("scalar", cA, lambda a: a.activation(out=lse4[:], in_=lse4[:], func=AF.Ln))
            for qq in range(4):
                def t_bc(t_, qq=qq):
                    for c in range(4):
                        t_.matmul(ps8[:, c, :], bmapT[:],
                                  lse4[:, (qq * 4 + c) * 512:(qq * 4 + c + 1) * 512],
                                  start=True, stop=True)
                S("tensor", cT, t_bc)

                def v_bc(v, qq=qq):
                    v.tensor_tensor(sink[:, qq * 2048:(qq + 1) * 2048],
                                    sink[:, qq * 2048:(qq + 1) * 2048],
                                    ps8[:, :, :].rearrange("p a b -> p (a b)"),
                                    ALU.subtract)
                S("vector", cV, v_bc)

            def v_mx(v):
                v.reduce_max(sm[:, 2:3], sink[:], axis=AX)
                v.drain()
                v.tensor_scalar(sm[:, 3:4], sm[:, 2:3], -1.0, None, ALU.mult)
            S("vector", cV, v_mx)

            def a_nl(a):
                a.activation(out=exps[:], in_=sink[:], func=AF.Exp,
                             bias=sm[:, 3:4], accum_out=sm[:, 4:5])
                a.drain()
                a.activation(out=sm[:, 5:6], in_=sm[:, 4:5], func=AF.Ln)
            S("scalar", cA, a_nl)

            def v_sub(v):
                v.tensor_tensor(sm[:, 6:7], sm[:, 5:6], sm[:, 2:3], ALU.add)
                v.drain()
                v.tensor_scalar(sink[:], sink[:], sm[:, 6:7], None, ALU.subtract)
            S("vector", cV, v_sub)

        def v_bis(v):
            v.memset(sm[:, 8:9], -40.0)
            v.memset(sm[:, 9:10], 1e-3)
            v.drain()
            for _ in range(BIS_IT):
                v.tensor_tensor(sm[:, 10:11], sm[:, 8:9], sm[:, 9:10], ALU.add)
                v.drain()
                v.tensor_scalar(sm[:, 11:12], sm[:, 10:11], 0.5, None, ALU.mult)
                v.drain()
                v.tensor_scalar(exps[:], sink[:], sm[:, 11:12], 0.0,
                                ALU.is_gt, ALU.add, accum_out=sm[:, 12:13])
                v.drain()
                v.tensor_scalar(smu[:, 0:1], sm[:, 12:13], float(CAP), None, ALU.is_ge)
                v.tensor_scalar(smu[:, 1:2], sm[:, 12:13], float(CAP), None, ALU.is_lt)
                v.drain()
                v.select(sm[:, 8:9], smu[:, 0:1], sm[:, 11:12], sm[:, 8:9])
                v.select(sm[:, 9:10], smu[:, 1:2], sm[:, 11:12], sm[:, 9:10])
                v.drain()
        S("vector", cV, v_bis)

        def v_rank(v):
            v.tensor_scalar(exps[:], sink[:], sm[:, 8:9], None, ALU.is_gt)
            v.drain()
            v.tensor_tensor_scan(rank[:], exps[:], exps[:], 0.0, ALU.add, ALU.bypass)
            v.drain()
            v.tensor_tensor(rank[:], rank[:], exps[:], ALU.mult)
        S("vector", cV, v_rank)

        scope1.close()
        scope2 = ExitStack()
        town = self.sb("r_town", (8, 2 * N2), F32, stack=scope2)
        def v_csel(v):
            v.tensor_copy(ccb[:], cstt[0:1, 2:3])
        S("vector", cV, v_csel)
        S("tensor", cT, lambda t_: t_.matmul(ps8[:, 0, 0:1], onesr64[0:1, 0:64],
                                             ccb[0:1, :], start=True, stop=True))

        def v_oh(v):
            v.tensor_copy(sm[:, 14:15], ps8[0:64, 0, 0:1])
            v.drain()
            v.scalar_tensor_tensor(sm[:, 15:16], sm[:, 14:15], -8.0, sm[:, 0:1],
                                   ALU.mult, ALU.add)
            v.drain()
            v.tensor_scalar(onesel[:], i8t[:], sm[:, 15:16], None, ALU.is_equal)
        S("vector", cV, v_oh)

        for src_t, off in ((sink, 0), (rank, N2)):
            for qq in range(4):
                def t_ex(t_, src_t=src_t, qq=qq):
                    for c in range(4):
                        t_.matmul(ps8[0:8, c, :], onesel[:],
                                  src_t[:, (qq * 4 + c) * 512:(qq * 4 + c + 1) * 512],
                                  start=True, stop=True)
                S("tensor", cT, t_ex)

                def v_ex(v, off=off, qq=qq):
                    v.tensor_copy(town[0:8, off + qq * 2048:off + (qq + 1) * 2048],
                                  ps8[0:8, :, :].rearrange("p a b -> p (a b)"))
                S("vector", cV, v_ex)

        for grp in range(8):
            def t_tr(t_, grp=grp):
                for j in range(8):
                    blk = grp * 8 + j
                    t_.transpose(psT[:, j, 0:8], town[0:8, blk * 128:(blk + 1) * 128],
                                 self.identf[0:8, 0:8])
                    t_.transpose(psT[:, j, 8:16],
                                 town[0:8, N2 + blk * 128:N2 + (blk + 1) * 128],
                                 self.identf[0:8, 0:8])
            S("tensor", cT, t_tr)

            def v_tr(v, grp=grp):
                v.tensor_copy(tstg[:, grp * 8:(grp + 1) * 8, :], psT[:])
            S("vector", cV, v_tr)

        scope2.close()
        selT = self.sb("r_selT", (128, 64, 256), BF16, stack=ps)
        rhs2 = self.sb("r_rhs2", (128, 64, 4), BF16, stack=ps)

        def v_split(v):
            t8 = tstg[:, :, 0:8]
            v.tensor_copy(tbhi[:], t8)
            v.drain()
            v.tensor_tensor(tres[:], t8, tbhi[:], ALU.subtract)
            v.drain()
            v.tensor_copy(tblo[:], tres[:])
            v.drain()
            v.tensor_copy(rhs2[:, :, 0:1], iota_blk[:].rearrange("p (a o) -> p a o", o=1))
            v.tensor_copy(rhs2[:, :, 1:2], iota_pc[:].rearrange("p (a o) -> p a o", o=1))
        S("vector", cV, v_split)

        for r in range(8):
            def v_sel(v, r=r):
                rk = tstg[:, :, 8 + r:9 + r]
                for blk in range(64):
                    v.tensor_scalar(selT[:, blk, :], self.iota_cap[:],
                                    rk[:, blk, :], None, ALU.is_equal)
                v.drain()
                v.tensor_copy(rhs2[:, :, 2:3], tbhi[:, :, r:r + 1])
                v.tensor_copy(rhs2[:, :, 3:4], tblo[:, :, r:r + 1])
            S("vector", cV, v_sel)

            def t_idx(t_, r=r):
                for hh in range(2):
                    for blk in range(64):
                        t_.matmul(psX[:, hh, :], selT[:, blk, hh * 128:(hh + 1) * 128],
                                  rhs2[:, blk, :], start=(blk == 0), stop=(blk == 63))
            S("tensor", cT, t_idx)

            def v_fin(v, r=r):
                for hh in range(2):
                    c_ = r * 2 + hh
                    v.tensor_copy(xev[:], psX[:, hh, 0:4])
                    v.drain()
                    q_ = xev[:]
                    v.scalar_tensor_tensor(excol[:, c_:c_ + 1], q_[:, 0:1], 128.0,
                                           q_[:, 1:2], ALU.mult, ALU.add)
                    v.tensor_tensor(gcol[:, c_:c_ + 1], q_[:, 2:3],
                                    q_[:, 3:4], ALU.add)
                    v.drain()
                v.tensor_copy(excb[:, r * 2:r * 2 + 2], excol[:, r * 2:r * 2 + 2])
            S("vector", cV, v_fin)

        S("scalar", cA, lambda a: a.activation(out=gcol[:], in_=gcol[:], func=AF.Exp))

        def g_st(g):
            for r in range(8):
                for hh in range(2):
                    c_ = r * 2 + hh
                    dG.inc(g.dma_start(out=self.ap("idx_own")[r, hh * 128:(hh + 1) * 128],
                                       in_=excb[:, c_:c_ + 1]), 16)
                    dG.inc(g.dma_start(out=self.ap("gates_own")[r, hh * 128:(hh + 1) * 128],
                                       in_=gcol[:, c_:c_ + 1]), 16)
        S("gpsimd", None, g_st)

        if "sink_dump" in self.dbg:
            def g_dmp(g):
                dG.inc(g.dma_start(out=self.ap("sink_dump"), in_=sink[:]), 16)
                dG.inc(g.dma_start(out=self.ap("rank_dump"), in_=rank[:]), 16)
            S("gpsimd", None, g_dmp)

        def g_ag(g):
            cC.inc(g.collective_compute(
                "AllGather", ALU.bypass, replica_groups=RG,
                ins=[self.ap("idx_own").opt()],
                outs=[self.ap("idx_all")[0:64, :].opt()]))
        S("gpsimd", None, g_ag)
        S("gpsimd", None, lambda g: g.wait_ge(cC.sem, cC.n))

    def ffn_E(self, block, ps):
        dY, dS, dG = self.dY, self.dS, self.dG
        cT, cV, cA, cG, cC = self.cT, self.cV, self.cA, self.cG, self.cC

        w1b = self.sb("e_w1", (128, 8, HID), BF16, stack=ps)
        w2b = self.sb("e_w2", (128, 32, DIM), BF16, stack=ps)
        b1s = self.sb("e_b1", (128, HID // 128), F32, stack=ps)
        b2s = self.sb("e_b2", (128, DIM), F32, stack=ps)
        idxs = self.sb("e_idxs", (128, 2), U32, stack=ps)
        idxf = self.sb("e_idxf", (128, 2), F32, stack=ps)
        gts = self.sb("e_gts", (128, 2), F32, stack=ps)
        ei = self.sb("e_ei", (128, 2, DIM), BF16, stack=ps)
        eiT = self.sb("e_eiT", (128, 8, 256), BF16, stack=ps)
        h1T = self.sb("e_h1T", (128, 32, 256), BF16, stack=ps)
        eos = self.sb("e_eos", (128, DIM), F32, stack=ps)
        eob = self.sb("e_eob", (128, 2, DIM), BF16, stack=ps)
        psE1 = self.psum("e_ps1", (128, 4, 256), stack=ps)
        psE2 = self.psum("e_ps2", (128, 2, DIM), stack=ps)
        psTr = self.psum("e_psT", (128, 8, 128), BF16, stack=ps)

        S = lambda e, c, f: self.seq(block, e, c, f)

        for e_l in range(E_OWN):
            def g_w(g, e_l=e_l):
                dG.inc(g.dma_start(out=w1b[:], in_=self.ap("w1")[e_l].rearrange(
                    "(k p) n -> p k n", p=128)), 16)
                dG.inc(g.dma_start(out=w2b[:], in_=self.ap("w2")[e_l].rearrange(
                    "(k p) n -> p k n", p=128)), 16)
                dG.inc(g.dma_start(out=b1s[:], in_=self.ap("b1c")[e_l]), 16)
                dG.inc(g.dma_start(out=b2s[:], in_=self.ap("b2t")[e_l]), 16)
            S("gpsimd", None, g_w)
            for b_ in range(4):
                r = e_l * 4 + b_

                def g_ix(g, r=r):
                    for hh in range(2):
                        dG.inc(g.dma_start(out=idxs[:, hh:hh + 1],
                                           in_=self.ap("idx_own")[r, hh * 128:(hh + 1) * 128]), 16)
                        dG.inc(g.dma_start(out=gts[:, hh:hh + 1],
                                           in_=self.ap("gates_own")[r, hh * 128:(hh + 1) * 128]), 16)
                S("gpsimd", None, g_ix)

                def v_ix(v, b_=b_):
                    v.tensor_copy(idxf[:], idxs[:])
                    v.drain()
                    v.tensor_scalar(idxf[:], idxf[:], float(8192 * b_), None, ALU.add)
                    v.drain()
                    v.tensor_copy(idxs[:], idxf[:])
                S("vector", cV, v_ix)

                def g_ga(g):
                    for hh in range(2):
                        dG.inc(g.indirect_dma_start(
                            out=ei[:, hh, :], out_offset=None,
                            in_=self.ap("ln1_all")[0:128, :],
                            in_offset=IndirectOffsetOnAxis(idxs[:, hh:hh + 1], 0)), 16)
                S("gpsimd", None, g_ga)

                for hh in range(2):
                    def t_tr(t_, hh=hh):
                        for k in range(8):
                            t_.transpose(psTr[:, k, :], ei[:, hh, k * 128:(k + 1) * 128],
                                         self.ident[:])
                    S("tensor", cT, t_tr)

                    def v_tr(v, hh=hh):
                        v.tensor_copy(eiT[:, :, hh * 128:(hh + 1) * 128], psTr[:])
                    S("vector", cV, v_tr)

                for mg in range(8):
                    def t_h1(t_, mg=mg):
                        for mm_ in range(4):
                            m = mg * 4 + mm_
                            for k in range(8):
                                t_.matmul(psE1[:, mm_, :],
                                          w1b[:, k, m * 128:(m + 1) * 128],
                                          eiT[:, k, :], start=(k == 0), stop=(k == 7))
                    S("tensor", cT, t_h1)

                    def a_h1(a, mg=mg):
                        for mm_ in range(4):
                            m = mg * 4 + mm_
                            a.activation(out=h1T[:, m, :], in_=psE1[:, mm_, :],
                                         func=AF.Gelu_apprx_tanh, bias=b1s[:, m:m + 1])
                    S("scalar", cA, a_h1)

                def t_eo(t_):
                    for sh in range(2):
                        for k2 in range(32):
                            t_.matmul(psE2[:, sh, 0:512],
                                      h1T[:, k2, sh * 128:(sh + 1) * 128],
                                      w2b[:, k2, 0:512], start=(k2 == 0), stop=(k2 == 31))
                            t_.matmul(psE2[:, sh, 512:1024],
                                      h1T[:, k2, sh * 128:(sh + 1) * 128],
                                      w2b[:, k2, 512:1024], start=(k2 == 0), stop=(k2 == 31))
                S("tensor", cT, t_eo)

                def v_eo(v):
                    for sh in range(2):
                        v.tensor_tensor(eos[:], psE2[:, sh, :], b2s[:], ALU.add)
                        v.drain()
                        v.tensor_scalar(eob[:, sh, :], eos[:], gts[:, sh:sh + 1],
                                        None, ALU.mult)
                        v.drain()
                S("vector", cV, v_eo)

                def g_eo(g, r=r):
                    for sh in range(2):
                        dG.inc(g.dma_start(
                            out=self.ap("eo_own")[r, sh * 128:(sh + 1) * 128, :],
                            in_=eob[:, sh, :]), 16)
                S("gpsimd", None, g_eo)

        def g_ag(g):
            cC.inc(g.collective_compute(
                "AllGather", ALU.bypass, replica_groups=RG,
                ins=[self.ap("eo_own").rearrange("a b c -> (a b) c").opt()],
                outs=[self.ap("eo_all").rearrange("a b c -> (a b) c").opt()]))
        S("gpsimd", None, g_ag)
        S("gpsimd", None, lambda g: g.wait_ge(cC.sem, cC.n))

    def ffn_C(self, block, ps):
        dY, dS, dG = self.dY, self.dS, self.dG
        cT, cV, cA, cG, cC = self.cT, self.cV, self.cA, self.cG, self.cC

        zt = self.sb("c_zt", (128, DIM), F32, stack=ps)
        cstt = self.sb("c_cst", (1, 8), F32, stack=ps)
        ccb2 = self.sb("c_ccb", (1, 2), BF16, stack=ps)
        onesr = self.sb("c_onesr", (1, 128), BF16, stack=ps)
        smc = self.sb("c_smc", (128, 8), F32, stack=ps)
        ipcol = self.sb("c_ip", (128, 1), F32, stack=ps)
        rowu = self.sb("c_rowu", (128, 1), U32, stack=ps)
        idg = self.sb("c_idg", (128, 256), F32, stack=ps)
        idc = self.sb("c_idc", (128, 2, 16), F32, stack=ps)
        eidx = self.sb("c_eidx", (128, 2), U32, stack=ps)
        eidf = self.sb("c_eidf", (128, 2), F32, stack=ps)
        eosb = self.sb("c_eosb", (128, 2, DIM), BF16, stack=ps)
        scol = self.sb("c_scol", (128, 2), U32, stack=ps)
        scu = self.sb("c_scu", (128, 1), U32, stack=ps)
        scf = self.sb("c_scf", (128, 8), F32, stack=ps)
        psC = self.psum("c_psC", (128, 2, 16), stack=ps)

        S = lambda e, c, f: self.seq(block, e, c, f)

        S("vector", cV, lambda v: (v.memset(zt[:], 0.0), v.memset(onesr[:], 1.0)))

        def g_zero(g):
            accv = self.ap("acc")
            for r0 in range(0, NTOK + 256, 128):
                dG.inc(g.dma_start(out=accv[r0:r0 + 128, :], in_=zt[:]), 16)
            dG.inc(g.dma_start(out=cstt[:], in_=self.ap("cconst")), 16)
        S("gpsimd", None, g_zero)

        def g_iotap(g):
            g.iota(ipcol[:], pattern=[[0, 1]], base=0, channel_multiplier=1,
                   allow_small_or_imprecise_dtypes=True)
        S("gpsimd", cG, g_iotap)

        S("vector", cV, lambda v: v.tensor_copy(ccb2[:], cstt[0:1, 0:2]))
        S("tensor", cT, lambda t_: t_.matmul(psC[:, 0, 0:2], onesr[0:1, :],
                                             ccb2[0:1, :], start=True, stop=True))

        def v_bc(v):
            v.tensor_copy(smc[:, 0:2], psC[:, 0, 0:2])
            v.drain()
            v.scalar_tensor_tensor(smc[:, 2:3], ipcol[:], 4.0, smc[:, 1:2],
                                   ALU.mult, ALU.add)
            v.drain()
            v.tensor_copy(rowu[:], smc[:, 2:3])
        S("vector", cV, v_bc)

        def g_gidx(g):
            dG.inc(g.indirect_dma_start(
                out=idg[:], out_offset=None,
                in_=self.ap("idx_all")[0:128, :],
                in_offset=IndirectOffsetOnAxis(rowu[:], 0),
                bounds_check=63, oob_is_err=False), 16)
        S("gpsimd", None, g_gidx)

        for hh in range(2):
            def t_tri(t_, hh=hh):
                t_.transpose(psC[:, hh, :], idg[0:16, hh * 128:(hh + 1) * 128],
                             self.identf[0:16, 0:16])
            S("tensor", cT, t_tri)
        S("vector", cV, lambda v: v.tensor_copy(idc[:], psC[:]))

        for e in range(16):
            def v_ec(v, e=e):
                for hh in range(2):
                    v.tensor_scalar(scf[:, hh:hh + 1], ipcol[:],
                                    float(128 * hh + 1024 * e), None, ALU.add)
                v.drain()
                for hh in range(2):
                    v.scalar_tensor_tensor(eidf[:, hh:hh + 1], smc[:, 1:2], 256.0,
                                           scf[:, hh:hh + 1], ALU.mult, ALU.add)
                v.drain()
                v.tensor_copy(eidx[:], eidf[:])
            S("vector", cV, v_ec)

            def g_geo(g, e=e):
                eof = self.ap("eo_all").rearrange("a b c -> (a b) c")
                for hh in range(2):
                    dG.inc(g.indirect_dma_start(
                        out=eosb[:, hh, :], out_offset=None,
                        in_=eof[0:128, :],
                        in_offset=IndirectOffsetOnAxis(eidx[:, hh:hh + 1], 0)), 16)
            S("gpsimd", None, g_geo)

            def v_sc(v, e=e):
                for hh in range(2):
                    d_ = scf[:, 2 + hh:3 + hh]
                    v.tensor_tensor(d_, idc[:, hh, e:e + 1], smc[:, 0:1], ALU.subtract)
                    v.drain()
                    v.tensor_scalar(scf[:, 4:5], d_, 0.0, None, ALU.is_ge)
                    v.tensor_scalar(scf[:, 5:6], d_, float(NTOK), None, ALU.is_lt)
                    v.drain()
                    v.tensor_tensor(scf[:, 6:7], scf[:, 4:5], scf[:, 5:6], ALU.mult)
                    v.tensor_scalar(scf[:, 7:8], ipcol[:], float(NTOK + 128 * hh),
                                    None, ALU.add)
                    v.drain()
                    v.tensor_scalar(scu[:, 0:1], scf[:, 6:7], 0.0, None, ALU.is_equal)
                    v.drain()
                    v.select(scf[:, 2 + hh:3 + hh], scu[:, 0:1], scf[:, 7:8], d_)
                    v.drain()
                v.tensor_copy(scol[:], scf[:, 2:4])
            S("vector", cV, v_sc)

            for hh in range(2):
                def g_sc(g, hh=hh):
                    dG.inc(g.indirect_dma_start(
                        out=self.ap("acc"),
                        out_offset=IndirectOffsetOnAxis(scol[:, hh:hh + 1], 0),
                        in_=eosb[:, hh, :], in_offset=None,
                        compute_op=ALU.add, oob_is_err=True,
                        bounds_check=NTOK + 255), 16)
                S("gpsimd", None, g_sc)

        accin = self.sb("c_acc", (128, 2, DIM), F32, stack=ps)
        x1in = self.sb("c_x1", (128, 2, DIM), F32, stack=ps)
        sq = self.sb("c_sq", (128, DIM), F32, stack=ps)
        stat = self.sb("c_stat", (128, 2, 16), F32, stack=ps)
        xcf = self.sb("c_xcf", (128, 2, DIM), F32, stack=ps)
        lnf = self.sb("c_lnf", (128, 2, DIM), F32, stack=ps)
        outt = self.sb("c_out", (128, 2, DIM), F32, stack=ps)

        f0 = dict(T=cT.n, V=cV.n, A=cA.n, dY=dY.n, dS=dS.n, dG=dG.n)
        a_ld = []

        @block.sync
        def _(y):
            for c, v in zip((cV, cA, cG, dG), (f0["V"], f0["A"], cG.n, f0["dG"])):
                if v > 0:
                    y.wait_ge(c.sem, v)
            accv = self.tm("acc") if False else self.ap("acc").rearrange(
                "(t p) f -> p t f", p=128)
            x1v = self.tm("x1_tok")
            for t in range(NT):
                if t >= 2:
                    y.wait_ge(cV.sem, f0["V"] + 2 * (t - 1))
                dY.inc(y.dma_start(out=accin[:, t % 2, :], in_=accv[:, t, :]), 16)
                a_ld.append(dY.inc(y.dma_start(out=x1in[:, t % 2, :], in_=x1v[:, t, :]), 16))

        @block.vector
        def _(v):
            for t in range(NT):
                v.wait_ge(dY.sem, a_ld[t])
                if t >= 2:
                    v.wait_ge(cA.sem, f0["A"] + (t - 1))
                st = stat[:, t % 2, :]
                at = accin[:, t % 2, :]
                v.tensor_tensor(sq[:], at, at, ALU.mult)
                v.reduce_sum(st[:, 0:1], at, axis=AX)
                v.drain()
                v.reduce_sum(st[:, 1:2], sq[:], axis=AX)
                v.tensor_scalar(st[:, 2:3], st[:, 0:1], 1.0 / DIM, None, ALU.mult)
                v.drain()
                v.tensor_scalar(st[:, 3:4], st[:, 1:2], 1.0 / DIM, None, ALU.mult)
                v.tensor_tensor(st[:, 4:5], st[:, 2:3], st[:, 2:3], ALU.mult)
                v.tensor_scalar(xcf[:, t % 2, :], at, st[:, 2:3], None, ALU.subtract)
                v.drain()
                v.tensor_tensor(st[:, 5:6], st[:, 3:4], st[:, 4:5], ALU.subtract)
                cV.inc(v.drain())
                v.wait_ge(cA.sem, f0["A"] + t + 1)
                if t >= 2:
                    v.wait_ge(dG.sem, f0["dG"] + 16 * (t - 1))
                v.tensor_tensor(outt[:, t % 2, :], x1in[:, t % 2, :],
                                lnf[:, t % 2, :], ALU.add)
                cV.inc(v.drain())

        @block.scalar
        def _(a):
            for t in range(NT):
                a.wait_ge(cV.sem, f0["V"] + 2 * t + 1)
                st = stat[:, t % 2, :]
                a.activation(out=st[:, 6:7], in_=st[:, 5:6], func=AF.Ln,
                             bias=self.konst[:, 0:1])
                a.drain()
                a.activation(out=st[:, 7:8], in_=st[:, 6:7], func=AF.Exp, scale=-0.5)
                a.drain()
                a.activation(out=lnf[:, t % 2, :], in_=xcf[:, t % 2, :],
                             func=AF.Copy, scale=st[:, 7:8])
                cA.inc(a.drain())

        @block.gpsimd
        def _(g):
            for t in range(NT):
                g.wait_ge(cV.sem, f0["V"] + 2 * t + 2)
                dG.inc(g.dma_start(out=self.tm("out")[:, t, :], in_=outt[:, t % 2, :]), 16)
            g.wait_ge(dG.sem, dG.n)
        self.expect(cV, f0["V"] + 2 * NT, "C V")

    def emit_debug(self, block):
        self.dbg = {d for d in self.dbg if d in self.dram}
        if not self.dbg:
            return
        for nm in sorted(self.dbg):
            t = self.dram[nm]
            self.dout("dbg_" + nm, list(t.shape), t.dtype)

        @block.gpsimd
        def _(g):
            for nm in sorted(self.dbg):
                src, dst = self.ap(nm), self.ap("dbg_" + nm)
                ins = g.dma_start(out=dst, in_=src)
                self.dG.inc(ins, 16)
            g.wait_ge(self.dG.sem, self.dG.n)


def shard_inputs(inputs):
    f32 = np.float32
    x = np.ascontiguousarray(np.asarray(inputs["L0_attn_x"], f32).reshape(NC * NTOK, DIM))
    qkv_w = np.ascontiguousarray(np.asarray(inputs["L0_attn_qkv_w"], f32))
    qkv_b = np.ascontiguousarray(
        np.broadcast_to(np.asarray(inputs["L0_attn_qkv_b"], f32)[None, :], (128, 3 * DIM)))

    def tiled(v):
        return np.ascontiguousarray(
            np.broadcast_to(np.tile(np.asarray(v, f32), HEADS)[None, :], (128, DIM)))

    qn_s, qn_b = tiled(inputs["L0_attn_qn_scale"]), tiled(inputs["L0_attn_qn_bias"])
    kn_s, kn_b = tiled(inputs["L0_attn_kn_scale"]), tiled(inputs["L0_attn_kn_bias"])
    G0 = np.ascontiguousarray(np.asarray(inputs["L0_attn_ttt_gate"], f32))
    V0 = np.ascontiguousarray(np.asarray(inputs["L0_attn_ttt_val"], f32))
    PT0 = np.ascontiguousarray(np.asarray(inputs["L0_attn_ttt_proj"], f32).T)
    proj_w = np.ascontiguousarray(np.asarray(inputs["L0_attn_proj_w"], f32))
    proj_b = np.ascontiguousarray(
        np.broadcast_to(np.asarray(inputs["L0_attn_proj_b"], f32)[None, :], (128, DIM)))
    gqT = np.ascontiguousarray(np.asarray(inputs["L0_ffn_gate_query"], f32).T)
    log_temp = np.asarray(inputs["L0_ffn_log_temp"], f32).reshape(1, 1)
    w1 = np.asarray(inputs["L0_ffn_w1"], f32)
    b1 = np.asarray(inputs["L0_ffn_b1"], f32)
    w2 = np.asarray(inputs["L0_ffn_w2"], f32)
    b2 = np.asarray(inputs["L0_ffn_b2"], f32)

    maps = []
    for c in range(NC):
        es = slice(2 * c, 2 * c + 2)
        b1c = np.ascontiguousarray(
            b1[es].reshape(E_OWN, HID // 128, 128).transpose(0, 2, 1))
        b2t = np.ascontiguousarray(
            np.broadcast_to(b2[es][:, None, :], (E_OWN, 128, DIM)))
        cs_ = slice(512 * c, 512 * (c + 1))
        cconst = np.zeros((1, 8), f32)
        cconst[0, 0] = (c % 2) * NTOK
        cconst[0, 1] = c // 2
        cconst[0, 2] = c
        maps.append({
            "x": np.ascontiguousarray(x[c * NTOK:(c + 1) * NTOK]),
            "qkv_w": qkv_w, "qkv_b": qkv_b,
            "qn_scale": qn_s, "qn_bias": qn_b,
            "kn_scale": kn_s, "kn_bias": kn_b,
            "ttt_gate": G0, "ttt_val": V0, "ttt_projT": PT0,
            "proj_w": proj_w, "proj_b": proj_b,
            "gate_queryT": gqT, "log_temp": log_temp,
            "gate_sh": np.ascontiguousarray(G0[:, cs_]),
            "val_sh": np.ascontiguousarray(V0[:, cs_]),
            "proj_sh": np.ascontiguousarray(PT0[:, cs_]),
            "w1": np.ascontiguousarray(w1[es]), "b1c": b1c,
            "w2": np.ascontiguousarray(w2[es]), "b2t": b2t,
            "cconst": cconst,
        })
    return maps


def run(inputs, dbg=(), race=False):
    b = B(dbg=dbg, race=race)
    nc = b.build()
    maps = shard_inputs(inputs)
    res = run_bass_kernel_spmd(nc, maps, core_ids=list(range(NC)))
    return res


def kernel(**inputs):
    res = run(inputs)
    out = np.concatenate([res.results[c]["out"] for c in range(NC)], axis=0)
    return out.reshape(4, 8192, DIM)


# version 07
# speedup vs baseline: 1.0235x; 1.0235x over previous
import sys
sys.path.insert(0, "/opt/trn_rl_repo")
import numpy as np
import ml_dtypes
from contextlib import ExitStack
import concourse.bass as bass
import concourse.mybir as mybir
from concourse.bass_core_rust import IndirectOffsetOnAxis
from concourse.bass_utils import run_bass_kernel_spmd

F32 = mybir.dt.float32
BF16 = mybir.dt.bfloat16
U32 = mybir.dt.uint32
AF = mybir.ActivationFunctionType
ALU = mybir.AluOpType
AX = mybir.AxisListType.X

NC = 8
NTOK = 4096
NT = NTOK // 128
DIM = 1024
HID = 4096
HEADS, HD = 16, 64
LR = 0.1
NS_ITERS, TTT_STEPS = 5, 2
NDTOT = 32768 * DIM
E_OWN, CAP, SINK_IT, BIS_IT = 2, 256, 6, 26
RG = [list(range(NC))]
NS_A, NS_B, NS_C = 3.4445, -4.775, 2.0315


class Ch:
    def __init__(self, nc, ctx, name):
        self.sem = ctx.enter_context(nc.semaphore(name))
        self.n = 0

    def inc(self, instr, k=1):
        instr.then_inc(self.sem, k)
        self.n += k
        return self.n


class B:
    def __init__(self, dbg=(), race=True):
        self.dbg = set(dbg)
        self.nc = bass.Bass("TRN2", target_bir_lowering=False, num_devices=NC,
                            detect_race_conditions=race)
        self.ctx = ExitStack()
        self.dram = {}

    def din(self, name, shape, dt=F32):
        self.dram[name] = self.nc.dram_tensor(name, list(shape), dt, kind="ExternalInput")

    def dout(self, name, shape, dt=F32):
        self.dram[name] = self.nc.dram_tensor(name, list(shape), dt, kind="ExternalOutput")

    def dint(self, name, shape, dt=F32):
        self.dram[name] = self.nc.dram_tensor(name, list(shape), dt, kind="Internal")

    _uid = 0

    def _nm(self, name):
        B._uid += 1
        return f"{name}_{B._uid}"

    def sb(self, name, shape, dt=F32, stack=None):
        return (stack or self.ctx).enter_context(
            self.nc.sbuf_tensor(self._nm(name), list(shape), dt))

    def psum(self, name, shape, dt=F32, stack=None):
        return (stack or self.ctx).enter_context(
            self.nc.psum_tensor(self._nm(name), list(shape), dt))

    def ap(self, name):
        return self.dram[name].ap()

    def tm(self, name):
        return self.ap(name).rearrange("(t p) f -> p t f", p=128)

    def declare_io(self):
        d = self.din
        d("x", (NTOK, DIM))
        d("qkv_w", (DIM, 3 * DIM))
        d("qkv_b", (128, 3 * DIM))
        d("qn_scale", (128, DIM))
        d("qn_bias", (128, DIM))
        d("kn_scale", (128, DIM))
        d("kn_bias", (128, DIM))
        d("ttt_gate", (DIM, HID))
        d("ttt_val", (DIM, HID))
        d("ttt_projT", (DIM, HID))
        d("proj_w", (DIM, DIM))
        d("proj_b", (128, DIM))
        d("gate_queryT", (DIM, 16))
        d("log_temp", (1, 1))
        d("w1", (E_OWN, DIM, HID))
        d("b1c", (E_OWN, 128, HID // 128))
        d("w2", (E_OWN, HID, DIM))
        d("b2t", (E_OWN, 128, DIM))
        d("cconst", (1, 8))
        d("gate_sh", (DIM, 512))
        d("val_sh", (DIM, 512))
        d("proj_sh", (DIM, 512))
        self.dout("out", (NTOK, DIM))

        di = self.dint
        di("ln0b", (NTOK, DIM), BF16)
        di("q_tok", (NTOK, DIM), BF16)
        di("k_tok", (NTOK, DIM), BF16)
        di("vc_tok", (NTOK, DIM), BF16)
        di("h_tok", (NTOK, HID), BF16)
        di("sw_tok", (NTOK, HID), BF16)
        di("uswp_tok", (NTOK, HID), BF16)
        di("R_tok", (NTOK, DIM), BF16)
        di("dV_tok", (NTOK, HID), BF16)
        di("dG_tok", (NTOK, HID), BF16)
        di("predq_tok", (NTOK, DIM), BF16)
        di("G_work", (8, DIM, 512), BF16)
        di("V_work", (8, DIM, 512), BF16)
        di("PT_work", (8, DIM, 512), BF16)
        di("G_m", (DIM, 512))
        di("V_m", (DIM, 512))
        di("PT_m", (DIM, 512))
        for g in ("gP", "gV", "gG"):
            di(g + "_part", (HID, DIM), BF16)
            di(g + "_rs", (HID // NC, DIM), BF16)
        for p in range(3):
            di(f"nsAp_{p}", (DIM, DIM), BF16)
            di(f"nsA_{p}", (DIM, DIM), BF16)
            di(f"nsXsp_{p}", (DIM, 512), BF16)
        di("norm_part", (1, 3))
        di("norm_out", (1, 3))
        di("x1_tok", (NTOK, DIM))
        di("ln1_tok", (NTOK, DIM), BF16)
        di("ln1_all", (NC * NTOK, DIM), BF16)
        di("logitsT_own", (16, NTOK))
        di("logitsT_all", (NC * 16, NTOK))
        di("trank_own", (16, 8192))
        di("idx_own", (8, CAP), U32)
        di("idx_all", (128, CAP), U32)
        di("gates_own", (8, CAP))
        di("gates_all", (NC * 8, CAP))
        di("eo_own", (8, CAP, DIM), BF16)
        di("eo_all", (NC * 8, CAP, DIM), BF16)
        di("acc", (NTOK + 256, DIM))
        di("sink_dump", (64, 8192))
        di("rank_dump", (64, 8192))

    def barrier(self, block):
        chans = [self.cT, self.cV, self.cA, self.cG, self.dY, self.dS, self.dG, self.cC]
        vals = [c.n for c in chans]

        def mk(ch):
            def f(eng):
                eng.drain()
                ch.inc(eng.nop())
                for c, v in zip(chans, vals):
                    if v > 0:
                        eng.wait_ge(c.sem, v)
            return f
        block.tensor(mk(self.cT))
        block.vector(mk(self.cV))
        block.scalar(mk(self.cA))
        block.gpsimd(mk(self.cG))

        def fy(eng):
            for c, v in zip(chans, vals):
                if v > 0:
                    eng.wait_ge(c.sem, v)
        block.sync(fy)

    def expect(self, ch, val, tag):
        assert ch.n == val, f"{tag}: predicted {val} != actual {ch.n}"

    def build(self):
        nc = self.nc
        self.declare_io()
        ctx = self.ctx
        self.dY = Ch(nc, ctx, "dY")
        self.dS = Ch(nc, ctx, "dS")
        self.dG = Ch(nc, ctx, "dG")
        self.cT = Ch(nc, ctx, "cT")
        self.cV = Ch(nc, ctx, "cV")
        self.cA = Ch(nc, ctx, "cA")
        self.cG = Ch(nc, ctx, "cG")
        self.cC = Ch(nc, ctx, "cC")

        self.konst = self.sb("konst", (128, 4))
        self.ident = self.sb("ident", (128, 128), BF16)
        self.identf = self.sb("identf", (128, 128), F32)
        self.iota_cap = self.sb("iota_cap", (128, 256))
        self.iota_np = self.sb("iota_np", (128, 64))

        with nc.Block() as block:
            @block.vector
            def _(v):
                v.memset(self.konst[:, 0:1], 1e-5)
                v.memset(self.konst[:, 1:2], 1e-6)
                v.memset(self.konst[:, 2:4], 0.0)
                v.memset(self.ident[:], 0.0)
                v.memset(self.identf[:], 0.0)
                self.cV.inc(v.drain())

            @block.gpsimd
            def _(g):
                g.wait_ge(self.cV.sem, self.cV.n)
                for t in (self.ident, self.identf):
                    g.affine_select(out=t[:], in_=t[:],
                                    compare_op=ALU.not_equal, fill=1.0, base=0,
                                    pattern=[[-1, 128]], channel_multiplier=1)
                g.iota(self.iota_cap[:], pattern=[[1, 256]], base=1,
                       channel_multiplier=0, allow_small_or_imprecise_dtypes=True)
                g.iota(self.iota_np[:], pattern=[[128, 64]], base=0,
                       channel_multiplier=1, allow_small_or_imprecise_dtypes=True)
                for src, dst in (("ttt_gate", "G_work"), ("ttt_val", "V_work"),
                                 ("ttt_projT", "PT_work")):
                    for b_ in range(8):
                        ins = g.dma_start(out=self.ap(dst)[b_],
                                          in_=self.ap(src)[:, b_ * 512:(b_ + 1) * 512])
                        self.dG.inc(ins, 16)
                self.cG.inc(g.nop())
                self.init_done = self.cG.n

            with ExitStack() as ps:
                self.phase_a1(block, ps)
                self.barrier(block)
            with ExitStack() as ps:
                self.phase_a2(block, ps)
                self.barrier(block)
            for step in range(TTT_STEPS):
                with ExitStack() as ps:
                    self.ttt_F(block, ps, step)
                    self.barrier(block)
                with ExitStack() as ps:
                    self.ttt_P(block, ps, step)
                    self.barrier(block)
                with ExitStack() as ps:
                    self.ttt_D(block, ps, step)
                    self.barrier(block)
                for xn, rn, on in (("h_tok", "R_tok", "gP_part"),
                                   ("dV_tok", "k_tok", "gV_part"),
                                   ("dG_tok", "k_tok", "gG_part")):
                    with ExitStack() as ps:
                        self.ttt_G1(block, ps, xn, rn, on)
                        self.barrier(block)
                self.ttt_RS(block)
                self.barrier(block)
                with ExitStack() as ps:
                    self.ttt_NS(block, ps, step)
                    self.barrier(block)
            with ExitStack() as ps:
                self.ttt_F(block, ps, 0, query=True)
                self.barrier(block)
            with ExitStack() as ps:
                self.ttt_P(block, ps, 0, query=True)
                self.barrier(block)
            with ExitStack() as ps:
                self.ttt_O(block, ps)
                self.barrier(block)
            with ExitStack() as ps:
                self.ttt_L(block, ps)
                self.barrier(block)
            if "skipffn" not in self.dbg:
                self.run_ffn(block)
            self.emit_debug(block)
        return nc

    def run_ffn(self, block):
        if True:
            with ExitStack() as ps:
                self.ffn_R(block, ps)
                self.barrier(block)
            with ExitStack() as ps:
                self.ffn_E(block, ps)
                self.barrier(block)
            with ExitStack() as ps:
                self.ffn_C(block, ps)
                self.barrier(block)

    def phase_a1(self, block, ps):
        nc = self.nc
        dY, dS, dG = self.dY, self.dS, self.dG
        cT, cV, cA, cG = self.cT, self.cV, self.cA, self.cG

        xin = self.sb("pa_x", (128, 2, DIM), stack=ps)
        ln0t = self.sb("pa_ln0", (128, 2, DIM), BF16, stack=ps)
        stat = self.sb("pa_stat", (128, 2, 16), stack=ps)
        sq = self.sb("pa_sq", (128, DIM), stack=ps)
        xc = self.sb("pa_xc", (128, 2, DIM), stack=ps)

        x_ld = []
        v1b, a1b, ds1b = cV.n, cA.n, dS.n

        @block.sync
        def _(y):
            xv = self.tm("x")
            for t in range(NT):
                if t >= 2:
                    y.wait_ge(cV.sem, v1b + (t - 1))
                    y.wait_ge(dS.sem, ds1b + 16 * (t - 1))
                ins = y.dma_start(out=xin[:, t % 2, :], in_=xv[:, t, :])
                x_ld.append(dY.inc(ins, 16))

        @block.vector
        def _(v):
            for t in range(NT):
                v.wait_ge(dY.sem, x_ld[t])
                if t >= 2:
                    v.wait_ge(cA.sem, a1b + (t - 1))
                st = stat[:, t % 2, :]
                xt = xin[:, t % 2, :]
                v.tensor_tensor(sq[:], xt, xt, ALU.mult)
                v.reduce_sum(st[:, 0:1], xt, axis=AX)
                v.drain()
                v.reduce_sum(st[:, 1:2], sq[:], axis=AX)
                v.tensor_scalar(st[:, 2:3], st[:, 0:1], 1.0 / DIM, None, ALU.mult)
                v.drain()
                v.tensor_scalar(st[:, 3:4], st[:, 1:2], 1.0 / DIM, None, ALU.mult)
                v.tensor_tensor(st[:, 4:5], st[:, 2:3], st[:, 2:3], ALU.mult)
                v.tensor_scalar(xc[:, t % 2, :], xt, st[:, 2:3], None, ALU.subtract)
                v.drain()
                v.tensor_tensor(st[:, 5:6], st[:, 3:4], st[:, 4:5], ALU.subtract)
                cV.inc(v.drain())

        @block.scalar
        def _(a):
            a.wait_ge(cG.sem, self.init_done)
            for t in range(NT):
                a.wait_ge(cV.sem, v1b + (t + 1))
                if t >= 2:
                    a.wait_ge(dS.sem, ds1b + 16 * (t - 1))
                st = stat[:, t % 2, :]
                a.activation(out=st[:, 6:7], in_=st[:, 5:6], func=AF.Ln, bias=self.konst[:, 0:1])
                a.drain()
                a.activation(out=st[:, 7:8], in_=st[:, 6:7], func=AF.Exp, scale=-0.5)
                a.drain()
                a.activation(out=ln0t[:, t % 2, :], in_=xc[:, t % 2, :],
                             func=AF.Copy, scale=st[:, 7:8])
                cA.inc(a.drain())
                ins = a.dma_start(out=self.tm("ln0b")[:, t, :], in_=ln0t[:, t % 2, :])
                dS.inc(ins, 16)
        self.expect(cV, v1b + NT, "pa1 V")
        self.expect(cA, a1b + NT, "pa1 A")
        self.expect(dS, ds1b + 16 * NT, "pa1 dS")

    def phase_a2(self, block, ps):
        nc = self.nc
        dY, dS, dG = self.dY, self.dS, self.dG
        cT, cV, cA, cG = self.cT, self.cV, self.cA, self.cG
        ln0T = self.sb("pa_ln0T", (128, 8, NTOK), BF16, stack=ps)
        wq = self.sb("pa_wq", (128, 8, 3 * DIM), BF16, stack=ps)
        cst = self.sb("pa_cst", (128, 7 * DIM), stack=ps)
        pqkv = self.psum("pa_pqkv", (128, 3 * DIM), stack=ps)
        hs = self.sb("pa_hs", (128, 128), stack=ps)
        qn1 = self.sb("pa_qn1", (128, DIM), stack=ps)
        qn2 = self.sb("pa_qn2", (128, DIM), stack=ps)
        ob = self.sb("pa_ob", (128, 2, 3, DIM), BF16, stack=ps)

        xbar_ld = []
        all_sp1 = dS.n
        t2b, v2b, a2b, ds2b, dg2b = cT.n, cV.n, cA.n, dS.n, dG.n
        del nc

        @block.sync
        def _(y):
            for i, nm in enumerate(("qn_scale", "qn_bias", "kn_scale", "kn_bias")):
                ins = y.dma_start(out=cst[:, i * DIM:(i + 1) * DIM], in_=self.ap(nm))
                dY.inc(ins, 16)
            ins = y.dma_start(out=cst[:, 4 * DIM:7 * DIM], in_=self.ap("qkv_b"))
            cst_ld = dY.inc(ins, 16)
            y.wait_ge(dS.sem, all_sp1)
            for k in range(8):
                ins = y.dma_start(out=ln0T[:, k, :],
                                  in_=self.ap("ln0b")[:, k * 128:(k + 1) * 128],
                                  transpose=True)
                xbar_ld.append(dY.inc(ins, 16))
            self.pa_cst_ld = cst_ld

        @block.gpsimd
        def _(g):
            ins = g.dma_start(out=wq[:],
                              in_=self.ap("qkv_w").rearrange("(k p) n -> p k n", p=128))
            self.pa_wq_ld = dG.inc(ins, 16)

        @block.tensor
        def _(t_):
            t_.wait_ge(dY.sem, xbar_ld[-1])
            t_.wait_ge(dG.sem, self.pa_wq_ld)
            for t in range(NT):
                if t >= 1:
                    t_.wait_ge(cV.sem, v2b + 3 * (t - 1) + 1)
                for k in range(8):
                    lhsT = ln0T[:, k, t * 128:(t + 1) * 128]
                    for n in range(6):
                        ins = t_.matmul(pqkv[:, n * 512:(n + 1) * 512], lhsT,
                                        wq[:, k, n * 512:(n + 1) * 512],
                                        start=(k == 0), stop=(k == 7))
                cT.inc(ins)

        sqq = self.sb("pa_sqq", (128, DIM), stack=ps)
        sqk = self.sb("pa_sqk", (128, DIM), stack=ps)
        qkvb = self.sb("pa_qkvb", (128, 2, 3 * DIM), stack=ps)

        @block.vector
        def _(v):
            v.wait_ge(dY.sem, self.pa_cst_ld)
            for t in range(NT):
                v.wait_ge(cT.sem, t2b + t + 1)
                if t >= 2:
                    v.wait_ge(cA.sem, a2b + 2 * t - 3)
                    v.wait_ge(dG.sem, dg2b + 32 * (t - 1))
                v.tensor_tensor(qkvb[:, t % 2, :], pqkv[:],
                                cst[:, 4 * DIM:7 * DIM], ALU.add)
                cV.inc(v.drain())
                qb = qkvb[:, t % 2, :]
                v.wait_ge(cA.sem, a2b + 2 * t + 1)
                h3q = hs[:, 0:16].rearrange("p (h o) -> p h o", o=1)
                h3k = hs[:, 64:80].rearrange("p (h o) -> p h o", o=1)
                v.reduce_sum(h3q, sqq[:].rearrange("p (h j) -> p h j", j=HD), axis=AX)
                v.reduce_sum(h3k, sqk[:].rearrange("p (h j) -> p h j", j=HD), axis=AX)
                cV.inc(v.drain())
                v.wait_ge(cA.sem, a2b + 2 * t + 2)
                for qi, (sqcol, s_off, b_off, psl) in enumerate((
                        (16, 0, DIM, slice(0, DIM)),
                        (80, 2 * DIM, 3 * DIM, slice(DIM, 2 * DIM)))):
                    v.tensor_scalar(hs[:, sqcol + 16:sqcol + 32],
                                    hs[:, sqcol:sqcol + 16], 1e-6, None, ALU.add)
                    v.drain()
                    v.reciprocal(hs[:, sqcol + 32:sqcol + 48],
                                 hs[:, sqcol + 16:sqcol + 32])
                    v.drain()
                    f3 = hs[:, sqcol + 32:sqcol + 48].rearrange("p (h o) -> p h o", o=1)
                    q3 = qb[:, psl].rearrange("p (h j) -> p h j", j=HD)
                    o3 = qn1[:].rearrange("p (h j) -> p h j", j=HD)
                    v.tensor_tensor(o3, q3, f3.to_broadcast([128, HEADS, HD]), ALU.mult)
                    v.drain()
                    v.tensor_tensor(qn2[:], qn1[:], cst[:, s_off:s_off + DIM], ALU.mult)
                    v.drain()
                    v.tensor_tensor(ob[:, t % 2, qi, :], qn2[:],
                                    cst[:, b_off:b_off + DIM], ALU.add)
                cV.inc(v.drain())

        @block.scalar
        def _(a):
            for t in range(NT):
                a.wait_ge(cV.sem, v2b + 3 * t + 1)
                if t >= 2:
                    a.wait_ge(dS.sem, ds2b + 16 * (t - 1))
                qb = qkvb[:, t % 2, :]
                a.activation(out=sqq[:], in_=qb[:, 0:DIM], func=AF.Square)
                a.activation(out=sqk[:], in_=qb[:, DIM:2 * DIM], func=AF.Square)
                a.activation(out=ob[:, t % 2, 2, :], in_=qb[:, 2 * DIM:3 * DIM],
                             func=AF.Copy, scale=2.0 / NDTOT)
                cA.inc(a.drain())
                ins = a.dma_start(out=self.tm("vc_tok")[:, t, :], in_=ob[:, t % 2, 2, :])
                dS.inc(ins, 16)
                a.wait_ge(cV.sem, v2b + 3 * t + 2)
                a.activation(out=hs[:, 16:32], in_=hs[:, 0:16], func=AF.Sqrt)
                a.activation(out=hs[:, 80:96], in_=hs[:, 64:80], func=AF.Sqrt)
                cA.inc(a.drain())

        @block.gpsimd
        def _(g):
            for t in range(NT):
                g.wait_ge(cV.sem, v2b + 3 * t + 3)
                ins = g.dma_start(out=self.tm("q_tok")[:, t, :], in_=ob[:, t % 2, 0, :])
                dG.inc(ins, 16)
                ins = g.dma_start(out=self.tm("k_tok")[:, t, :], in_=ob[:, t % 2, 1, :])
                dG.inc(ins, 16)
        self.expect(cT, t2b + NT, "pa2 T")
        self.expect(cV, v2b + 3 * NT, "pa2 V")
        self.expect(cA, a2b + 2 * NT, "pa2 A")
        self.expect(dG, dg2b + 32 * NT + 16, "pa2 dG")

    def ttt_F(self, block, ps, step, query=False):
        dY, dS, dG = self.dY, self.dS, self.dG
        cT, cV, cA, cG = self.cT, self.cV, self.cA, self.cG
        src_tok = "q_tok" if query else "k_tok"

        kT = self.sb("f_kT", (128, 8, NTOK), BF16, stack=ps)
        gh = self.sb("f_gh", (128, 8, 4, 512), BF16, stack=ps)
        vh = self.sb("f_vh", (128, 8, 4, 512), BF16, stack=ps)
        pa = self.psum("f_pa", (128, 2048), stack=ps)
        pu = self.psum("f_pu", (128, 2048), stack=ps)
        swt = self.sb("f_swt", (128, 2, 2048), BF16, stack=ps)
        swpt = self.sb("f_swpt", (128, 2, 2048), BF16, stack=ps)
        ubt = self.sb("f_ubt", (128, 2, 2048), BF16, stack=ps)
        ht = self.sb("f_ht", (128, 2, 2048), BF16, stack=ps)
        uswpt = self.sb("f_uswpt", (128, 2, 2048), BF16, stack=ps)

        kT_ld = []
        gv_ld = []
        t0 = dict(T=cT.n, V=cV.n, A=cA.n, dG=dG.n, dS=dS.n, dY=dY.n)
        NSP = 32 if not query else 16

        @block.sync
        def _(y):
            for k in range(8):
                ins = y.dma_start(out=kT[:, k, :],
                                  in_=self.ap(src_tok)[:, k * 128:(k + 1) * 128],
                                  transpose=True)
                kT_ld.append(dY.inc(ins, 16))
            for i in range(2 * NT):
                h, t = divmod(i, NT)
                y.wait_ge(cV.sem, t0["V"] + i + 1)
                ins = y.dma_start(out=self.tm("h_tok")[:, t, h * 2048:(h + 1) * 2048],
                                  in_=ht[:, i % 2, :])
                dY.inc(ins, 16)
                if not query:
                    ins = y.dma_start(
                        out=self.tm("uswp_tok")[:, t, h * 2048:(h + 1) * 2048],
                        in_=uswpt[:, i % 2, :])
                    dY.inc(ins, 16)

        @block.gpsimd
        def _(g):
            for h in range(2):
                if h == 1:
                    g.wait_ge(cT.sem, t0["T"] + NT)
                for nm, dst in (("G_work", gh), ("V_work", vh)):
                    for b_ in range(4):
                        ins = g.dma_start(
                            out=dst[:, :, b_, :],
                            in_=self.ap(nm)[4 * h + b_].rearrange(
                                "(k p) n -> p k n", p=128))
                        dG.inc(ins, 16)
                    gv_ld.append(dG.n)

        @block.tensor
        def _(t_):
            t_.wait_ge(dY.sem, kT_ld[-1])
            for h in range(2):
                t_.wait_ge(dG.sem, gv_ld[2 * h + 1])
                for t in range(NT):
                    i = h * NT + t
                    if i >= 1:
                        t_.wait_ge(cA.sem, t0["A"] + i)
                    for k in range(8):
                        lhsT = kT[:, k, t * 128:(t + 1) * 128]
                        for b_ in range(4):
                            t_.matmul(pa[:, b_ * 512:(b_ + 1) * 512], lhsT,
                                      gh[:, k, b_, :], start=(k == 0), stop=(k == 7))
                        for b_ in range(4):
                            ins = t_.matmul(pu[:, b_ * 512:(b_ + 1) * 512], lhsT,
                                            vh[:, k, b_, :], start=(k == 0), stop=(k == 7))
                    cT.inc(ins)

        @block.scalar
        def _(a):
            for i in range(2 * NT):
                a.wait_ge(cT.sem, t0["T"] + i + 1)
                if i >= 2:
                    a.wait_ge(cV.sem, t0["V"] + i - 1)
                    if not query:
                        a.wait_ge(dS.sem, t0["dS"] + 16 * (i - 1))
                a.activation(out=swt[:, i % 2, :], in_=pa[:], func=AF.Silu)
                if not query:
                    a.activation(out=swpt[:, i % 2, :], in_=pa[:], func=AF.Derivative_silu)
                a.activation(out=ubt[:, i % 2, :], in_=pu[:], func=AF.Copy)
                cA.inc(a.drain())
                if not query:
                    h, t = divmod(i, NT)
                    ins = a.dma_start(
                        out=self.tm("sw_tok")[:, t, h * 2048:(h + 1) * 2048],
                        in_=swt[:, i % 2, :])
                    dS.inc(ins, 16)

        @block.vector
        def _(v):
            for i in range(2 * NT):
                v.wait_ge(cA.sem, t0["A"] + i + 1)
                if i >= 2:
                    v.wait_ge(dY.sem, t0["dY"] + 128 + NSP * (i - 1))
                v.tensor_tensor(ht[:, i % 2, :], swt[:, i % 2, :],
                                ubt[:, i % 2, :], ALU.mult)
                if not query:
                    v.tensor_tensor(uswpt[:, i % 2, :], ubt[:, i % 2, :],
                                    swpt[:, i % 2, :], ALU.mult)
                cV.inc(v.drain())

        self.expect(cT, t0["T"] + 2 * NT, "F T")
        self.expect(cV, t0["V"] + 2 * NT, "F V")

    def ttt_P(self, block, ps, step, query=False):
        dY, dS, dG = self.dY, self.dS, self.dG
        cT, cV, cA = self.cT, self.cV, self.cA

        hT = self.sb("p_hT", (128, 1, 32, 1024), BF16, stack=ps)
        Pb = self.sb("p_Pb", (128, 32, 1024), BF16, stack=ps)
        vct = self.sb("p_vct", (128, 4, DIM), BF16, stack=ps)
        Rt = self.sb("p_Rt", (128, 4, DIM), BF16, stack=ps)
        pp = self.psum("p_pp", (128, 4, DIM), stack=ps)
        NSC, TSC = 4, 8
        S = lambda e, c, f: self.seq(block, e, c, f)

        def s_pb(a):
            for b_ in range(8):
                for j in range(4):
                    dS.inc(a.dma_start(
                        out=Pb[:, b_ * 4 + j, :],
                        in_=self.ap("PT_work")[b_][:, j * 128:(j + 1) * 128],
                        transpose=True), 16)
        S("scalar", None, s_pb)

        def v_z(v):
            for i in range(4):
                v.memset(vct[:, i, :], 0.0)
        S("vector", cV, v_z)

        for sc in range(NSC):
            def y_h(y, sc=sc):
                for j in range(32):
                    dY.inc(y.dma_start(
                        out=hT[:, 0, j, :],
                        in_=self.ap("h_tok")[sc * 1024:(sc + 1) * 1024,
                                             j * 128:(j + 1) * 128],
                        transpose=True), 16)
            S("sync", None, y_h)
            for tg in range(2):
                gts = [sc * TSC + tg * 4 + i for i in range(4)]

                if not query:
                    def y_vc(y, gts=gts):
                        for i, gt in enumerate(gts):
                            dY.inc(y.dma_start(out=vct[:, i, :],
                                               in_=self.tm("vc_tok")[:, gt, :]), 16)
                    S("sync", None, y_vc)

                def t_mm(t_, sc=sc, tg=tg, gts=gts):
                    for i, gt in enumerate(gts):
                        t = tg * 4 + i
                        for k in range(32):
                            lhsT = hT[:, 0, k, t * 128:(t + 1) * 128]
                            t_.matmul(pp[:, i, 0:512], lhsT, Pb[:, k, 0:512],
                                      start=(k == 0), stop=(k == 31))
                            t_.matmul(pp[:, i, 512:1024], lhsT, Pb[:, k, 512:1024],
                                      start=(k == 0), stop=(k == 31))
                S("tensor", cT, t_mm)

                def v_r(v, gts=gts):
                    for i, gt in enumerate(gts):
                        v.scalar_tensor_tensor(Rt[:, i, :], pp[:, i, :],
                                               1.0 if query else 2.0 / NDTOT,
                                               vct[:, i, :], ALU.mult, ALU.subtract)
                S("vector", cV, v_r)

                def g_sp(g, gts=gts):
                    dst = "predq_tok" if query else "R_tok"
                    for i, gt in enumerate(gts):
                        dG.inc(g.dma_start(out=self.tm(dst)[:, gt, :],
                                           in_=Rt[:, i, :]), 16)
                S("gpsimd", None, g_sp)
        S("gpsimd", None, lambda g: g.wait_ge(dG.sem, dG.n))

    def ttt_D(self, block, ps, step):
        dY, dS, dG = self.dY, self.dS, self.dG
        cT, cV, cA = self.cT, self.cV, self.cA

        RT = self.sb("d_RT", (128, 8, NTOK), BF16, stack=ps)
        PTh = self.sb("d_PTh", (128, 8, 4, 512), BF16, stack=ps)
        swt = self.sb("d_swt", (128, 2, 2048), BF16, stack=ps)
        uswpt = self.sb("d_uswpt", (128, 2, 2048), BF16, stack=ps)
        dvt = self.sb("d_dvt", (128, 2, 2048), BF16, stack=ps)
        dgt = self.sb("d_dgt", (128, 2, 2048), BF16, stack=ps)
        pdh = self.psum("d_pdh", (128, 2, 2048), stack=ps)

        t0 = dict(T=cT.n, V=cV.n, dY=dY.n, dS=dS.n, dG=dG.n)
        RT_ld = []
        pt_ld = []
        s_ld = []

        @block.sync
        def _(y):
            for k in range(8):
                ins = y.dma_start(out=RT[:, k, :],
                                  in_=self.ap("R_tok")[:, k * 128:(k + 1) * 128],
                                  transpose=True)
                RT_ld.append(dY.inc(ins, 16))
            for i in range(2 * NT):
                h, t = divmod(i, NT)
                if i >= 2:
                    y.wait_ge(cV.sem, t0["V"] + i - 1)
                ins = y.dma_start(out=swt[:, i % 2, :],
                                  in_=self.tm("sw_tok")[:, t, h * 2048:(h + 1) * 2048])
                dY.inc(ins, 16)
                ins = y.dma_start(out=uswpt[:, i % 2, :],
                                  in_=self.tm("uswp_tok")[:, t, h * 2048:(h + 1) * 2048])
                s_ld.append(dY.inc(ins, 16))

        @block.scalar
        def _(a):
            for h in range(2):
                if h == 1:
                    a.wait_ge(cT.sem, t0["T"] + NT)
                for b_ in range(4):
                    ins = a.dma_start(
                        out=PTh[:, :, b_, :],
                        in_=self.ap("PT_work")[4 * h + b_].rearrange(
                            "(k p) n -> p k n", p=128))
                    dS.inc(ins, 16)
                pt_ld.append(dS.n)

        @block.tensor
        def _(t_):
            t_.wait_ge(dY.sem, RT_ld[-1])
            for h in range(2):
                t_.wait_ge(dS.sem, pt_ld[h])
                for t in range(NT):
                    i = h * NT + t
                    if i >= 2:
                        t_.wait_ge(cV.sem, t0["V"] + i - 1)
                    for k in range(8):
                        lhsT = RT[:, k, t * 128:(t + 1) * 128]
                        for b_ in range(4):
                            ins = t_.matmul(pdh[:, i % 2, b_ * 512:(b_ + 1) * 512],
                                            lhsT, PTh[:, k, b_, :],
                                            start=(k == 0), stop=(k == 7))
                    cT.inc(ins)

        @block.vector
        def _(v):
            for i in range(2 * NT):
                v.wait_ge(cT.sem, t0["T"] + i + 1)
                v.wait_ge(dY.sem, s_ld[i])
                if i >= 2:
                    v.wait_ge(dG.sem, t0["dG"] + 32 * (i - 1))
                v.tensor_tensor(dvt[:, i % 2, :], pdh[:, i % 2, :],
                                swt[:, i % 2, :], ALU.mult)
                v.tensor_tensor(dgt[:, i % 2, :], pdh[:, i % 2, :],
                                uswpt[:, i % 2, :], ALU.mult)
                cV.inc(v.drain())

        @block.gpsimd
        def _(g):
            for i in range(2 * NT):
                g.wait_ge(cV.sem, t0["V"] + i + 1)
                h, t = divmod(i, NT)
                ins = g.dma_start(out=self.tm("dV_tok")[:, t, h * 2048:(h + 1) * 2048],
                                  in_=dvt[:, i % 2, :])
                dG.inc(ins, 16)
                ins = g.dma_start(out=self.tm("dG_tok")[:, t, h * 2048:(h + 1) * 2048],
                                  in_=dgt[:, i % 2, :])
                dG.inc(ins, 16)
        self.expect(cT, t0["T"] + 2 * NT, "D T")

    def ttt_G1(self, block, ps, xname, rhsname, outname):
        dY, dS, dG = self.dY, self.dS, self.dG
        cT, cV = self.cT, self.cV

        rhs = self.sb("g_rhs", (128, 32, 1024), BF16, stack=ps)
        strip = self.sb("g_strip", (128, 2, 32, 128), BF16, stack=ps)
        ev = self.sb("g_ev", (128, 2, DIM), BF16, stack=ps)
        pg = self.psum("g_pg", (128, 2, DIM), stack=ps)

        t0 = dict(T=cT.n, V=cV.n, dY=dY.n, dS=dS.n, dG=dG.n)
        rhs_ld = []
        strip_ld = []

        @block.scalar
        def _(a):
            ins = a.dma_start(out=rhs[:],
                              in_=self.ap(rhsname).rearrange("(t p) f -> p t f", p=128))
            rhs_ld.append(dS.inc(ins, 16))

        @block.sync
        def _(y):
            xv = self.ap(xname).rearrange("(t p) f -> p t f", p=128)
            for m in range(32):
                if m >= 2:
                    y.wait_ge(cT.sem, t0["T"] + m - 1)
                ins = y.dma_start(out=strip[:, m % 2, :, :],
                                  in_=xv[:, :, m * 128:(m + 1) * 128])
                strip_ld.append(dY.inc(ins, 16))

        @block.tensor
        def _(t_):
            t_.wait_ge(dS.sem, rhs_ld[0])
            for m in range(32):
                t_.wait_ge(dY.sem, strip_ld[m])
                if m >= 2:
                    t_.wait_ge(cV.sem, t0["V"] + m - 1)
                for kt in range(32):
                    lhsT = strip[:, m % 2, kt, :]
                    t_.matmul(pg[:, m % 2, 0:512], lhsT, rhs[:, kt, 0:512],
                              start=(kt == 0), stop=(kt == 31))
                    ins = t_.matmul(pg[:, m % 2, 512:1024], lhsT, rhs[:, kt, 512:1024],
                                    start=(kt == 0), stop=(kt == 31))
                cT.inc(ins)

        @block.vector
        def _(v):
            for m in range(32):
                v.wait_ge(cT.sem, t0["T"] + m + 1)
                if m >= 2:
                    v.wait_ge(dG.sem, t0["dG"] + 16 * (m - 1))
                v.tensor_copy(ev[:, m % 2, :], pg[:, m % 2, :])
                cV.inc(v.drain())

        @block.gpsimd
        def _(g):
            for m in range(32):
                g.wait_ge(cV.sem, t0["V"] + m + 1)
                ins = g.dma_start(out=self.ap(outname)[m * 128:(m + 1) * 128, :],
                                  in_=ev[:, m % 2, :])
                dG.inc(ins, 16)
        self.expect(cT, t0["T"] + 32, "G1 T")

    def ttt_RS(self, block):
        cC, dG = self.cC, self.dG

        @block.gpsimd
        def _(g):
            g.wait_ge(dG.sem, dG.n)
            for nm in ("gP", "gV", "gG"):
                ins = g.collective_compute(
                    "ReduceScatter", ALU.add, replica_groups=RG,
                    ins=[self.ap(nm + "_part").opt()],
                    outs=[self.ap(nm + "_rs").opt()])
                cC.inc(ins)
            g.wait_ge(cC.sem, cC.n)

    def ttt_NS(self, block, ps, step):
        dY, dS, dG = self.dY, self.dS, self.dG
        cT, cV, cA, cG, cC = self.cT, self.cV, self.cA, self.cG, self.cC

        Xb = self.sb("ns_Xb", (128, 3, 8, 512), BF16, stack=ps)
        X = self.sb("ns_X", (128, 3, 8, 512), F32, stack=ps)
        XT = self.sb("ns_XT", (128, 4, 1024), BF16, stack=ps)
        Asb = self.sb("ns_Asb", (128, 8, 1024), BF16, stack=ps)
        Y32 = self.sb("ns_Y32", (128, 8, 512), F32, stack=ps)
        Yb = self.sb("ns_Yb", (128, 8, 512), BF16, stack=ps)
        evA = self.sb("ns_evA", (128, 2, 1024), BF16, stack=ps)
        mst = self.sb("ns_mst", (128, 8, 512), F32, stack=ps)
        sq_s = self.sb("ns_sq", (128, 4096), F32, stack=ps)
        sc = self.sb("ns_sc", (128, 8), F32, stack=ps)
        ones = self.sb("ns_ones", (128, 1), BF16, stack=ps)
        ones_row = self.sb("ns_onesr", (1, 128), BF16, stack=ps)
        scb = self.sb("ns_scb", (128, 4), BF16, stack=ps)
        inv = self.sb("ns_inv", (128, 4), F32, stack=ps)
        psA = self.psum("ns_psA", (128, 2, 1024), stack=ps)
        psY = self.psum("ns_psY", (128, 2, 512), stack=ps)
        psZ = self.psum("ns_psZ", (128, 2, 512), stack=ps)

        P_RS = ("gG_rs", "gV_rs", "gP_rs")
        P_M = ("G_m", "V_m", "PT_m")
        P_WK = ("G_work", "V_work", "PT_work")
        P_SH = ("gate_sh", "val_sh", "proj_sh")

        b0 = dict(T=cT.n, V=cV.n, A=cA.n, dY=dY.n, dS=dS.n, dG=dG.n, C=cC.n)

        xb_ld = [b0["dY"] + 128 * (pi + 1) for pi in range(3)]

        @block.sync
        def _(y):
            y.wait_ge(cC.sem, b0["C"])
            for pi in range(3):
                for j in range(8):
                    ins = y.dma_start(out=Xb[:, pi, j, :],
                                      in_=self.ap(P_RS[pi])[:, j * 128:(j + 1) * 128],
                                      transpose=True)
                    dY.inc(ins, 16)

        @block.scalar
        def _(a):
            for pi in range(3):
                a.wait_ge(dY.sem, xb_ld[pi])
                a.activation(out=sq_s[:], in_=Xb[:, pi].rearrange("p a b -> p (a b)"),
                             func=AF.Square, accum_out=sc[:, pi:pi + 1])
                cA.inc(a.drain())

        @block.vector
        def _(v):
            v.memset(ones[:], 1.0)
            v.memset(ones_row[:], 1.0)
            for pi in range(3):
                v.wait_ge(cA.sem, b0["A"] + pi + 1)
                v.tensor_copy(scb[:, pi:pi + 1], sc[:, pi:pi + 1])
            cV.inc(v.drain())

        @block.tensor
        def _(t_):
            t_.wait_ge(cV.sem, b0["V"] + 1)
            for pi in range(3):
                ins = t_.matmul(psZ[0:1, 0, pi:pi + 1], ones[:], scb[:, pi:pi + 1],
                                start=True, stop=True)
            cT.inc(ins)

        @block.vector
        def _(v):
            v.wait_ge(cT.sem, b0["T"] + 1)
            v.tensor_copy(sc[0:1, 0:3], psZ[0:1, 0, 0:3])
            cV.inc(v.drain())

        @block.gpsimd
        def _(g):
            g.wait_ge(cV.sem, b0["V"] + 2)
            ins = g.dma_start(out=self.ap("norm_part"), in_=sc[0:1, 0:3])
            dG.inc(ins, 16)
            g.wait_ge(dG.sem, dG.n)
            ins = g.collective_compute(
                "AllReduce", ALU.add, replica_groups=RG,
                ins=[self.ap("norm_part").opt()], outs=[self.ap("norm_out").opt()])
            cC.inc(ins)
            g.wait_ge(cC.sem, cC.n)
            ins = g.dma_start(out=sc[0:1, 4:7], in_=self.ap("norm_out"))
            dG.inc(ins, 16)
        nrm_ld = dG.n

        @block.scalar
        def _(a):
            a.wait_ge(dG.sem, nrm_ld)
            a.activation(out=sc[0:1, 0:3], in_=sc[0:1, 4:7], func=AF.Sqrt)
            cA.inc(a.drain())

        @block.vector
        def _(v):
            v.wait_ge(cA.sem, cA.n)
            v.tensor_scalar(sc[0:1, 0:3], sc[0:1, 0:3], 1e-7, None, ALU.add)
            v.drain()
            v.reciprocal(sc[0:1, 0:3], sc[0:1, 0:3])
            v.drain()
            v.tensor_copy(scb[0:1, 0:3], sc[0:1, 0:3])
            cV.inc(v.drain())

        @block.tensor
        def _(t_):
            t_.wait_ge(cV.sem, cV.n)
            ins = t_.matmul(psZ[:, 0, 0:3], ones_row[0:1, :], scb[0:1, 0:3],
                            start=True, stop=True)
            cT.inc(ins)

        @block.vector
        def _(v):
            v.wait_ge(cT.sem, cT.n)
            v.tensor_copy(inv[:, 0:3], psZ[:, 0, 0:3])
            v.drain()
            for pi in range(3):
                v.tensor_scalar(X[:, pi].rearrange("p a b -> p (a b)"),
                                Xb[:, pi].rearrange("p a b -> p (a b)"),
                                inv[:, pi:pi + 1], None, ALU.mult)
            cV.inc(v.drain())

        nb = dict(T=cT.n, V=cV.n, dY=dY.n, dG=dG.n, C=cC.n)
        CV_R, CT_R, DY_R, DG_R = 25, 24, 80, 144
        NR = 3 * NS_ITERS

        def vb(r):
            return nb["V"] + r * CV_R

        def tb(r):
            return nb["T"] + r * CT_R

        @block.vector
        def _(v):
            for r in range(NR):
                pi = r // NS_ITERS
                if r >= 1:
                    v.wait_ge(cT.sem, tb(r - 1) + 16)
                    v.wait_ge(dG.sem, nb["dG"] + (r - 1) * DG_R + 16)
                v.tensor_copy(Xb[:, pi].rearrange("p a b -> p (a b)"),
                              X[:, pi].rearrange("p a b -> p (a b)"))
                cV.inc(v.drain())
                for m in range(8):
                    v.wait_ge(cT.sem, tb(r) + m + 1)
                    v.tensor_copy(evA[:, m % 2, :], psA[:, m % 2, :])
                    cV.inc(v.drain())
                for m in range(8):
                    v.wait_ge(cT.sem, tb(r) + 8 + m + 1)
                    v.tensor_copy(Y32[:, m, :], psY[:, m % 2, :])
                    v.tensor_copy(Yb[:, m, :], psY[:, m % 2, :])
                    cV.inc(v.drain())
                for m in range(8):
                    v.wait_ge(cT.sem, tb(r) + 16 + m + 1)
                    v.tensor_scalar(X[:, pi, m, :], X[:, pi, m, :], NS_A, None, ALU.mult)
                    v.drain()
                    v.scalar_tensor_tensor(X[:, pi, m, :], Y32[:, m, :], NS_B,
                                           X[:, pi, m, :], ALU.mult, ALU.add)
                    v.drain()
                    v.scalar_tensor_tensor(X[:, pi, m, :], psZ[:, m % 2, :], NS_C,
                                           X[:, pi, m, :], ALU.mult, ALU.add)
                    cV.inc(v.drain())

        @block.gpsimd
        def _(g):
            for r in range(NR):
                pi = r // NS_ITERS
                g.wait_ge(cV.sem, vb(r) + 1)
                ins = g.dma_start(
                    out=self.ap(f"nsXsp_{pi}").rearrange("(k p) n -> p k n", p=128),
                    in_=Xb[:, pi].rearrange("p a b -> p a b"))
                dG.inc(ins, 16)
                for m in range(8):
                    g.wait_ge(cV.sem, vb(r) + 1 + m + 1)
                    ins = g.dma_start(
                        out=self.ap(f"nsAp_{pi}")[m * 128:(m + 1) * 128, :]
                        .rearrange("(o p) f -> p o f", p=128)[:, 0, :],
                        in_=evA[:, m % 2, :])
                    dG.inc(ins, 16)
                g.wait_ge(dG.sem, nb["dG"] + r * DG_R + DG_R)
                ins = g.collective_compute(
                    "AllReduce", ALU.add, replica_groups=RG,
                    ins=[self.ap(f"nsAp_{pi}").opt()],
                    outs=[self.ap(f"nsA_{pi}").opt()])
                cC.inc(ins)

        @block.sync
        def _(y):
            for r in range(NR):
                pi = r // NS_ITERS
                y.wait_ge(dG.sem, nb["dG"] + r * DG_R + 16)
                for c in range(4):
                    ins = y.dma_start(
                        out=XT[:, c, :],
                        in_=self.ap(f"nsXsp_{pi}")[:, c * 128:(c + 1) * 128],
                        transpose=True)
                    dY.inc(ins, 16)
                y.wait_ge(cC.sem, nb["C"] + r + 1)
                ins = y.dma_start(out=Asb[:],
                                  in_=self.ap(f"nsA_{pi}").rearrange(
                                      "(k p) n -> p k n", p=128))
                dY.inc(ins, 16)

        @block.tensor
        def _(t_):
            for r in range(NR):
                pi = r // NS_ITERS
                t_.wait_ge(dY.sem, nb["dY"] + r * DY_R + 64)
                for m in range(8):
                    if m >= 2:
                        t_.wait_ge(cV.sem, vb(r) + 1 + m - 1)
                    for kc in range(4):
                        lhsT = XT[:, kc, m * 128:(m + 1) * 128]
                        t_.matmul(psA[:, m % 2, 0:512], lhsT, XT[:, kc, 0:512],
                                  start=(kc == 0), stop=(kc == 3))
                        ins = t_.matmul(psA[:, m % 2, 512:1024], lhsT,
                                        XT[:, kc, 512:1024],
                                        start=(kc == 0), stop=(kc == 3))
                    cT.inc(ins)
                t_.wait_ge(dY.sem, nb["dY"] + r * DY_R + 80)
                for m in range(8):
                    if m >= 2:
                        t_.wait_ge(cV.sem, vb(r) + 9 + m - 1)
                    for k in range(8):
                        ins = t_.matmul(psY[:, m % 2, :],
                                        Asb[:, k, m * 128:(m + 1) * 128],
                                        Xb[:, pi, k, :], start=(k == 0), stop=(k == 7))
                    cT.inc(ins)
                t_.wait_ge(cV.sem, vb(r) + 17)
                for m in range(8):
                    if m >= 2:
                        t_.wait_ge(cV.sem, vb(r) + 17 + m - 1)
                    for k in range(8):
                        ins = t_.matmul(psZ[:, m % 2, :],
                                        Asb[:, k, m * 128:(m + 1) * 128],
                                        Yb[:, k, :], start=(k == 0), stop=(k == 7))
                    cT.inc(ins)

        ib = dict(T=cT.n, V=cV.n, dY=dY.n, dG=dG.n, C=cC.n)
        self.expect(cT, nb["T"] + NR * CT_R, "NS T")
        self.expect(cV, nb["V"] + NR * CV_R, "NS V")

        @block.sync
        def _(y):
            for pi in range(3):
                srcn = P_SH[pi] if step == 0 else P_M[pi]
                y.wait_ge(cV.sem, ib["V"] + pi)
                if pi >= 1:
                    y.wait_ge(dG.sem, ib["dG"] + 32 * (pi - 1) + 16)
                ins = y.dma_start(out=mst[:],
                                  in_=self.ap(srcn).rearrange("(k p) n -> p k n", p=128))
                dY.inc(ins, 16)

        @block.vector
        def _(v):
            for pi in range(3):
                v.wait_ge(dY.sem, ib["dY"] + 16 * (pi + 1))
                v.scalar_tensor_tensor(mst[:].rearrange("p a b -> p (a b)"),
                                       X[:, pi].rearrange("p a b -> p (a b)"), -LR,
                                       mst[:].rearrange("p a b -> p (a b)"),
                                       ALU.mult, ALU.add)
                v.drain()
                v.tensor_copy(Xb[:, pi].rearrange("p a b -> p (a b)"),
                              mst[:].rearrange("p a b -> p (a b)"))
                cV.inc(v.drain())

        @block.gpsimd
        def _(g):
            for pi in range(3):
                g.wait_ge(cV.sem, ib["V"] + pi + 1)
                ins = g.dma_start(out=self.ap(P_M[pi]).rearrange("(k p) n -> p k n", p=128),
                                  in_=mst[:])
                dG.inc(ins, 16)
                ins = g.dma_start(out=self.ap(f"nsXsp_{pi}").rearrange(
                    "(k p) n -> p k n", p=128), in_=Xb[:, pi].rearrange("p a b -> p a b"))
                dG.inc(ins, 16)
                g.wait_ge(dG.sem, dG.n)
                ins = g.collective_compute(
                    "AllGather", ALU.bypass, replica_groups=RG,
                    ins=[self.ap(f"nsXsp_{pi}").opt()],
                    outs=[self.ap(P_WK[pi]).rearrange("b k n -> (b k) n").opt()])
                cC.inc(ins)
            g.wait_ge(cC.sem, cC.n)

    def ttt_O(self, block, ps):
        dY, dS, dG = self.dY, self.dS, self.dG
        cT, cV, cA = self.cT, self.cV, self.cA

        pqT = self.sb("o_pqT", (128, 8, NTOK), BF16, stack=ps)
        pw = self.sb("o_pw", (128, 8, DIM), BF16, stack=ps)
        pjb = self.sb("o_pjb", (128, DIM), F32, stack=ps)
        xin = self.sb("o_x", (128, 2, DIM), F32, stack=ps)
        ob = self.sb("o_ob", (128, 2, DIM), F32, stack=ps)
        sq = self.sb("o_sq", (128, DIM), F32, stack=ps)
        stat = self.sb("o_stat", (128, 2, 16), F32, stack=ps)
        xco = self.sb("o_xco", (128, 2, DIM), F32, stack=ps)
        lno = self.sb("o_lno", (128, 2, DIM), F32, stack=ps)
        x1b = self.sb("o_x1", (128, 2, DIM), F32, stack=ps)
        xc1 = self.sb("o_xc1", (128, 2, DIM), F32, stack=ps)
        ln1t = self.sb("o_ln1", (128, 2, DIM), BF16, stack=ps)
        po = self.psum("o_po", (128, 2, DIM), stack=ps)

        t0 = dict(T=cT.n, V=cV.n, A=cA.n, dY=dY.n, dS=dS.n, dG=dG.n)
        pq_ld = []
        x_ld = []
        pw_ld = []

        @block.sync
        def _(y):
            for k in range(8):
                ins = y.dma_start(out=pqT[:, k, :],
                                  in_=self.ap("predq_tok")[:, k * 128:(k + 1) * 128],
                                  transpose=True)
                pq_ld.append(dY.inc(ins, 16))
            xv = self.tm("x")
            for t in range(NT):
                if t >= 2:
                    y.wait_ge(cV.sem, t0["V"] + 2 * (t - 1))
                ins = y.dma_start(out=xin[:, t % 2, :], in_=xv[:, t, :])
                x_ld.append(dY.inc(ins, 16))

        @block.gpsimd
        def _(g):
            ins = g.dma_start(out=pw[:],
                              in_=self.ap("proj_w").rearrange("(k p) n -> p k n", p=128))
            dG.inc(ins, 16)
            ins = g.dma_start(out=pjb[:], in_=self.ap("proj_b"))
            pw_ld.append(dG.inc(ins, 16))
            for t in range(NT):
                g.wait_ge(cV.sem, t0["V"] + 2 * t + 2)
                ins = g.dma_start(out=self.tm("x1_tok")[:, t, :], in_=x1b[:, t % 2, :])
                dG.inc(ins, 16)

        @block.tensor
        def _(t_):
            t_.wait_ge(dY.sem, pq_ld[-1])
            t_.wait_ge(dG.sem, pw_ld[0])
            for t in range(NT):
                if t >= 1:
                    t_.wait_ge(cV.sem, t0["V"] + 2 * t - 1)
                for k in range(8):
                    lhsT = pqT[:, k, t * 128:(t + 1) * 128]
                    t_.matmul(po[:, t % 2, 0:512], lhsT, pw[:, k, 0:512],
                              start=(k == 0), stop=(k == 7))
                    ins = t_.matmul(po[:, t % 2, 512:1024], lhsT, pw[:, k, 512:1024],
                                    start=(k == 0), stop=(k == 7))
                cT.inc(ins)

        @block.vector
        def _(v):
            for t in range(NT):
                v.wait_ge(cT.sem, t0["T"] + t + 1)
                if t >= 2:
                    v.wait_ge(cA.sem, t0["A"] + 2 * t - 3)
                st = stat[:, t % 2, :]
                v.tensor_tensor(ob[:, t % 2, :], po[:, t % 2, :], pjb[:], ALU.add)
                v.drain()
                v.tensor_tensor(sq[:], ob[:, t % 2, :], ob[:, t % 2, :], ALU.mult)
                v.reduce_sum(st[:, 0:1], ob[:, t % 2, :], axis=AX)
                v.drain()
                v.reduce_sum(st[:, 1:2], sq[:], axis=AX)
                v.tensor_scalar(st[:, 2:3], st[:, 0:1], 1.0 / DIM, None, ALU.mult)
                v.drain()
                v.tensor_scalar(st[:, 3:4], st[:, 1:2], 1.0 / DIM, None, ALU.mult)
                v.tensor_tensor(st[:, 4:5], st[:, 2:3], st[:, 2:3], ALU.mult)
                v.tensor_scalar(xco[:, t % 2, :], ob[:, t % 2, :], st[:, 2:3],
                                None, ALU.subtract)
                v.drain()
                v.tensor_tensor(st[:, 5:6], st[:, 3:4], st[:, 4:5], ALU.subtract)
                cV.inc(v.drain())
                v.wait_ge(cA.sem, t0["A"] + 2 * t + 1)
                v.wait_ge(dY.sem, x_ld[t])
                if t >= 2:
                    v.wait_ge(dG.sem, t0["dG"] + 32 + 16 * (t - 1))
                v.tensor_tensor(x1b[:, t % 2, :], xin[:, t % 2, :],
                                lno[:, t % 2, :], ALU.add)
                v.drain()
                v.tensor_tensor(sq[:], x1b[:, t % 2, :], x1b[:, t % 2, :], ALU.mult)
                v.reduce_sum(st[:, 8:9], x1b[:, t % 2, :], axis=AX)
                v.drain()
                v.reduce_sum(st[:, 9:10], sq[:], axis=AX)
                v.tensor_scalar(st[:, 10:11], st[:, 8:9], 1.0 / DIM, None, ALU.mult)
                v.drain()
                v.tensor_scalar(st[:, 11:12], st[:, 9:10], 1.0 / DIM, None, ALU.mult)
                v.tensor_tensor(st[:, 12:13], st[:, 10:11], st[:, 10:11], ALU.mult)
                v.tensor_scalar(xc1[:, t % 2, :], x1b[:, t % 2, :], st[:, 10:11],
                                None, ALU.subtract)
                v.drain()
                v.tensor_tensor(st[:, 13:14], st[:, 11:12], st[:, 12:13], ALU.subtract)
                cV.inc(v.drain())

        @block.scalar
        def _(a):
            for t in range(NT):
                a.wait_ge(cV.sem, t0["V"] + 2 * t + 1)
                if t >= 2:
                    a.wait_ge(dS.sem, t0["dS"] + 16 * (t - 1))
                st = stat[:, t % 2, :]
                a.activation(out=st[:, 6:7], in_=st[:, 5:6], func=AF.Ln,
                             bias=self.konst[:, 0:1])
                a.drain()
                a.activation(out=st[:, 7:8], in_=st[:, 6:7], func=AF.Exp, scale=-0.5)
                a.drain()
                a.activation(out=lno[:, t % 2, :], in_=xco[:, t % 2, :],
                             func=AF.Copy, scale=st[:, 7:8])
                cA.inc(a.drain())
                a.wait_ge(cV.sem, t0["V"] + 2 * t + 2)
                a.activation(out=st[:, 14:15], in_=st[:, 13:14], func=AF.Ln,
                             bias=self.konst[:, 0:1])
                a.drain()
                a.activation(out=st[:, 15:16], in_=st[:, 14:15], func=AF.Exp, scale=-0.5)
                a.drain()
                a.activation(out=ln1t[:, t % 2, :], in_=xc1[:, t % 2, :],
                             func=AF.Copy, scale=st[:, 15:16])
                cA.inc(a.drain())
                ins = a.dma_start(out=self.tm("ln1_tok")[:, t, :], in_=ln1t[:, t % 2, :])
                dS.inc(ins, 16)
        self.expect(cT, t0["T"] + NT, "O T")
        self.expect(cV, t0["V"] + 2 * NT, "O V")
        self.expect(cA, t0["A"] + 2 * NT, "O A")

    def ttt_L(self, block, ps):
        dY, dS, dG = self.dY, self.dS, self.dG
        cT, cV, cA, cC = self.cT, self.cV, self.cA, self.cC

        ln1T = self.sb("l_ln1T", (128, 8, NTOK), BF16, stack=ps)
        gq = self.sb("l_gq", (128, 8, 16), BF16, stack=ps)
        lt = self.sb("l_lt", (1, 1), F32, stack=ps)
        ltb = self.sb("l_ltb", (16, 1), F32, stack=ps)
        Lsb = self.sb("l_L", (16, NTOK), F32, stack=ps)
        pL = self.psum("l_pL", (16, 2, 512), stack=ps)
        pB = self.psum("l_pB", (16, 1), stack=ps)
        onesr = self.sb("l_onesr", (1, 16), BF16, stack=ps)
        ltb_b = self.sb("l_ltbb", (1, 1), BF16, stack=ps)

        t0 = dict(T=cT.n, V=cV.n, A=cA.n, dY=dY.n, dS=dS.n, dG=dG.n)
        ld = []

        @block.sync
        def _(y):
            for k in range(8):
                ins = y.dma_start(out=ln1T[:, k, :],
                                  in_=self.ap("ln1_tok")[:, k * 128:(k + 1) * 128],
                                  transpose=True)
                ld.append(dY.inc(ins, 16))

        @block.gpsimd
        def _(g):
            ins = g.dma_start(out=gq[:],
                              in_=self.ap("gate_queryT").rearrange("(k p) n -> p k n", p=128))
            dG.inc(ins, 16)
            ins = g.dma_start(out=lt[:], in_=self.ap("log_temp"))
            ld.append(dG.inc(ins, 16))

        @block.scalar
        def _(a):
            a.wait_ge(dG.sem, ld[-1])
            a.activation(out=lt[:], in_=lt[:], func=AF.Exp, scale=-1.0)
            cA.inc(a.drain())

        @block.vector
        def _(v):
            v.memset(onesr[:], 1.0)
            v.wait_ge(cA.sem, t0["A"] + 1)
            v.tensor_copy(ltb_b[:], lt[:])
            cV.inc(v.drain())

        @block.tensor
        def _(t_):
            v0 = t0["V"] + 1
            t_.wait_ge(cV.sem, v0)
            ins = t_.matmul(pB[:], onesr[0:1, :], ltb_b[0:1, :], start=True, stop=True)
            cT.inc(ins)
            t_.wait_ge(dY.sem, t0["dY"] + 128)
            t_.wait_ge(dG.sem, t0["dG"] + 16)
            for n in range(8):
                if n >= 2:
                    t_.wait_ge(cV.sem, v0 + n - 1)
                for k in range(8):
                    ins = t_.matmul(pL[:, n % 2, :], gq[:, k, :],
                                    ln1T[:, k, n * 512:(n + 1) * 512],
                                    start=(k == 0), stop=(k == 7))
                cT.inc(ins)

        @block.vector
        def _(v):
            v.wait_ge(cT.sem, t0["T"] + 1)
            v.tensor_copy(ltb[:], pB[:])
            cV.inc(v.drain())
            for n in range(8):
                v.wait_ge(cT.sem, t0["T"] + n + 2)
                v.tensor_scalar(Lsb[:, n * 512:(n + 1) * 512], pL[:, n % 2, :],
                                ltb[:, 0:1], None, ALU.mult)
                cV.inc(v.drain())

        @block.gpsimd
        def _(g):
            g.wait_ge(cV.sem, cV.n)
            ins = g.dma_start(out=self.ap("logitsT_own"), in_=Lsb[:])
            dG.inc(ins, 16)
            g.wait_ge(dG.sem, dG.n)
            ins = g.collective_compute(
                "AllGather", ALU.bypass, replica_groups=RG,
                ins=[self.ap("ln1_tok").opt()], outs=[self.ap("ln1_all").opt()])
            cC.inc(ins)
            ins = g.collective_compute(
                "AllGather", ALU.bypass, replica_groups=RG,
                ins=[self.ap("logitsT_own").opt()], outs=[self.ap("logitsT_all").opt()])
            cC.inc(ins)
            g.wait_ge(cC.sem, cC.n)

    def seq(self, block, engname, ch, fn):
        chans = [self.cT, self.cV, self.cA, self.cG, self.dY, self.dS, self.dG, self.cC]
        vals = [c.n for c in chans]

        def body(eng):
            for c, v in zip(chans, vals):
                if v > 0:
                    eng.wait_ge(c.sem, v)
            fn(eng)
            if ch is not None:
                ch.inc(eng.drain())
        getattr(block, engname)(body)

    def ffn_R(self, block, ps):
        dY, dS, dG = self.dY, self.dS, self.dG
        cT, cV, cA, cG, cC = self.cT, self.cV, self.cA, self.cG, self.cC
        N2 = 8192

        sink = self.sb("r_sink", (64, N2), F32, stack=ps)
        rank = self.sb("r_rank", (64, N2), F32, stack=ps)
        sm = self.sb("r_sm", (64, 24), F32, stack=ps)
        bmap = self.sb("r_bmap", (64, 4), F32, stack=ps)
        bmapT = self.sb("r_bmapT", (4, 64), F32, stack=ps)
        onesel = self.sb("r_onesel", (64, 8), F32, stack=ps)
        i8t = self.sb("r_i8t", (64, 8), F32, stack=ps)
        tstg = self.sb("r_tstg", (128, 64, 16), F32, stack=ps)
        tbhi = self.sb("r_tbhi", (128, 64, 8), BF16, stack=ps)
        tblo = self.sb("r_tblo", (128, 64, 8), BF16, stack=ps)
        tres = self.sb("r_tres", (128, 64, 8), F32, stack=ps)
        iota_blk = self.sb("r_iblk", (128, 64), BF16, stack=ps)
        iota_pc = self.sb("r_ipc", (128, 64), BF16, stack=ps)
        excol = self.sb("r_excol", (128, 16), F32, stack=ps)
        gcol = self.sb("r_gcol", (128, 16), F32, stack=ps)
        excb = self.sb("r_excb", (128, 16), U32, stack=ps)
        cstt = self.sb("r_cst", (1, 8), F32, stack=ps)
        ccb = self.sb("r_ccb", (1, 1), BF16, stack=ps)
        smu = self.sb("r_smu", (64, 2), U32, stack=ps)
        xev = self.sb("r_xev", (128, 4), F32, stack=ps)
        onesr64 = self.sb("r_o64", (1, 64), BF16, stack=ps)
        ps8 = self.psum("r_ps8", (64, 4, 512), stack=ps)
        psT = self.psum("r_psT", (128, 8, 16), stack=ps)
        psX = self.psum("r_psX", (128, 2, 4), stack=ps)
        scope1 = ExitStack()
        exps = self.sb("r_exps", (64, N2), F32, stack=scope1)
        lse4 = self.sb("r_lse4", (4, N2), F32, stack=scope1)

        S = lambda e, c, f: self.seq(block, e, c, f)

        def st_(y):
            ag3 = self.ap("logitsT_all").rearrange("(c e) t -> c e t", e=16)
            for e in range(16):
                for half in range(2):
                    dY.inc(y.dma_start(
                        out=sink[e * 4:(e + 1) * 4, half * 4096:(half + 1) * 4096],
                        in_=ag3[half::2, e]), 16)
            dY.inc(y.dma_start(out=cstt[:], in_=self.ap("cconst")), 16)
        S("sync", None, st_)

        def g_iota(g):
            g.iota(sm[:, 0:1], pattern=[[0, 1]], base=0, channel_multiplier=1,
                   allow_small_or_imprecise_dtypes=True)
            g.iota(i8t[:], pattern=[[1, 8]], base=0, channel_multiplier=0,
                   allow_small_or_imprecise_dtypes=True)
            g.iota(iota_blk[:], pattern=[[1, 64]], base=0, channel_multiplier=0,
                   allow_small_or_imprecise_dtypes=True)
            g.iota(iota_pc[:], pattern=[[0, 64]], base=0, channel_multiplier=1,
                   allow_small_or_imprecise_dtypes=True)
        S("gpsimd", cG, g_iota)

        def g_bT(g):
            g.iota(lse4[0:4, 0:64], pattern=[[0, 16], [1, 4]], base=0,
                   channel_multiplier=0, allow_small_or_imprecise_dtypes=True)
            g.iota(lse4[0:4, 64:65], pattern=[[0, 1]], base=0,
                   channel_multiplier=1, allow_small_or_imprecise_dtypes=True)
        S("gpsimd", cG, g_bT)

        def v_bT(v):
            v.memset(onesr64[:], 1.0)
            v.tensor_scalar(bmapT[:], lse4[0:4, 0:64], lse4[0:4, 64:65], None,
                            ALU.is_equal)
        S("vector", cV, v_bT)
        S("tensor", cT, lambda t_: t_.transpose(
            ps8[0:64, 0, 0:4], bmapT[:], self.identf[0:4, 0:4]))
        S("vector", cV, lambda v: v.tensor_copy(bmap[:], ps8[0:64, 0, 0:4]))

        for it in range(SINK_IT):
            S("scalar", cA, lambda a: a.activation(out=exps[:], in_=sink[:], func=AF.Exp))
            for qq in range(4):
                def t_es(t_, qq=qq):
                    for c in range(4):
                        t_.matmul(ps8[0:4, c, :], bmap[:],
                                  exps[:, (qq * 4 + c) * 512:(qq * 4 + c + 1) * 512],
                                  start=True, stop=True)
                S("tensor", cT, t_es)

                def v_es(v, qq=qq):
                    v.tensor_copy(lse4[:, qq * 2048:(qq + 1) * 2048],
                                  ps8[0:4, :, :].rearrange("p a b -> p (a b)"))
                S("vector", cV, v_es)
            S("scalar", cA, lambda a: a.activation(out=lse4[:], in_=lse4[:], func=AF.Ln))
            for qq in range(4):
                def t_bc(t_, qq=qq):
                    for c in range(4):
                        t_.matmul(ps8[:, c, :], bmapT[:],
                                  lse4[:, (qq * 4 + c) * 512:(qq * 4 + c + 1) * 512],
                                  start=True, stop=True)
                S("tensor", cT, t_bc)

                def v_bc(v, qq=qq):
                    v.tensor_tensor(sink[:, qq * 2048:(qq + 1) * 2048],
                                    sink[:, qq * 2048:(qq + 1) * 2048],
                                    ps8[:, :, :].rearrange("p a b -> p (a b)"),
                                    ALU.subtract)
                S("vector", cV, v_bc)

            def v_mx(v):
                v.reduce_max(sm[:, 2:3], sink[:], axis=AX)
                v.drain()
                v.tensor_scalar(sm[:, 3:4], sm[:, 2:3], -1.0, None, ALU.mult)
            S("vector", cV, v_mx)

            def a_nl(a):
                a.activation(out=exps[:], in_=sink[:], func=AF.Exp,
                             bias=sm[:, 3:4], accum_out=sm[:, 4:5])
                a.drain()
                a.activation(out=sm[:, 5:6], in_=sm[:, 4:5], func=AF.Ln)
            S("scalar", cA, a_nl)

            def v_sub(v):
                v.tensor_tensor(sm[:, 6:7], sm[:, 5:6], sm[:, 2:3], ALU.add)
                v.drain()
                v.tensor_scalar(sink[:], sink[:], sm[:, 6:7], None, ALU.subtract)
            S("vector", cV, v_sub)

        def v_bis(v):
            v.memset(sm[:, 8:9], -40.0)
            v.memset(sm[:, 9:10], 1e-3)
            v.drain()
            for _ in range(BIS_IT):
                v.tensor_tensor(sm[:, 10:11], sm[:, 8:9], sm[:, 9:10], ALU.add)
                v.drain()
                v.tensor_scalar(sm[:, 11:12], sm[:, 10:11], 0.5, None, ALU.mult)
                v.drain()
                v.tensor_scalar(exps[:], sink[:], sm[:, 11:12], 0.0,
                                ALU.is_gt, ALU.add, accum_out=sm[:, 12:13])
                v.drain()
                v.tensor_scalar(smu[:, 0:1], sm[:, 12:13], float(CAP), None, ALU.is_ge)
                v.tensor_scalar(smu[:, 1:2], sm[:, 12:13], float(CAP), None, ALU.is_lt)
                v.drain()
                v.select(sm[:, 8:9], smu[:, 0:1], sm[:, 11:12], sm[:, 8:9])
                v.select(sm[:, 9:10], smu[:, 1:2], sm[:, 11:12], sm[:, 9:10])
                v.drain()
        S("vector", cV, v_bis)

        def v_rank(v):
            v.tensor_scalar(exps[:], sink[:], sm[:, 8:9], None, ALU.is_gt)
            v.drain()
            v.tensor_tensor_scan(rank[:], exps[:], exps[:], 0.0, ALU.add, ALU.bypass)
            v.drain()
            v.tensor_tensor(rank[:], rank[:], exps[:], ALU.mult)
        S("vector", cV, v_rank)

        scope1.close()
        scope2 = ExitStack()
        town = self.sb("r_town", (8, 2 * N2), F32, stack=scope2)
        def v_csel(v):
            v.tensor_copy(ccb[:], cstt[0:1, 2:3])
        S("vector", cV, v_csel)
        S("tensor", cT, lambda t_: t_.matmul(ps8[:, 0, 0:1], onesr64[0:1, 0:64],
                                             ccb[0:1, :], start=True, stop=True))

        def v_oh(v):
            v.tensor_copy(sm[:, 14:15], ps8[0:64, 0, 0:1])
            v.drain()
            v.scalar_tensor_tensor(sm[:, 15:16], sm[:, 14:15], -8.0, sm[:, 0:1],
                                   ALU.mult, ALU.add)
            v.drain()
            v.tensor_scalar(onesel[:], i8t[:], sm[:, 15:16], None, ALU.is_equal)
        S("vector", cV, v_oh)

        for src_t, off in ((sink, 0), (rank, N2)):
            for qq in range(4):
                def t_ex(t_, src_t=src_t, qq=qq):
                    for c in range(4):
                        t_.matmul(ps8[0:8, c, :], onesel[:],
                                  src_t[:, (qq * 4 + c) * 512:(qq * 4 + c + 1) * 512],
                                  start=True, stop=True)
                S("tensor", cT, t_ex)

                def v_ex(v, off=off, qq=qq):
                    v.tensor_copy(town[0:8, off + qq * 2048:off + (qq + 1) * 2048],
                                  ps8[0:8, :, :].rearrange("p a b -> p (a b)"))
                S("vector", cV, v_ex)

        for grp in range(8):
            def t_tr(t_, grp=grp):
                for j in range(8):
                    blk = grp * 8 + j
                    t_.transpose(psT[:, j, 0:8], town[0:8, blk * 128:(blk + 1) * 128],
                                 self.identf[0:8, 0:8])
                    t_.transpose(psT[:, j, 8:16],
                                 town[0:8, N2 + blk * 128:N2 + (blk + 1) * 128],
                                 self.identf[0:8, 0:8])
            S("tensor", cT, t_tr)

            def v_tr(v, grp=grp):
                v.tensor_copy(tstg[:, grp * 8:(grp + 1) * 8, :], psT[:])
            S("vector", cV, v_tr)

        scope2.close()
        selT = self.sb("r_selT", (128, 64, 256), BF16, stack=ps)
        rhs2 = self.sb("r_rhs2", (128, 64, 4), BF16, stack=ps)

        def v_split(v):
            t8 = tstg[:, :, 0:8]
            v.tensor_copy(tbhi[:], t8)
            v.drain()
            v.tensor_tensor(tres[:], t8, tbhi[:], ALU.subtract)
            v.drain()
            v.tensor_copy(tblo[:], tres[:])
            v.drain()
            v.tensor_copy(rhs2[:, :, 0:1], iota_blk[:].rearrange("p (a o) -> p a o", o=1))
            v.tensor_copy(rhs2[:, :, 1:2], iota_pc[:].rearrange("p (a o) -> p a o", o=1))
        S("vector", cV, v_split)

        for r in range(8):
            def v_sel(v, r=r):
                rk = tstg[:, :, 8 + r:9 + r]
                for blk in range(64):
                    v.tensor_scalar(selT[:, blk, :], self.iota_cap[:],
                                    rk[:, blk, :], None, ALU.is_equal)
                v.drain()
                v.tensor_copy(rhs2[:, :, 2:3], tbhi[:, :, r:r + 1])
                v.tensor_copy(rhs2[:, :, 3:4], tblo[:, :, r:r + 1])
            S("vector", cV, v_sel)

            def t_idx(t_, r=r):
                for hh in range(2):
                    for blk in range(64):
                        t_.matmul(psX[:, hh, :], selT[:, blk, hh * 128:(hh + 1) * 128],
                                  rhs2[:, blk, :], start=(blk == 0), stop=(blk == 63))
            S("tensor", cT, t_idx)

            def v_fin(v, r=r):
                for hh in range(2):
                    c_ = r * 2 + hh
                    v.tensor_copy(xev[:], psX[:, hh, 0:4])
                    v.drain()
                    q_ = xev[:]
                    v.scalar_tensor_tensor(excol[:, c_:c_ + 1], q_[:, 0:1], 128.0,
                                           q_[:, 1:2], ALU.mult, ALU.add)
                    v.tensor_tensor(gcol[:, c_:c_ + 1], q_[:, 2:3],
                                    q_[:, 3:4], ALU.add)
                    v.drain()
                v.tensor_copy(excb[:, r * 2:r * 2 + 2], excol[:, r * 2:r * 2 + 2])
            S("vector", cV, v_fin)

        S("scalar", cA, lambda a: a.activation(out=gcol[:], in_=gcol[:], func=AF.Exp))

        def g_st(g):
            for r in range(8):
                for hh in range(2):
                    c_ = r * 2 + hh
                    dG.inc(g.dma_start(out=self.ap("idx_own")[r, hh * 128:(hh + 1) * 128],
                                       in_=excb[:, c_:c_ + 1]), 16)
                    dG.inc(g.dma_start(out=self.ap("gates_own")[r, hh * 128:(hh + 1) * 128],
                                       in_=gcol[:, c_:c_ + 1]), 16)
        S("gpsimd", None, g_st)

        if "sink_dump" in self.dbg:
            def g_dmp(g):
                dG.inc(g.dma_start(out=self.ap("sink_dump"), in_=sink[:]), 16)
                dG.inc(g.dma_start(out=self.ap("rank_dump"), in_=rank[:]), 16)
            S("gpsimd", None, g_dmp)

        def g_ag(g):
            cC.inc(g.collective_compute(
                "AllGather", ALU.bypass, replica_groups=RG,
                ins=[self.ap("idx_own").opt()],
                outs=[self.ap("idx_all")[0:64, :].opt()]))
        S("gpsimd", None, g_ag)
        S("gpsimd", None, lambda g: g.wait_ge(cC.sem, cC.n))

    def ffn_E(self, block, ps):
        dY, dS, dG = self.dY, self.dS, self.dG
        cT, cV, cA, cG, cC = self.cT, self.cV, self.cA, self.cG, self.cC

        w1b = self.sb("e_w1", (128, 8, HID), BF16, stack=ps)
        w2b = self.sb("e_w2", (128, 32, DIM), BF16, stack=ps)
        b1s = self.sb("e_b1", (128, HID // 128), F32, stack=ps)
        b2s = self.sb("e_b2", (128, DIM), F32, stack=ps)
        idxs = self.sb("e_idxs", (128, 2), U32, stack=ps)
        idxf = self.sb("e_idxf", (128, 2), F32, stack=ps)
        gts = self.sb("e_gts", (128, 2), F32, stack=ps)
        ei = self.sb("e_ei", (128, 2, DIM), BF16, stack=ps)
        eiT = self.sb("e_eiT", (128, 8, 256), BF16, stack=ps)
        h1T = self.sb("e_h1T", (128, 32, 256), BF16, stack=ps)
        eos = self.sb("e_eos", (128, DIM), F32, stack=ps)
        eob = self.sb("e_eob", (128, 2, DIM), BF16, stack=ps)
        psE1 = self.psum("e_ps1", (128, 4, 256), stack=ps)
        psE2 = self.psum("e_ps2", (128, 2, DIM), stack=ps)
        psTr = self.psum("e_psT", (128, 8, 128), BF16, stack=ps)

        S = lambda e, c, f: self.seq(block, e, c, f)

        for e_l in range(E_OWN):
            def g_w(g, e_l=e_l):
                dG.inc(g.dma_start(out=w1b[:], in_=self.ap("w1")[e_l].rearrange(
                    "(k p) n -> p k n", p=128)), 16)
                dG.inc(g.dma_start(out=w2b[:], in_=self.ap("w2")[e_l].rearrange(
                    "(k p) n -> p k n", p=128)), 16)
                dG.inc(g.dma_start(out=b1s[:], in_=self.ap("b1c")[e_l]), 16)
                dG.inc(g.dma_start(out=b2s[:], in_=self.ap("b2t")[e_l]), 16)
            S("gpsimd", None, g_w)
            for b_ in range(4):
                r = e_l * 4 + b_

                def g_ix(g, r=r):
                    for hh in range(2):
                        dG.inc(g.dma_start(out=idxs[:, hh:hh + 1],
                                           in_=self.ap("idx_own")[r, hh * 128:(hh + 1) * 128]), 16)
                        dG.inc(g.dma_start(out=gts[:, hh:hh + 1],
                                           in_=self.ap("gates_own")[r, hh * 128:(hh + 1) * 128]), 16)
                S("gpsimd", None, g_ix)

                def v_ix(v, b_=b_):
                    v.tensor_copy(idxf[:], idxs[:])
                    v.drain()
                    v.tensor_scalar(idxf[:], idxf[:], float(8192 * b_), None, ALU.add)
                    v.drain()
                    v.tensor_copy(idxs[:], idxf[:])
                S("vector", cV, v_ix)

                def g_ga(g):
                    for hh in range(2):
                        dG.inc(g.indirect_dma_start(
                            out=ei[:, hh, :], out_offset=None,
                            in_=self.ap("ln1_all")[0:128, :],
                            in_offset=IndirectOffsetOnAxis(idxs[:, hh:hh + 1], 0)), 16)
                S("gpsimd", None, g_ga)

                for hh in range(2):
                    def t_tr(t_, hh=hh):
                        for k in range(8):
                            t_.transpose(psTr[:, k, :], ei[:, hh, k * 128:(k + 1) * 128],
                                         self.ident[:])
                    S("tensor", cT, t_tr)

                    def v_tr(v, hh=hh):
                        v.tensor_copy(eiT[:, :, hh * 128:(hh + 1) * 128], psTr[:])
                    S("vector", cV, v_tr)

                for mg in range(8):
                    def t_h1(t_, mg=mg):
                        for mm_ in range(4):
                            m = mg * 4 + mm_
                            for k in range(8):
                                t_.matmul(psE1[:, mm_, :],
                                          w1b[:, k, m * 128:(m + 1) * 128],
                                          eiT[:, k, :], start=(k == 0), stop=(k == 7))
                    S("tensor", cT, t_h1)

                    def a_h1(a, mg=mg):
                        for mm_ in range(4):
                            m = mg * 4 + mm_
                            a.activation(out=h1T[:, m, :], in_=psE1[:, mm_, :],
                                         func=AF.Gelu_apprx_tanh, bias=b1s[:, m:m + 1])
                    S("scalar", cA, a_h1)

                def t_eo(t_):
                    for sh in range(2):
                        for k2 in range(32):
                            t_.matmul(psE2[:, sh, 0:512],
                                      h1T[:, k2, sh * 128:(sh + 1) * 128],
                                      w2b[:, k2, 0:512], start=(k2 == 0), stop=(k2 == 31))
                            t_.matmul(psE2[:, sh, 512:1024],
                                      h1T[:, k2, sh * 128:(sh + 1) * 128],
                                      w2b[:, k2, 512:1024], start=(k2 == 0), stop=(k2 == 31))
                S("tensor", cT, t_eo)

                def v_eo(v):
                    for sh in range(2):
                        v.tensor_tensor(eos[:], psE2[:, sh, :], b2s[:], ALU.add)
                        v.drain()
                        v.tensor_scalar(eob[:, sh, :], eos[:], gts[:, sh:sh + 1],
                                        None, ALU.mult)
                        v.drain()
                S("vector", cV, v_eo)

                def g_eo(g, r=r):
                    for sh in range(2):
                        dG.inc(g.dma_start(
                            out=self.ap("eo_own")[r, sh * 128:(sh + 1) * 128, :],
                            in_=eob[:, sh, :]), 16)
                S("gpsimd", None, g_eo)

        def g_ag(g):
            cC.inc(g.collective_compute(
                "AllGather", ALU.bypass, replica_groups=RG,
                ins=[self.ap("eo_own").rearrange("a b c -> (a b) c").opt()],
                outs=[self.ap("eo_all").rearrange("a b c -> (a b) c").opt()]))
        S("gpsimd", None, g_ag)
        S("gpsimd", None, lambda g: g.wait_ge(cC.sem, cC.n))

    def ffn_C(self, block, ps):
        dY, dS, dG = self.dY, self.dS, self.dG
        cT, cV, cA, cG, cC = self.cT, self.cV, self.cA, self.cG, self.cC

        zt = self.sb("c_zt", (128, DIM), F32, stack=ps)
        cstt = self.sb("c_cst", (1, 8), F32, stack=ps)
        ccb2 = self.sb("c_ccb", (1, 2), BF16, stack=ps)
        onesr = self.sb("c_onesr", (1, 128), BF16, stack=ps)
        smc = self.sb("c_smc", (128, 8), F32, stack=ps)
        ipcol = self.sb("c_ip", (128, 1), F32, stack=ps)
        rowu = self.sb("c_rowu", (128, 1), U32, stack=ps)
        idg = self.sb("c_idg", (128, 256), F32, stack=ps)
        idc = self.sb("c_idc", (128, 2, 16), F32, stack=ps)
        eidx = self.sb("c_eidx", (128, 2), U32, stack=ps)
        eidf = self.sb("c_eidf", (128, 2), F32, stack=ps)
        eosb = self.sb("c_eosb", (128, 2, DIM), BF16, stack=ps)
        scol = self.sb("c_scol", (128, 2), U32, stack=ps)
        scu = self.sb("c_scu", (128, 1), U32, stack=ps)
        scf = self.sb("c_scf", (128, 8), F32, stack=ps)
        psC = self.psum("c_psC", (128, 2, 16), stack=ps)

        S = lambda e, c, f: self.seq(block, e, c, f)

        S("vector", cV, lambda v: (v.memset(zt[:], 0.0), v.memset(onesr[:], 1.0)))

        def g_zero(g):
            accv = self.ap("acc")
            for r0 in range(0, NTOK + 256, 128):
                dG.inc(g.dma_start(out=accv[r0:r0 + 128, :], in_=zt[:]), 16)
            dG.inc(g.dma_start(out=cstt[:], in_=self.ap("cconst")), 16)
        S("gpsimd", None, g_zero)

        def g_iotap(g):
            g.iota(ipcol[:], pattern=[[0, 1]], base=0, channel_multiplier=1,
                   allow_small_or_imprecise_dtypes=True)
        S("gpsimd", cG, g_iotap)

        S("vector", cV, lambda v: v.tensor_copy(ccb2[:], cstt[0:1, 0:2]))
        S("tensor", cT, lambda t_: t_.matmul(psC[:, 0, 0:2], onesr[0:1, :],
                                             ccb2[0:1, :], start=True, stop=True))

        def v_bc(v):
            v.tensor_copy(smc[:, 0:2], psC[:, 0, 0:2])
            v.drain()
            v.scalar_tensor_tensor(smc[:, 2:3], ipcol[:], 4.0, smc[:, 1:2],
                                   ALU.mult, ALU.add)
            v.drain()
            v.tensor_copy(rowu[:], smc[:, 2:3])
        S("vector", cV, v_bc)

        def g_gidx(g):
            dG.inc(g.indirect_dma_start(
                out=idg[:], out_offset=None,
                in_=self.ap("idx_all")[0:128, :],
                in_offset=IndirectOffsetOnAxis(rowu[:], 0),
                bounds_check=63, oob_is_err=False), 16)
        S("gpsimd", None, g_gidx)

        for hh in range(2):
            def t_tri(t_, hh=hh):
                t_.transpose(psC[:, hh, :], idg[0:16, hh * 128:(hh + 1) * 128],
                             self.identf[0:16, 0:16])
            S("tensor", cT, t_tri)
        S("vector", cV, lambda v: v.tensor_copy(idc[:], psC[:]))

        for e in range(16):
            def v_ec(v, e=e):
                for hh in range(2):
                    v.tensor_scalar(scf[:, hh:hh + 1], ipcol[:],
                                    float(128 * hh + 1024 * e), None, ALU.add)
                v.drain()
                for hh in range(2):
                    v.scalar_tensor_tensor(eidf[:, hh:hh + 1], smc[:, 1:2], 256.0,
                                           scf[:, hh:hh + 1], ALU.mult, ALU.add)
                v.drain()
                v.tensor_copy(eidx[:], eidf[:])
            S("vector", cV, v_ec)

            def g_geo(g, e=e):
                eof = self.ap("eo_all").rearrange("a b c -> (a b) c")
                for hh in range(2):
                    dG.inc(g.indirect_dma_start(
                        out=eosb[:, hh, :], out_offset=None,
                        in_=eof[0:128, :],
                        in_offset=IndirectOffsetOnAxis(eidx[:, hh:hh + 1], 0)), 16)
            S("gpsimd", None, g_geo)

            def v_sc(v, e=e):
                for hh in range(2):
                    d_ = scf[:, 2 + hh:3 + hh]
                    v.tensor_tensor(d_, idc[:, hh, e:e + 1], smc[:, 0:1], ALU.subtract)
                    v.drain()
                    v.tensor_scalar(scf[:, 4:5], d_, 0.0, None, ALU.is_ge)
                    v.tensor_scalar(scf[:, 5:6], d_, float(NTOK), None, ALU.is_lt)
                    v.drain()
                    v.tensor_tensor(scf[:, 6:7], scf[:, 4:5], scf[:, 5:6], ALU.mult)
                    v.tensor_scalar(scf[:, 7:8], ipcol[:], float(NTOK + 128 * hh),
                                    None, ALU.add)
                    v.drain()
                    v.tensor_scalar(scu[:, 0:1], scf[:, 6:7], 0.0, None, ALU.is_equal)
                    v.drain()
                    v.select(scf[:, 2 + hh:3 + hh], scu[:, 0:1], scf[:, 7:8], d_)
                    v.drain()
                v.tensor_copy(scol[:], scf[:, 2:4])
            S("vector", cV, v_sc)

            for hh in range(2):
                def g_sc(g, hh=hh):
                    dG.inc(g.indirect_dma_start(
                        out=self.ap("acc"),
                        out_offset=IndirectOffsetOnAxis(scol[:, hh:hh + 1], 0),
                        in_=eosb[:, hh, :], in_offset=None,
                        compute_op=ALU.add, oob_is_err=True,
                        bounds_check=NTOK + 255), 16)
                S("gpsimd", None, g_sc)

        accin = self.sb("c_acc", (128, 2, DIM), F32, stack=ps)
        x1in = self.sb("c_x1", (128, 2, DIM), F32, stack=ps)
        sq = self.sb("c_sq", (128, DIM), F32, stack=ps)
        stat = self.sb("c_stat", (128, 2, 16), F32, stack=ps)
        xcf = self.sb("c_xcf", (128, 2, DIM), F32, stack=ps)
        lnf = self.sb("c_lnf", (128, 2, DIM), F32, stack=ps)
        outt = self.sb("c_out", (128, 2, DIM), F32, stack=ps)

        f0 = dict(T=cT.n, V=cV.n, A=cA.n, dY=dY.n, dS=dS.n, dG=dG.n)
        a_ld = []

        @block.sync
        def _(y):
            for c, v in zip((cV, cA, cG, dG), (f0["V"], f0["A"], cG.n, f0["dG"])):
                if v > 0:
                    y.wait_ge(c.sem, v)
            accv = self.tm("acc") if False else self.ap("acc").rearrange(
                "(t p) f -> p t f", p=128)
            x1v = self.tm("x1_tok")
            for t in range(NT):
                if t >= 2:
                    y.wait_ge(cV.sem, f0["V"] + 2 * (t - 1))
                dY.inc(y.dma_start(out=accin[:, t % 2, :], in_=accv[:, t, :]), 16)
                a_ld.append(dY.inc(y.dma_start(out=x1in[:, t % 2, :], in_=x1v[:, t, :]), 16))

        @block.vector
        def _(v):
            for t in range(NT):
                v.wait_ge(dY.sem, a_ld[t])
                if t >= 2:
                    v.wait_ge(cA.sem, f0["A"] + (t - 1))
                st = stat[:, t % 2, :]
                at = accin[:, t % 2, :]
                v.tensor_tensor(sq[:], at, at, ALU.mult)
                v.reduce_sum(st[:, 0:1], at, axis=AX)
                v.drain()
                v.reduce_sum(st[:, 1:2], sq[:], axis=AX)
                v.tensor_scalar(st[:, 2:3], st[:, 0:1], 1.0 / DIM, None, ALU.mult)
                v.drain()
                v.tensor_scalar(st[:, 3:4], st[:, 1:2], 1.0 / DIM, None, ALU.mult)
                v.tensor_tensor(st[:, 4:5], st[:, 2:3], st[:, 2:3], ALU.mult)
                v.tensor_scalar(xcf[:, t % 2, :], at, st[:, 2:3], None, ALU.subtract)
                v.drain()
                v.tensor_tensor(st[:, 5:6], st[:, 3:4], st[:, 4:5], ALU.subtract)
                cV.inc(v.drain())
                v.wait_ge(cA.sem, f0["A"] + t + 1)
                if t >= 2:
                    v.wait_ge(dG.sem, f0["dG"] + 16 * (t - 1))
                v.tensor_tensor(outt[:, t % 2, :], x1in[:, t % 2, :],
                                lnf[:, t % 2, :], ALU.add)
                cV.inc(v.drain())

        @block.scalar
        def _(a):
            for t in range(NT):
                a.wait_ge(cV.sem, f0["V"] + 2 * t + 1)
                st = stat[:, t % 2, :]
                a.activation(out=st[:, 6:7], in_=st[:, 5:6], func=AF.Ln,
                             bias=self.konst[:, 0:1])
                a.drain()
                a.activation(out=st[:, 7:8], in_=st[:, 6:7], func=AF.Exp, scale=-0.5)
                a.drain()
                a.activation(out=lnf[:, t % 2, :], in_=xcf[:, t % 2, :],
                             func=AF.Copy, scale=st[:, 7:8])
                cA.inc(a.drain())

        @block.gpsimd
        def _(g):
            for t in range(NT):
                g.wait_ge(cV.sem, f0["V"] + 2 * t + 2)
                dG.inc(g.dma_start(out=self.tm("out")[:, t, :], in_=outt[:, t % 2, :]), 16)
            g.wait_ge(dG.sem, dG.n)
        self.expect(cV, f0["V"] + 2 * NT, "C V")

    def emit_debug(self, block):
        self.dbg = {d for d in self.dbg if d in self.dram}
        if not self.dbg:
            return
        for nm in sorted(self.dbg):
            t = self.dram[nm]
            self.dout("dbg_" + nm, list(t.shape), t.dtype)

        @block.gpsimd
        def _(g):
            for nm in sorted(self.dbg):
                src, dst = self.ap(nm), self.ap("dbg_" + nm)
                ins = g.dma_start(out=dst, in_=src)
                self.dG.inc(ins, 16)
            g.wait_ge(self.dG.sem, self.dG.n)


def shard_inputs(inputs):
    f32 = np.float32
    x = np.ascontiguousarray(np.asarray(inputs["L0_attn_x"], f32).reshape(NC * NTOK, DIM))
    qkv_w = np.ascontiguousarray(np.asarray(inputs["L0_attn_qkv_w"], f32))
    qkv_b = np.ascontiguousarray(
        np.broadcast_to(np.asarray(inputs["L0_attn_qkv_b"], f32)[None, :], (128, 3 * DIM)))

    def tiled(v):
        return np.ascontiguousarray(
            np.broadcast_to(np.tile(np.asarray(v, f32), HEADS)[None, :], (128, DIM)))

    qn_s, qn_b = tiled(inputs["L0_attn_qn_scale"]), tiled(inputs["L0_attn_qn_bias"])
    kn_s, kn_b = tiled(inputs["L0_attn_kn_scale"]), tiled(inputs["L0_attn_kn_bias"])
    G0 = np.ascontiguousarray(np.asarray(inputs["L0_attn_ttt_gate"], f32))
    V0 = np.ascontiguousarray(np.asarray(inputs["L0_attn_ttt_val"], f32))
    PT0 = np.ascontiguousarray(np.asarray(inputs["L0_attn_ttt_proj"], f32).T)
    proj_w = np.ascontiguousarray(np.asarray(inputs["L0_attn_proj_w"], f32))
    proj_b = np.ascontiguousarray(
        np.broadcast_to(np.asarray(inputs["L0_attn_proj_b"], f32)[None, :], (128, DIM)))
    gqT = np.ascontiguousarray(np.asarray(inputs["L0_ffn_gate_query"], f32).T)
    log_temp = np.asarray(inputs["L0_ffn_log_temp"], f32).reshape(1, 1)
    w1 = np.asarray(inputs["L0_ffn_w1"], f32)
    b1 = np.asarray(inputs["L0_ffn_b1"], f32)
    w2 = np.asarray(inputs["L0_ffn_w2"], f32)
    b2 = np.asarray(inputs["L0_ffn_b2"], f32)

    maps = []
    for c in range(NC):
        es = slice(2 * c, 2 * c + 2)
        b1c = np.ascontiguousarray(
            b1[es].reshape(E_OWN, HID // 128, 128).transpose(0, 2, 1))
        b2t = np.ascontiguousarray(
            np.broadcast_to(b2[es][:, None, :], (E_OWN, 128, DIM)))
        cs_ = slice(512 * c, 512 * (c + 1))
        cconst = np.zeros((1, 8), f32)
        cconst[0, 0] = (c % 2) * NTOK
        cconst[0, 1] = c // 2
        cconst[0, 2] = c
        maps.append({
            "x": np.ascontiguousarray(x[c * NTOK:(c + 1) * NTOK]),
            "qkv_w": qkv_w, "qkv_b": qkv_b,
            "qn_scale": qn_s, "qn_bias": qn_b,
            "kn_scale": kn_s, "kn_bias": kn_b,
            "ttt_gate": G0, "ttt_val": V0, "ttt_projT": PT0,
            "proj_w": proj_w, "proj_b": proj_b,
            "gate_queryT": gqT, "log_temp": log_temp,
            "gate_sh": np.ascontiguousarray(G0[:, cs_]),
            "val_sh": np.ascontiguousarray(V0[:, cs_]),
            "proj_sh": np.ascontiguousarray(PT0[:, cs_]),
            "w1": np.ascontiguousarray(w1[es]), "b1c": b1c,
            "w2": np.ascontiguousarray(w2[es]), "b2t": b2t,
            "cconst": cconst,
        })
    return maps


def run(inputs, dbg=(), race=False):
    b = B(dbg=dbg, race=race)
    nc = b.build()
    maps = shard_inputs(inputs)
    res = run_bass_kernel_spmd(nc, maps, core_ids=list(range(NC)))
    return res


def kernel(**inputs):
    res = run(inputs)
    out = np.concatenate([res.results[c]["out"] for c in range(NC)], axis=0)
    return out.reshape(4, 8192, DIM)


# version 08
# speedup vs baseline: 1.0326x; 1.0089x over previous
import sys
sys.path.insert(0, "/opt/trn_rl_repo")
import numpy as np
import ml_dtypes
from contextlib import ExitStack
import concourse.bass as bass
import concourse.mybir as mybir
from concourse.bass_core_rust import IndirectOffsetOnAxis
from concourse.bass_utils import run_bass_kernel_spmd

F32 = mybir.dt.float32
BF16 = mybir.dt.bfloat16
U32 = mybir.dt.uint32
AF = mybir.ActivationFunctionType
ALU = mybir.AluOpType
AX = mybir.AxisListType.X

NC = 8
NTOK = 4096
NT = NTOK // 128
DIM = 1024
HID = 4096
HEADS, HD = 16, 64
LR = 0.1
NS_ITERS, TTT_STEPS = 5, 2
NDTOT = 32768 * DIM
E_OWN, CAP, SINK_IT, BIS_IT = 2, 256, 6, 26
RG = [list(range(NC))]
NS_A, NS_B, NS_C = 3.4445, -4.775, 2.0315


class Ch:
    def __init__(self, nc, ctx, name):
        self.sem = ctx.enter_context(nc.semaphore(name))
        self.n = 0

    def inc(self, instr, k=1):
        instr.then_inc(self.sem, k)
        self.n += k
        return self.n


class B:
    def __init__(self, dbg=(), race=True):
        self.dbg = set(dbg)
        self.nc = bass.Bass("TRN2", target_bir_lowering=False, num_devices=NC,
                            detect_race_conditions=race)
        self.ctx = ExitStack()
        self.dram = {}

    def din(self, name, shape, dt=F32):
        self.dram[name] = self.nc.dram_tensor(name, list(shape), dt, kind="ExternalInput")

    def dout(self, name, shape, dt=F32):
        self.dram[name] = self.nc.dram_tensor(name, list(shape), dt, kind="ExternalOutput")

    def dint(self, name, shape, dt=F32):
        self.dram[name] = self.nc.dram_tensor(name, list(shape), dt, kind="Internal")

    _uid = 0

    def _nm(self, name):
        B._uid += 1
        return f"{name}_{B._uid}"

    def sb(self, name, shape, dt=F32, stack=None):
        return (stack or self.ctx).enter_context(
            self.nc.sbuf_tensor(self._nm(name), list(shape), dt))

    def psum(self, name, shape, dt=F32, stack=None):
        return (stack or self.ctx).enter_context(
            self.nc.psum_tensor(self._nm(name), list(shape), dt))

    def ap(self, name):
        return self.dram[name].ap()

    def tm(self, name):
        return self.ap(name).rearrange("(t p) f -> p t f", p=128)

    def declare_io(self):
        d = self.din
        d("x", (NTOK, DIM))
        d("qkv_w", (DIM, 3 * DIM))
        d("qkv_b", (128, 3 * DIM))
        d("qn_scale", (128, DIM))
        d("qn_bias", (128, DIM))
        d("kn_scale", (128, DIM))
        d("kn_bias", (128, DIM))
        d("ttt_gate", (DIM, HID))
        d("ttt_val", (DIM, HID))
        d("ttt_projT", (DIM, HID))
        d("proj_w", (DIM, DIM))
        d("proj_b", (128, DIM))
        d("gate_queryT", (DIM, 16))
        d("log_temp", (1, 1))
        d("w1", (E_OWN, DIM, HID))
        d("b1c", (E_OWN, 128, HID // 128))
        d("w2", (E_OWN, HID, DIM))
        d("b2t", (E_OWN, 128, DIM))
        d("cconst", (1, 8))
        d("gate_sh", (DIM, 512))
        d("val_sh", (DIM, 512))
        d("proj_sh", (DIM, 512))
        self.dout("out", (NTOK, DIM))

        di = self.dint
        di("ln0b", (NTOK, DIM), BF16)
        di("q_tok", (NTOK, DIM), BF16)
        di("k_tok", (NTOK, DIM), BF16)
        di("vc_tok", (NTOK, DIM), BF16)
        di("h_tok", (NTOK, HID), BF16)
        di("sw_tok", (NTOK, HID), BF16)
        di("uswp_tok", (NTOK, HID), BF16)
        di("R_tok", (NTOK, DIM), BF16)
        di("dV_tok", (NTOK, HID), BF16)
        di("dG_tok", (NTOK, HID), BF16)
        di("predq_tok", (NTOK, DIM), BF16)
        di("G_work", (8, DIM, 512), BF16)
        di("V_work", (8, DIM, 512), BF16)
        di("PT_work", (8, DIM, 512), BF16)
        di("G_m", (DIM, 512))
        di("V_m", (DIM, 512))
        di("PT_m", (DIM, 512))
        for g in ("gP", "gV", "gG"):
            di(g + "_part", (HID, DIM), BF16)
            di(g + "_rs", (HID // NC, DIM), BF16)
        for p in range(3):
            di(f"nsAp_{p}", (DIM, DIM), BF16)
            di(f"nsA_{p}", (DIM, DIM), BF16)
            di(f"nsXsp_{p}", (DIM, 512), BF16)
        di("norm_part", (1, 3))
        di("norm_out", (1, 3))
        di("x1_tok", (NTOK, DIM))
        di("ln1_tok", (NTOK, DIM), BF16)
        di("ln1_all", (NC * NTOK, DIM), BF16)
        di("logitsT_own", (16, NTOK))
        di("logitsT_all", (NC * 16, NTOK))
        di("trank_own", (16, 8192))
        di("idx_own", (8, CAP), U32)
        di("idx_all", (128, CAP), U32)
        di("gates_own", (8, CAP))
        di("gates_all", (NC * 8, CAP))
        di("eo_own", (8, CAP, DIM), BF16)
        di("eo_all", (NC * 8, CAP, DIM), BF16)
        di("acc", (NTOK + 256, DIM))
        di("sink_dump", (64, 8192))
        di("rank_dump", (64, 8192))

    def barrier(self, block):
        chans = [self.cT, self.cV, self.cA, self.cG, self.dY, self.dS, self.dG, self.cC]
        vals = [c.n for c in chans]

        def mk(ch):
            def f(eng):
                eng.drain()
                ch.inc(eng.nop())
                for c, v in zip(chans, vals):
                    if v > 0:
                        eng.wait_ge(c.sem, v)
            return f
        block.tensor(mk(self.cT))
        block.vector(mk(self.cV))
        block.scalar(mk(self.cA))
        block.gpsimd(mk(self.cG))

        def fy(eng):
            for c, v in zip(chans, vals):
                if v > 0:
                    eng.wait_ge(c.sem, v)
        block.sync(fy)

    def expect(self, ch, val, tag):
        assert ch.n == val, f"{tag}: predicted {val} != actual {ch.n}"

    def build(self):
        nc = self.nc
        self.declare_io()
        ctx = self.ctx
        self.dY = Ch(nc, ctx, "dY")
        self.dS = Ch(nc, ctx, "dS")
        self.dG = Ch(nc, ctx, "dG")
        self.cT = Ch(nc, ctx, "cT")
        self.cV = Ch(nc, ctx, "cV")
        self.cA = Ch(nc, ctx, "cA")
        self.cG = Ch(nc, ctx, "cG")
        self.cC = Ch(nc, ctx, "cC")

        self.konst = self.sb("konst", (128, 4))
        self.ident = self.sb("ident", (128, 128), BF16)
        self.identf = self.sb("identf", (128, 128), F32)
        self.iota_cap = self.sb("iota_cap", (128, 256))
        self.iota_np = self.sb("iota_np", (128, 64))

        with nc.Block() as block:
            @block.vector
            def _(v):
                v.memset(self.konst[:, 0:1], 1e-5)
                v.memset(self.konst[:, 1:2], 1e-6)
                v.memset(self.konst[:, 2:4], 0.0)
                v.memset(self.ident[:], 0.0)
                v.memset(self.identf[:], 0.0)
                self.cV.inc(v.drain())

            @block.gpsimd
            def _(g):
                g.wait_ge(self.cV.sem, self.cV.n)
                for t in (self.ident, self.identf):
                    g.affine_select(out=t[:], in_=t[:],
                                    compare_op=ALU.not_equal, fill=1.0, base=0,
                                    pattern=[[-1, 128]], channel_multiplier=1)
                g.iota(self.iota_cap[:], pattern=[[1, 256]], base=1,
                       channel_multiplier=0, allow_small_or_imprecise_dtypes=True)
                g.iota(self.iota_np[:], pattern=[[128, 64]], base=0,
                       channel_multiplier=1, allow_small_or_imprecise_dtypes=True)
                for src, dst in (("ttt_gate", "G_work"), ("ttt_val", "V_work"),
                                 ("ttt_projT", "PT_work")):
                    for b_ in range(8):
                        ins = g.dma_start(out=self.ap(dst)[b_],
                                          in_=self.ap(src)[:, b_ * 512:(b_ + 1) * 512])
                        self.dG.inc(ins, 16)
                self.cG.inc(g.nop())
                self.init_done = self.cG.n

            with ExitStack() as ps:
                self.phase_a1(block, ps)
                self.barrier(block)
            with ExitStack() as ps:
                self.phase_a2(block, ps)
                self.barrier(block)
            for step in range(TTT_STEPS):
                with ExitStack() as ps:
                    self.ttt_F(block, ps, step)
                    self.barrier(block)
                with ExitStack() as ps:
                    self.ttt_P(block, ps, step)
                    self.barrier(block)
                with ExitStack() as ps:
                    self.ttt_D(block, ps, step)
                    self.barrier(block)
                for xn, rn, on in (("h_tok", "R_tok", "gP_part"),
                                   ("dV_tok", "k_tok", "gV_part"),
                                   ("dG_tok", "k_tok", "gG_part")):
                    with ExitStack() as ps:
                        self.ttt_G1(block, ps, xn, rn, on)
                        self.barrier(block)
                self.ttt_RS(block)
                self.barrier(block)
                with ExitStack() as ps:
                    self.ttt_NS(block, ps, step)
                    self.barrier(block)
            with ExitStack() as ps:
                self.ttt_F(block, ps, 0, query=True)
                self.barrier(block)
            with ExitStack() as ps:
                self.ttt_P(block, ps, 0, query=True)
                self.barrier(block)
            with ExitStack() as ps:
                self.ttt_O(block, ps)
                self.barrier(block)
            with ExitStack() as ps:
                self.ttt_L(block, ps)
                self.barrier(block)
            if "skipffn" not in self.dbg:
                self.run_ffn(block)
            self.emit_debug(block)
        return nc

    def run_ffn(self, block):
        if True:
            with ExitStack() as ps:
                self.ffn_R(block, ps)
                self.barrier(block)
            with ExitStack() as ps:
                self.ffn_E(block, ps)
                self.barrier(block)
            with ExitStack() as ps:
                self.ffn_C(block, ps)
                self.barrier(block)

    def phase_a1(self, block, ps):
        nc = self.nc
        dY, dS, dG = self.dY, self.dS, self.dG
        cT, cV, cA, cG = self.cT, self.cV, self.cA, self.cG

        xin = self.sb("pa_x", (128, 2, DIM), stack=ps)
        ln0t = self.sb("pa_ln0", (128, 2, DIM), BF16, stack=ps)
        stat = self.sb("pa_stat", (128, 2, 16), stack=ps)
        sq = self.sb("pa_sq", (128, DIM), stack=ps)
        xc = self.sb("pa_xc", (128, 2, DIM), stack=ps)

        x_ld = []
        v1b, a1b, ds1b = cV.n, cA.n, dS.n

        @block.sync
        def _(y):
            xv = self.tm("x")
            for t in range(NT):
                if t >= 2:
                    y.wait_ge(cV.sem, v1b + (t - 1))
                    y.wait_ge(dS.sem, ds1b + 16 * (t - 1))
                ins = y.dma_start(out=xin[:, t % 2, :], in_=xv[:, t, :])
                x_ld.append(dY.inc(ins, 16))

        @block.vector
        def _(v):
            for t in range(NT):
                v.wait_ge(dY.sem, x_ld[t])
                if t >= 2:
                    v.wait_ge(cA.sem, a1b + (t - 1))
                st = stat[:, t % 2, :]
                xt = xin[:, t % 2, :]
                v.tensor_tensor(sq[:], xt, xt, ALU.mult)
                v.reduce_sum(st[:, 0:1], xt, axis=AX)
                v.drain()
                v.reduce_sum(st[:, 1:2], sq[:], axis=AX)
                v.tensor_scalar(st[:, 2:3], st[:, 0:1], 1.0 / DIM, None, ALU.mult)
                v.drain()
                v.tensor_scalar(st[:, 3:4], st[:, 1:2], 1.0 / DIM, None, ALU.mult)
                v.tensor_tensor(st[:, 4:5], st[:, 2:3], st[:, 2:3], ALU.mult)
                v.tensor_scalar(xc[:, t % 2, :], xt, st[:, 2:3], None, ALU.subtract)
                v.drain()
                v.tensor_tensor(st[:, 5:6], st[:, 3:4], st[:, 4:5], ALU.subtract)
                cV.inc(v.drain())

        @block.scalar
        def _(a):
            a.wait_ge(cG.sem, self.init_done)
            for t in range(NT):
                a.wait_ge(cV.sem, v1b + (t + 1))
                if t >= 2:
                    a.wait_ge(dS.sem, ds1b + 16 * (t - 1))
                st = stat[:, t % 2, :]
                a.activation(out=st[:, 6:7], in_=st[:, 5:6], func=AF.Ln, bias=self.konst[:, 0:1])
                a.drain()
                a.activation(out=st[:, 7:8], in_=st[:, 6:7], func=AF.Exp, scale=-0.5)
                a.drain()
                a.activation(out=ln0t[:, t % 2, :], in_=xc[:, t % 2, :],
                             func=AF.Copy, scale=st[:, 7:8])
                cA.inc(a.drain())
                ins = a.dma_start(out=self.tm("ln0b")[:, t, :], in_=ln0t[:, t % 2, :])
                dS.inc(ins, 16)
        self.expect(cV, v1b + NT, "pa1 V")
        self.expect(cA, a1b + NT, "pa1 A")
        self.expect(dS, ds1b + 16 * NT, "pa1 dS")

    def phase_a2(self, block, ps):
        nc = self.nc
        dY, dS, dG = self.dY, self.dS, self.dG
        cT, cV, cA, cG = self.cT, self.cV, self.cA, self.cG
        ln0T = self.sb("pa_ln0T", (128, 8, NTOK), BF16, stack=ps)
        wq = self.sb("pa_wq", (128, 8, 3 * DIM), BF16, stack=ps)
        cst = self.sb("pa_cst", (128, 7 * DIM), stack=ps)
        pqkv = self.psum("pa_pqkv", (128, 3 * DIM), stack=ps)
        hs = self.sb("pa_hs", (128, 128), stack=ps)
        qn1 = self.sb("pa_qn1", (128, DIM), stack=ps)
        qn2 = self.sb("pa_qn2", (128, DIM), stack=ps)
        ob = self.sb("pa_ob", (128, 2, 3, DIM), BF16, stack=ps)

        xbar_ld = []
        all_sp1 = dS.n
        t2b, v2b, a2b, ds2b, dg2b = cT.n, cV.n, cA.n, dS.n, dG.n
        del nc

        @block.sync
        def _(y):
            for i, nm in enumerate(("qn_scale", "qn_bias", "kn_scale", "kn_bias")):
                ins = y.dma_start(out=cst[:, i * DIM:(i + 1) * DIM], in_=self.ap(nm))
                dY.inc(ins, 16)
            ins = y.dma_start(out=cst[:, 4 * DIM:7 * DIM], in_=self.ap("qkv_b"))
            cst_ld = dY.inc(ins, 16)
            y.wait_ge(dS.sem, all_sp1)
            for k in range(8):
                ins = y.dma_start(out=ln0T[:, k, :],
                                  in_=self.ap("ln0b")[:, k * 128:(k + 1) * 128],
                                  transpose=True)
                xbar_ld.append(dY.inc(ins, 16))
            self.pa_cst_ld = cst_ld

        @block.gpsimd
        def _(g):
            ins = g.dma_start(out=wq[:],
                              in_=self.ap("qkv_w").rearrange("(k p) n -> p k n", p=128))
            self.pa_wq_ld = dG.inc(ins, 16)

        @block.tensor
        def _(t_):
            t_.wait_ge(dY.sem, xbar_ld[-1])
            t_.wait_ge(dG.sem, self.pa_wq_ld)
            for t in range(NT):
                if t >= 1:
                    t_.wait_ge(cV.sem, v2b + 3 * (t - 1) + 1)
                for k in range(8):
                    lhsT = ln0T[:, k, t * 128:(t + 1) * 128]
                    for n in range(6):
                        ins = t_.matmul(pqkv[:, n * 512:(n + 1) * 512], lhsT,
                                        wq[:, k, n * 512:(n + 1) * 512],
                                        start=(k == 0), stop=(k == 7))
                cT.inc(ins)

        sqq = self.sb("pa_sqq", (128, DIM), stack=ps)
        sqk = self.sb("pa_sqk", (128, DIM), stack=ps)
        qkvb = self.sb("pa_qkvb", (128, 2, 3 * DIM), stack=ps)

        @block.vector
        def _(v):
            v.wait_ge(dY.sem, self.pa_cst_ld)
            for t in range(NT):
                v.wait_ge(cT.sem, t2b + t + 1)
                if t >= 2:
                    v.wait_ge(cA.sem, a2b + 2 * t - 3)
                    v.wait_ge(dG.sem, dg2b + 32 * (t - 1))
                v.tensor_tensor(qkvb[:, t % 2, :], pqkv[:],
                                cst[:, 4 * DIM:7 * DIM], ALU.add)
                cV.inc(v.drain())
                qb = qkvb[:, t % 2, :]
                v.wait_ge(cA.sem, a2b + 2 * t + 1)
                h3q = hs[:, 0:16].rearrange("p (h o) -> p h o", o=1)
                h3k = hs[:, 64:80].rearrange("p (h o) -> p h o", o=1)
                v.reduce_sum(h3q, sqq[:].rearrange("p (h j) -> p h j", j=HD), axis=AX)
                v.reduce_sum(h3k, sqk[:].rearrange("p (h j) -> p h j", j=HD), axis=AX)
                cV.inc(v.drain())
                v.wait_ge(cA.sem, a2b + 2 * t + 2)
                for qi, (sqcol, s_off, b_off, psl) in enumerate((
                        (16, 0, DIM, slice(0, DIM)),
                        (80, 2 * DIM, 3 * DIM, slice(DIM, 2 * DIM)))):
                    v.tensor_scalar(hs[:, sqcol + 16:sqcol + 32],
                                    hs[:, sqcol:sqcol + 16], 1e-6, None, ALU.add)
                    v.drain()
                    v.reciprocal(hs[:, sqcol + 32:sqcol + 48],
                                 hs[:, sqcol + 16:sqcol + 32])
                    v.drain()
                    f3 = hs[:, sqcol + 32:sqcol + 48].rearrange("p (h o) -> p h o", o=1)
                    q3 = qb[:, psl].rearrange("p (h j) -> p h j", j=HD)
                    o3 = qn1[:].rearrange("p (h j) -> p h j", j=HD)
                    v.tensor_tensor(o3, q3, f3.to_broadcast([128, HEADS, HD]), ALU.mult)
                    v.drain()
                    v.tensor_tensor(qn2[:], qn1[:], cst[:, s_off:s_off + DIM], ALU.mult)
                    v.drain()
                    v.tensor_tensor(ob[:, t % 2, qi, :], qn2[:],
                                    cst[:, b_off:b_off + DIM], ALU.add)
                cV.inc(v.drain())

        @block.scalar
        def _(a):
            for t in range(NT):
                a.wait_ge(cV.sem, v2b + 3 * t + 1)
                if t >= 2:
                    a.wait_ge(dS.sem, ds2b + 16 * (t - 1))
                qb = qkvb[:, t % 2, :]
                a.activation(out=sqq[:], in_=qb[:, 0:DIM], func=AF.Square)
                a.activation(out=sqk[:], in_=qb[:, DIM:2 * DIM], func=AF.Square)
                a.activation(out=ob[:, t % 2, 2, :], in_=qb[:, 2 * DIM:3 * DIM],
                             func=AF.Copy, scale=2.0 / NDTOT)
                cA.inc(a.drain())
                ins = a.dma_start(out=self.tm("vc_tok")[:, t, :], in_=ob[:, t % 2, 2, :])
                dS.inc(ins, 16)
                a.wait_ge(cV.sem, v2b + 3 * t + 2)
                a.activation(out=hs[:, 16:32], in_=hs[:, 0:16], func=AF.Sqrt)
                a.activation(out=hs[:, 80:96], in_=hs[:, 64:80], func=AF.Sqrt)
                cA.inc(a.drain())

        @block.gpsimd
        def _(g):
            for t in range(NT):
                g.wait_ge(cV.sem, v2b + 3 * t + 3)
                ins = g.dma_start(out=self.tm("q_tok")[:, t, :], in_=ob[:, t % 2, 0, :])
                dG.inc(ins, 16)
                ins = g.dma_start(out=self.tm("k_tok")[:, t, :], in_=ob[:, t % 2, 1, :])
                dG.inc(ins, 16)
        self.expect(cT, t2b + NT, "pa2 T")
        self.expect(cV, v2b + 3 * NT, "pa2 V")
        self.expect(cA, a2b + 2 * NT, "pa2 A")
        self.expect(dG, dg2b + 32 * NT + 16, "pa2 dG")

    def ttt_F(self, block, ps, step, query=False):
        dY, dS, dG = self.dY, self.dS, self.dG
        cT, cV, cA, cG = self.cT, self.cV, self.cA, self.cG
        src_tok = "q_tok" if query else "k_tok"

        kT = self.sb("f_kT", (128, 8, NTOK), BF16, stack=ps)
        gh = self.sb("f_gh", (128, 8, 4, 512), BF16, stack=ps)
        vh = self.sb("f_vh", (128, 8, 4, 512), BF16, stack=ps)
        pa = self.psum("f_pa", (128, 2048), stack=ps)
        pu = self.psum("f_pu", (128, 2048), stack=ps)
        swt = self.sb("f_swt", (128, 2, 2048), BF16, stack=ps)
        swpt = self.sb("f_swpt", (128, 2, 2048), BF16, stack=ps)
        ubt = self.sb("f_ubt", (128, 2, 2048), BF16, stack=ps)
        ht = self.sb("f_ht", (128, 2, 2048), BF16, stack=ps)
        uswpt = self.sb("f_uswpt", (128, 2, 2048), BF16, stack=ps)

        kT_ld = []
        gv_ld = []
        t0 = dict(T=cT.n, V=cV.n, A=cA.n, dG=dG.n, dS=dS.n, dY=dY.n)
        NSP = 32 if not query else 16

        @block.sync
        def _(y):
            for k in range(8):
                ins = y.dma_start(out=kT[:, k, :],
                                  in_=self.ap(src_tok)[:, k * 128:(k + 1) * 128],
                                  transpose=True)
                kT_ld.append(dY.inc(ins, 16))
            for i in range(2 * NT):
                h, t = divmod(i, NT)
                y.wait_ge(cV.sem, t0["V"] + i + 1)
                ins = y.dma_start(out=self.tm("h_tok")[:, t, h * 2048:(h + 1) * 2048],
                                  in_=ht[:, i % 2, :])
                dY.inc(ins, 16)
                if not query:
                    ins = y.dma_start(
                        out=self.tm("uswp_tok")[:, t, h * 2048:(h + 1) * 2048],
                        in_=uswpt[:, i % 2, :])
                    dY.inc(ins, 16)

        @block.gpsimd
        def _(g):
            for h in range(2):
                if h == 1:
                    g.wait_ge(cT.sem, t0["T"] + NT)
                for nm, dst in (("G_work", gh), ("V_work", vh)):
                    for b_ in range(4):
                        ins = g.dma_start(
                            out=dst[:, :, b_, :],
                            in_=self.ap(nm)[4 * h + b_].rearrange(
                                "(k p) n -> p k n", p=128))
                        dG.inc(ins, 16)
                    gv_ld.append(dG.n)

        @block.tensor
        def _(t_):
            t_.wait_ge(dY.sem, kT_ld[-1])
            for h in range(2):
                t_.wait_ge(dG.sem, gv_ld[2 * h + 1])
                for t in range(NT):
                    i = h * NT + t
                    if i >= 1:
                        t_.wait_ge(cA.sem, t0["A"] + i)
                    for k in range(8):
                        lhsT = kT[:, k, t * 128:(t + 1) * 128]
                        for b_ in range(4):
                            t_.matmul(pa[:, b_ * 512:(b_ + 1) * 512], lhsT,
                                      gh[:, k, b_, :], start=(k == 0), stop=(k == 7))
                        for b_ in range(4):
                            ins = t_.matmul(pu[:, b_ * 512:(b_ + 1) * 512], lhsT,
                                            vh[:, k, b_, :], start=(k == 0), stop=(k == 7))
                    cT.inc(ins)

        @block.scalar
        def _(a):
            for i in range(2 * NT):
                a.wait_ge(cT.sem, t0["T"] + i + 1)
                if i >= 2:
                    a.wait_ge(cV.sem, t0["V"] + i - 1)
                    if not query:
                        a.wait_ge(dS.sem, t0["dS"] + 16 * (i - 1))
                a.activation(out=swt[:, i % 2, :], in_=pa[:], func=AF.Silu)
                if not query:
                    a.activation(out=swpt[:, i % 2, :], in_=pa[:], func=AF.Derivative_silu)
                a.activation(out=ubt[:, i % 2, :], in_=pu[:], func=AF.Copy)
                cA.inc(a.drain())
                if not query:
                    h, t = divmod(i, NT)
                    ins = a.dma_start(
                        out=self.tm("sw_tok")[:, t, h * 2048:(h + 1) * 2048],
                        in_=swt[:, i % 2, :])
                    dS.inc(ins, 16)

        @block.vector
        def _(v):
            for i in range(2 * NT):
                v.wait_ge(cA.sem, t0["A"] + i + 1)
                if i >= 2:
                    v.wait_ge(dY.sem, t0["dY"] + 128 + NSP * (i - 1))
                v.tensor_tensor(ht[:, i % 2, :], swt[:, i % 2, :],
                                ubt[:, i % 2, :], ALU.mult)
                if not query:
                    v.tensor_tensor(uswpt[:, i % 2, :], ubt[:, i % 2, :],
                                    swpt[:, i % 2, :], ALU.mult)
                cV.inc(v.drain())

        self.expect(cT, t0["T"] + 2 * NT, "F T")
        self.expect(cV, t0["V"] + 2 * NT, "F V")

    def ttt_P(self, block, ps, step, query=False):
        dY, dS, dG = self.dY, self.dS, self.dG
        cT, cV, cA = self.cT, self.cV, self.cA

        hT = self.sb("p_hT", (128, 1, 32, 1024), BF16, stack=ps)
        Pb = self.sb("p_Pb", (128, 32, 1024), BF16, stack=ps)
        vct = self.sb("p_vct", (128, 4, DIM), BF16, stack=ps)
        Rt = self.sb("p_Rt", (128, 4, DIM), BF16, stack=ps)
        pp = self.psum("p_pp", (128, 4, DIM), stack=ps)
        NSC, TSC = 4, 8
        S = lambda e, c, f: self.seq(block, e, c, f)

        def s_pb(a):
            for b_ in range(8):
                for j in range(4):
                    dS.inc(a.dma_start(
                        out=Pb[:, b_ * 4 + j, :],
                        in_=self.ap("PT_work")[b_][:, j * 128:(j + 1) * 128],
                        transpose=True), 16)
        S("scalar", None, s_pb)

        def v_z(v):
            for i in range(4):
                v.memset(vct[:, i, :], 0.0)
        S("vector", cV, v_z)

        for sc in range(NSC):
            def y_h(y, sc=sc):
                for j in range(32):
                    dY.inc(y.dma_start(
                        out=hT[:, 0, j, :],
                        in_=self.ap("h_tok")[sc * 1024:(sc + 1) * 1024,
                                             j * 128:(j + 1) * 128],
                        transpose=True), 16)
            S("sync", None, y_h)
            for tg in range(2):
                gts = [sc * TSC + tg * 4 + i for i in range(4)]

                if not query:
                    def y_vc(y, gts=gts):
                        for i, gt in enumerate(gts):
                            dY.inc(y.dma_start(out=vct[:, i, :],
                                               in_=self.tm("vc_tok")[:, gt, :]), 16)
                    S("sync", None, y_vc)

                def t_mm(t_, sc=sc, tg=tg, gts=gts):
                    for i, gt in enumerate(gts):
                        t = tg * 4 + i
                        for k in range(32):
                            lhsT = hT[:, 0, k, t * 128:(t + 1) * 128]
                            t_.matmul(pp[:, i, 0:512], lhsT, Pb[:, k, 0:512],
                                      start=(k == 0), stop=(k == 31))
                            t_.matmul(pp[:, i, 512:1024], lhsT, Pb[:, k, 512:1024],
                                      start=(k == 0), stop=(k == 31))
                S("tensor", cT, t_mm)

                def v_r(v, gts=gts):
                    for i, gt in enumerate(gts):
                        v.scalar_tensor_tensor(Rt[:, i, :], pp[:, i, :],
                                               1.0 if query else 2.0 / NDTOT,
                                               vct[:, i, :], ALU.mult, ALU.subtract)
                S("vector", cV, v_r)

                def g_sp(g, gts=gts):
                    dst = "predq_tok" if query else "R_tok"
                    for i, gt in enumerate(gts):
                        dG.inc(g.dma_start(out=self.tm(dst)[:, gt, :],
                                           in_=Rt[:, i, :]), 16)
                S("gpsimd", None, g_sp)
        S("gpsimd", None, lambda g: g.wait_ge(dG.sem, dG.n))

    def ttt_D(self, block, ps, step):
        dY, dS, dG = self.dY, self.dS, self.dG
        cT, cV, cA = self.cT, self.cV, self.cA

        RT = self.sb("d_RT", (128, 8, NTOK), BF16, stack=ps)
        PTh = self.sb("d_PTh", (128, 8, 4, 512), BF16, stack=ps)
        swt = self.sb("d_swt", (128, 2, 2048), BF16, stack=ps)
        uswpt = self.sb("d_uswpt", (128, 2, 2048), BF16, stack=ps)
        dvt = self.sb("d_dvt", (128, 2, 2048), BF16, stack=ps)
        dgt = self.sb("d_dgt", (128, 2, 2048), BF16, stack=ps)
        pdh = self.psum("d_pdh", (128, 2, 2048), stack=ps)

        t0 = dict(T=cT.n, V=cV.n, dY=dY.n, dS=dS.n, dG=dG.n)
        RT_ld = []
        pt_ld = []
        s_ld = []

        @block.sync
        def _(y):
            for k in range(8):
                ins = y.dma_start(out=RT[:, k, :],
                                  in_=self.ap("R_tok")[:, k * 128:(k + 1) * 128],
                                  transpose=True)
                RT_ld.append(dY.inc(ins, 16))
            for i in range(2 * NT):
                h, t = divmod(i, NT)
                if i >= 2:
                    y.wait_ge(cV.sem, t0["V"] + i - 1)
                ins = y.dma_start(out=swt[:, i % 2, :],
                                  in_=self.tm("sw_tok")[:, t, h * 2048:(h + 1) * 2048])
                dY.inc(ins, 16)
                ins = y.dma_start(out=uswpt[:, i % 2, :],
                                  in_=self.tm("uswp_tok")[:, t, h * 2048:(h + 1) * 2048])
                s_ld.append(dY.inc(ins, 16))

        @block.scalar
        def _(a):
            for h in range(2):
                if h == 1:
                    a.wait_ge(cT.sem, t0["T"] + NT)
                for b_ in range(4):
                    ins = a.dma_start(
                        out=PTh[:, :, b_, :],
                        in_=self.ap("PT_work")[4 * h + b_].rearrange(
                            "(k p) n -> p k n", p=128))
                    dS.inc(ins, 16)
                pt_ld.append(dS.n)

        @block.tensor
        def _(t_):
            t_.wait_ge(dY.sem, RT_ld[-1])
            for h in range(2):
                t_.wait_ge(dS.sem, pt_ld[h])
                for t in range(NT):
                    i = h * NT + t
                    if i >= 2:
                        t_.wait_ge(cV.sem, t0["V"] + i - 1)
                    for k in range(8):
                        lhsT = RT[:, k, t * 128:(t + 1) * 128]
                        for b_ in range(4):
                            ins = t_.matmul(pdh[:, i % 2, b_ * 512:(b_ + 1) * 512],
                                            lhsT, PTh[:, k, b_, :],
                                            start=(k == 0), stop=(k == 7))
                    cT.inc(ins)

        @block.vector
        def _(v):
            for i in range(2 * NT):
                v.wait_ge(cT.sem, t0["T"] + i + 1)
                v.wait_ge(dY.sem, s_ld[i])
                if i >= 2:
                    v.wait_ge(dG.sem, t0["dG"] + 32 * (i - 1))
                v.tensor_tensor(dvt[:, i % 2, :], pdh[:, i % 2, :],
                                swt[:, i % 2, :], ALU.mult)
                v.tensor_tensor(dgt[:, i % 2, :], pdh[:, i % 2, :],
                                uswpt[:, i % 2, :], ALU.mult)
                cV.inc(v.drain())

        @block.gpsimd
        def _(g):
            for i in range(2 * NT):
                g.wait_ge(cV.sem, t0["V"] + i + 1)
                h, t = divmod(i, NT)
                ins = g.dma_start(out=self.tm("dV_tok")[:, t, h * 2048:(h + 1) * 2048],
                                  in_=dvt[:, i % 2, :])
                dG.inc(ins, 16)
                ins = g.dma_start(out=self.tm("dG_tok")[:, t, h * 2048:(h + 1) * 2048],
                                  in_=dgt[:, i % 2, :])
                dG.inc(ins, 16)
        self.expect(cT, t0["T"] + 2 * NT, "D T")

    def ttt_G1(self, block, ps, xname, rhsname, outname):
        dY, dS, dG = self.dY, self.dS, self.dG
        cT, cV = self.cT, self.cV

        rhs = self.sb("g_rhs", (128, 32, 1024), BF16, stack=ps)
        strip = self.sb("g_strip", (128, 2, 32, 128), BF16, stack=ps)
        ev = self.sb("g_ev", (128, 2, DIM), BF16, stack=ps)
        pg = self.psum("g_pg", (128, 2, DIM), stack=ps)

        t0 = dict(T=cT.n, V=cV.n, dY=dY.n, dS=dS.n, dG=dG.n)
        rhs_ld = []
        strip_ld = []

        @block.scalar
        def _(a):
            ins = a.dma_start(out=rhs[:],
                              in_=self.ap(rhsname).rearrange("(t p) f -> p t f", p=128))
            rhs_ld.append(dS.inc(ins, 16))

        @block.sync
        def _(y):
            xv = self.ap(xname).rearrange("(t p) f -> p t f", p=128)
            for m in range(32):
                if m >= 2:
                    y.wait_ge(cT.sem, t0["T"] + m - 1)
                ins = y.dma_start(out=strip[:, m % 2, :, :],
                                  in_=xv[:, :, m * 128:(m + 1) * 128])
                strip_ld.append(dY.inc(ins, 16))

        @block.tensor
        def _(t_):
            t_.wait_ge(dS.sem, rhs_ld[0])
            for m in range(32):
                t_.wait_ge(dY.sem, strip_ld[m])
                if m >= 2:
                    t_.wait_ge(cV.sem, t0["V"] + m - 1)
                for kt in range(32):
                    lhsT = strip[:, m % 2, kt, :]
                    t_.matmul(pg[:, m % 2, 0:512], lhsT, rhs[:, kt, 0:512],
                              start=(kt == 0), stop=(kt == 31))
                    ins = t_.matmul(pg[:, m % 2, 512:1024], lhsT, rhs[:, kt, 512:1024],
                                    start=(kt == 0), stop=(kt == 31))
                cT.inc(ins)

        @block.vector
        def _(v):
            for m in range(32):
                v.wait_ge(cT.sem, t0["T"] + m + 1)
                if m >= 2:
                    v.wait_ge(dG.sem, t0["dG"] + 16 * (m - 1))
                v.tensor_copy(ev[:, m % 2, :], pg[:, m % 2, :])
                cV.inc(v.drain())

        @block.gpsimd
        def _(g):
            for m in range(32):
                g.wait_ge(cV.sem, t0["V"] + m + 1)
                ins = g.dma_start(out=self.ap(outname)[m * 128:(m + 1) * 128, :],
                                  in_=ev[:, m % 2, :])
                dG.inc(ins, 16)
        self.expect(cT, t0["T"] + 32, "G1 T")

    def ttt_RS(self, block):
        cC, dG = self.cC, self.dG

        @block.gpsimd
        def _(g):
            g.wait_ge(dG.sem, dG.n)
            for nm in ("gP", "gV", "gG"):
                ins = g.collective_compute(
                    "ReduceScatter", ALU.add, replica_groups=RG,
                    ins=[self.ap(nm + "_part").opt()],
                    outs=[self.ap(nm + "_rs").opt()])
                cC.inc(ins)
            g.wait_ge(cC.sem, cC.n)

    def ttt_NS(self, block, ps, step):
        dY, dS, dG = self.dY, self.dS, self.dG
        cT, cV, cA, cG, cC = self.cT, self.cV, self.cA, self.cG, self.cC

        Xb = self.sb("ns_Xb", (128, 3, 8, 512), BF16, stack=ps)
        X = self.sb("ns_X", (128, 3, 8, 512), F32, stack=ps)
        XT = self.sb("ns_XT", (128, 4, 1024), BF16, stack=ps)
        Asb = self.sb("ns_Asb", (128, 8, 1024), BF16, stack=ps)
        Y32 = self.sb("ns_Y32", (128, 8, 512), F32, stack=ps)
        Yb = self.sb("ns_Yb", (128, 8, 512), BF16, stack=ps)
        evA = self.sb("ns_evA", (128, 2, 1024), BF16, stack=ps)
        mst = self.sb("ns_mst", (128, 8, 512), F32, stack=ps)
        sq_s = self.sb("ns_sq", (128, 4096), F32, stack=ps)
        sc = self.sb("ns_sc", (128, 8), F32, stack=ps)
        ones = self.sb("ns_ones", (128, 1), BF16, stack=ps)
        ones_row = self.sb("ns_onesr", (1, 128), BF16, stack=ps)
        scb = self.sb("ns_scb", (128, 4), BF16, stack=ps)
        inv = self.sb("ns_inv", (128, 4), F32, stack=ps)
        psA = self.psum("ns_psA", (128, 2, 1024), stack=ps)
        psY = self.psum("ns_psY", (128, 2, 512), stack=ps)
        psZ = self.psum("ns_psZ", (128, 2, 512), stack=ps)

        P_RS = ("gG_rs", "gV_rs", "gP_rs")
        P_M = ("G_m", "V_m", "PT_m")
        P_WK = ("G_work", "V_work", "PT_work")
        P_SH = ("gate_sh", "val_sh", "proj_sh")

        b0 = dict(T=cT.n, V=cV.n, A=cA.n, dY=dY.n, dS=dS.n, dG=dG.n, C=cC.n)

        xb_ld = [b0["dY"] + 128 * (pi + 1) for pi in range(3)]

        @block.sync
        def _(y):
            y.wait_ge(cC.sem, b0["C"])
            for pi in range(3):
                for j in range(8):
                    ins = y.dma_start(out=Xb[:, pi, j, :],
                                      in_=self.ap(P_RS[pi])[:, j * 128:(j + 1) * 128],
                                      transpose=True)
                    dY.inc(ins, 16)

        @block.scalar
        def _(a):
            for pi in range(3):
                a.wait_ge(dY.sem, xb_ld[pi])
                a.activation(out=sq_s[:], in_=Xb[:, pi].rearrange("p a b -> p (a b)"),
                             func=AF.Square, accum_out=sc[:, pi:pi + 1])
                cA.inc(a.drain())

        @block.vector
        def _(v):
            v.memset(ones[:], 1.0)
            v.memset(ones_row[:], 1.0)
            for pi in range(3):
                v.wait_ge(cA.sem, b0["A"] + pi + 1)
                v.tensor_copy(scb[:, pi:pi + 1], sc[:, pi:pi + 1])
            cV.inc(v.drain())

        @block.tensor
        def _(t_):
            t_.wait_ge(cV.sem, b0["V"] + 1)
            for pi in range(3):
                ins = t_.matmul(psZ[0:1, 0, pi:pi + 1], ones[:], scb[:, pi:pi + 1],
                                start=True, stop=True)
            cT.inc(ins)

        @block.vector
        def _(v):
            v.wait_ge(cT.sem, b0["T"] + 1)
            v.tensor_copy(sc[0:1, 0:3], psZ[0:1, 0, 0:3])
            cV.inc(v.drain())

        @block.gpsimd
        def _(g):
            g.wait_ge(cV.sem, b0["V"] + 2)
            ins = g.dma_start(out=self.ap("norm_part"), in_=sc[0:1, 0:3])
            dG.inc(ins, 16)
            g.wait_ge(dG.sem, dG.n)
            ins = g.collective_compute(
                "AllReduce", ALU.add, replica_groups=RG,
                ins=[self.ap("norm_part").opt()], outs=[self.ap("norm_out").opt()])
            cC.inc(ins)
            g.wait_ge(cC.sem, cC.n)
            ins = g.dma_start(out=sc[0:1, 4:7], in_=self.ap("norm_out"))
            dG.inc(ins, 16)
        nrm_ld = dG.n

        @block.scalar
        def _(a):
            a.wait_ge(dG.sem, nrm_ld)
            a.activation(out=sc[0:1, 0:3], in_=sc[0:1, 4:7], func=AF.Sqrt)
            cA.inc(a.drain())

        @block.vector
        def _(v):
            v.wait_ge(cA.sem, cA.n)
            v.tensor_scalar(sc[0:1, 0:3], sc[0:1, 0:3], 1e-7, None, ALU.add)
            v.drain()
            v.reciprocal(sc[0:1, 0:3], sc[0:1, 0:3])
            v.drain()
            v.tensor_copy(scb[0:1, 0:3], sc[0:1, 0:3])
            cV.inc(v.drain())

        @block.tensor
        def _(t_):
            t_.wait_ge(cV.sem, cV.n)
            ins = t_.matmul(psZ[:, 0, 0:3], ones_row[0:1, :], scb[0:1, 0:3],
                            start=True, stop=True)
            cT.inc(ins)

        @block.vector
        def _(v):
            v.wait_ge(cT.sem, cT.n)
            v.tensor_copy(inv[:, 0:3], psZ[:, 0, 0:3])
            v.drain()
            for pi in range(3):
                v.tensor_scalar(X[:, pi].rearrange("p a b -> p (a b)"),
                                Xb[:, pi].rearrange("p a b -> p (a b)"),
                                inv[:, pi:pi + 1], None, ALU.mult)
            cV.inc(v.drain())

        nb = dict(T=cT.n, V=cV.n, dY=dY.n, dG=dG.n, C=cC.n)
        CV_R, CT_R, DY_R, DG_R = 25, 24, 80, 144
        NR = 3 * NS_ITERS

        def vb(r):
            return nb["V"] + r * CV_R

        def tb(r):
            return nb["T"] + r * CT_R

        @block.vector
        def _(v):
            for r in range(NR):
                pi = r // NS_ITERS
                if r >= 1:
                    v.wait_ge(cT.sem, tb(r - 1) + 16)
                    v.wait_ge(dG.sem, nb["dG"] + (r - 1) * DG_R + 16)
                v.tensor_copy(Xb[:, pi].rearrange("p a b -> p (a b)"),
                              X[:, pi].rearrange("p a b -> p (a b)"))
                cV.inc(v.drain())
                for m in range(8):
                    v.wait_ge(cT.sem, tb(r) + m + 1)
                    v.tensor_copy(evA[:, m % 2, :], psA[:, m % 2, :])
                    cV.inc(v.drain())
                for m in range(8):
                    v.wait_ge(cT.sem, tb(r) + 8 + m + 1)
                    v.tensor_copy(Y32[:, m, :], psY[:, m % 2, :])
                    v.tensor_copy(Yb[:, m, :], psY[:, m % 2, :])
                    cV.inc(v.drain())
                for m in range(8):
                    v.wait_ge(cT.sem, tb(r) + 16 + m + 1)
                    v.tensor_scalar(X[:, pi, m, :], X[:, pi, m, :], NS_A, None, ALU.mult)
                    v.drain()
                    v.scalar_tensor_tensor(X[:, pi, m, :], Y32[:, m, :], NS_B,
                                           X[:, pi, m, :], ALU.mult, ALU.add)
                    v.drain()
                    v.scalar_tensor_tensor(X[:, pi, m, :], psZ[:, m % 2, :], NS_C,
                                           X[:, pi, m, :], ALU.mult, ALU.add)
                    cV.inc(v.drain())

        @block.gpsimd
        def _(g):
            for r in range(NR):
                pi = r // NS_ITERS
                g.wait_ge(cV.sem, vb(r) + 1)
                ins = g.dma_start(
                    out=self.ap(f"nsXsp_{pi}").rearrange("(k p) n -> p k n", p=128),
                    in_=Xb[:, pi].rearrange("p a b -> p a b"))
                dG.inc(ins, 16)
                for m in range(8):
                    g.wait_ge(cV.sem, vb(r) + 1 + m + 1)
                    ins = g.dma_start(
                        out=self.ap(f"nsAp_{pi}")[m * 128:(m + 1) * 128, :]
                        .rearrange("(o p) f -> p o f", p=128)[:, 0, :],
                        in_=evA[:, m % 2, :])
                    dG.inc(ins, 16)
                g.wait_ge(dG.sem, nb["dG"] + r * DG_R + DG_R)
                ins = g.collective_compute(
                    "AllReduce", ALU.add, replica_groups=RG,
                    ins=[self.ap(f"nsAp_{pi}").opt()],
                    outs=[self.ap(f"nsA_{pi}").opt()])
                cC.inc(ins)

        @block.sync
        def _(y):
            for r in range(NR):
                pi = r // NS_ITERS
                y.wait_ge(dG.sem, nb["dG"] + r * DG_R + 16)
                for c in range(4):
                    ins = y.dma_start(
                        out=XT[:, c, :],
                        in_=self.ap(f"nsXsp_{pi}")[:, c * 128:(c + 1) * 128],
                        transpose=True)
                    dY.inc(ins, 16)
                y.wait_ge(cC.sem, nb["C"] + r + 1)
                ins = y.dma_start(out=Asb[:],
                                  in_=self.ap(f"nsA_{pi}").rearrange(
                                      "(k p) n -> p k n", p=128))
                dY.inc(ins, 16)

        @block.tensor
        def _(t_):
            for r in range(NR):
                pi = r // NS_ITERS
                t_.wait_ge(dY.sem, nb["dY"] + r * DY_R + 64)
                for m in range(8):
                    if m >= 2:
                        t_.wait_ge(cV.sem, vb(r) + 1 + m - 1)
                    for kc in range(4):
                        lhsT = XT[:, kc, m * 128:(m + 1) * 128]
                        t_.matmul(psA[:, m % 2, 0:512], lhsT, XT[:, kc, 0:512],
                                  start=(kc == 0), stop=(kc == 3))
                        ins = t_.matmul(psA[:, m % 2, 512:1024], lhsT,
                                        XT[:, kc, 512:1024],
                                        start=(kc == 0), stop=(kc == 3))
                    cT.inc(ins)
                t_.wait_ge(dY.sem, nb["dY"] + r * DY_R + 80)
                for m in range(8):
                    if m >= 2:
                        t_.wait_ge(cV.sem, vb(r) + 9 + m - 1)
                    for k in range(8):
                        ins = t_.matmul(psY[:, m % 2, :],
                                        Asb[:, k, m * 128:(m + 1) * 128],
                                        Xb[:, pi, k, :], start=(k == 0), stop=(k == 7))
                    cT.inc(ins)
                t_.wait_ge(cV.sem, vb(r) + 17)
                for m in range(8):
                    if m >= 2:
                        t_.wait_ge(cV.sem, vb(r) + 17 + m - 1)
                    for k in range(8):
                        ins = t_.matmul(psZ[:, m % 2, :],
                                        Asb[:, k, m * 128:(m + 1) * 128],
                                        Yb[:, k, :], start=(k == 0), stop=(k == 7))
                    cT.inc(ins)

        ib = dict(T=cT.n, V=cV.n, dY=dY.n, dG=dG.n, C=cC.n)
        self.expect(cT, nb["T"] + NR * CT_R, "NS T")
        self.expect(cV, nb["V"] + NR * CV_R, "NS V")

        @block.sync
        def _(y):
            for pi in range(3):
                srcn = P_SH[pi] if step == 0 else P_M[pi]
                y.wait_ge(cV.sem, ib["V"] + pi)
                if pi >= 1:
                    y.wait_ge(dG.sem, ib["dG"] + 32 * (pi - 1) + 16)
                ins = y.dma_start(out=mst[:],
                                  in_=self.ap(srcn).rearrange("(k p) n -> p k n", p=128))
                dY.inc(ins, 16)

        @block.vector
        def _(v):
            for pi in range(3):
                v.wait_ge(dY.sem, ib["dY"] + 16 * (pi + 1))
                v.scalar_tensor_tensor(mst[:].rearrange("p a b -> p (a b)"),
                                       X[:, pi].rearrange("p a b -> p (a b)"), -LR,
                                       mst[:].rearrange("p a b -> p (a b)"),
                                       ALU.mult, ALU.add)
                v.drain()
                v.tensor_copy(Xb[:, pi].rearrange("p a b -> p (a b)"),
                              mst[:].rearrange("p a b -> p (a b)"))
                cV.inc(v.drain())

        @block.gpsimd
        def _(g):
            for pi in range(3):
                g.wait_ge(cV.sem, ib["V"] + pi + 1)
                ins = g.dma_start(out=self.ap(P_M[pi]).rearrange("(k p) n -> p k n", p=128),
                                  in_=mst[:])
                dG.inc(ins, 16)
                ins = g.dma_start(out=self.ap(f"nsXsp_{pi}").rearrange(
                    "(k p) n -> p k n", p=128), in_=Xb[:, pi].rearrange("p a b -> p a b"))
                dG.inc(ins, 16)
                g.wait_ge(dG.sem, dG.n)
                ins = g.collective_compute(
                    "AllGather", ALU.bypass, replica_groups=RG,
                    ins=[self.ap(f"nsXsp_{pi}").opt()],
                    outs=[self.ap(P_WK[pi]).rearrange("b k n -> (b k) n").opt()])
                cC.inc(ins)
            g.wait_ge(cC.sem, cC.n)

    def ttt_O(self, block, ps):
        dY, dS, dG = self.dY, self.dS, self.dG
        cT, cV, cA = self.cT, self.cV, self.cA

        pqT = self.sb("o_pqT", (128, 8, NTOK), BF16, stack=ps)
        pw = self.sb("o_pw", (128, 8, DIM), BF16, stack=ps)
        pjb = self.sb("o_pjb", (128, DIM), F32, stack=ps)
        xin = self.sb("o_x", (128, 2, DIM), F32, stack=ps)
        ob = self.sb("o_ob", (128, 2, DIM), F32, stack=ps)
        sq = self.sb("o_sq", (128, DIM), F32, stack=ps)
        stat = self.sb("o_stat", (128, 2, 16), F32, stack=ps)
        xco = self.sb("o_xco", (128, 2, DIM), F32, stack=ps)
        lno = self.sb("o_lno", (128, 2, DIM), F32, stack=ps)
        x1b = self.sb("o_x1", (128, 2, DIM), F32, stack=ps)
        xc1 = self.sb("o_xc1", (128, 2, DIM), F32, stack=ps)
        ln1t = self.sb("o_ln1", (128, 2, DIM), BF16, stack=ps)
        po = self.psum("o_po", (128, 2, DIM), stack=ps)

        t0 = dict(T=cT.n, V=cV.n, A=cA.n, dY=dY.n, dS=dS.n, dG=dG.n)
        pq_ld = []
        x_ld = []
        pw_ld = []

        @block.sync
        def _(y):
            for k in range(8):
                ins = y.dma_start(out=pqT[:, k, :],
                                  in_=self.ap("predq_tok")[:, k * 128:(k + 1) * 128],
                                  transpose=True)
                pq_ld.append(dY.inc(ins, 16))
            xv = self.tm("x")
            for t in range(NT):
                if t >= 2:
                    y.wait_ge(cV.sem, t0["V"] + 2 * (t - 1))
                ins = y.dma_start(out=xin[:, t % 2, :], in_=xv[:, t, :])
                x_ld.append(dY.inc(ins, 16))

        @block.gpsimd
        def _(g):
            ins = g.dma_start(out=pw[:],
                              in_=self.ap("proj_w").rearrange("(k p) n -> p k n", p=128))
            dG.inc(ins, 16)
            ins = g.dma_start(out=pjb[:], in_=self.ap("proj_b"))
            pw_ld.append(dG.inc(ins, 16))
            for t in range(NT):
                g.wait_ge(cV.sem, t0["V"] + 2 * t + 2)
                ins = g.dma_start(out=self.tm("x1_tok")[:, t, :], in_=x1b[:, t % 2, :])
                dG.inc(ins, 16)

        @block.tensor
        def _(t_):
            t_.wait_ge(dY.sem, pq_ld[-1])
            t_.wait_ge(dG.sem, pw_ld[0])
            for t in range(NT):
                if t >= 1:
                    t_.wait_ge(cV.sem, t0["V"] + 2 * t - 1)
                for k in range(8):
                    lhsT = pqT[:, k, t * 128:(t + 1) * 128]
                    t_.matmul(po[:, t % 2, 0:512], lhsT, pw[:, k, 0:512],
                              start=(k == 0), stop=(k == 7))
                    ins = t_.matmul(po[:, t % 2, 512:1024], lhsT, pw[:, k, 512:1024],
                                    start=(k == 0), stop=(k == 7))
                cT.inc(ins)

        @block.vector
        def _(v):
            for t in range(NT):
                v.wait_ge(cT.sem, t0["T"] + t + 1)
                if t >= 2:
                    v.wait_ge(cA.sem, t0["A"] + 2 * t - 3)
                st = stat[:, t % 2, :]
                v.tensor_tensor(ob[:, t % 2, :], po[:, t % 2, :], pjb[:], ALU.add)
                v.drain()
                v.tensor_tensor(sq[:], ob[:, t % 2, :], ob[:, t % 2, :], ALU.mult)
                v.reduce_sum(st[:, 0:1], ob[:, t % 2, :], axis=AX)
                v.drain()
                v.reduce_sum(st[:, 1:2], sq[:], axis=AX)
                v.tensor_scalar(st[:, 2:3], st[:, 0:1], 1.0 / DIM, None, ALU.mult)
                v.drain()
                v.tensor_scalar(st[:, 3:4], st[:, 1:2], 1.0 / DIM, None, ALU.mult)
                v.tensor_tensor(st[:, 4:5], st[:, 2:3], st[:, 2:3], ALU.mult)
                v.tensor_scalar(xco[:, t % 2, :], ob[:, t % 2, :], st[:, 2:3],
                                None, ALU.subtract)
                v.drain()
                v.tensor_tensor(st[:, 5:6], st[:, 3:4], st[:, 4:5], ALU.subtract)
                cV.inc(v.drain())
                v.wait_ge(cA.sem, t0["A"] + 2 * t + 1)
                v.wait_ge(dY.sem, x_ld[t])
                if t >= 2:
                    v.wait_ge(dG.sem, t0["dG"] + 32 + 16 * (t - 1))
                v.tensor_tensor(x1b[:, t % 2, :], xin[:, t % 2, :],
                                lno[:, t % 2, :], ALU.add)
                v.drain()
                v.tensor_tensor(sq[:], x1b[:, t % 2, :], x1b[:, t % 2, :], ALU.mult)
                v.reduce_sum(st[:, 8:9], x1b[:, t % 2, :], axis=AX)
                v.drain()
                v.reduce_sum(st[:, 9:10], sq[:], axis=AX)
                v.tensor_scalar(st[:, 10:11], st[:, 8:9], 1.0 / DIM, None, ALU.mult)
                v.drain()
                v.tensor_scalar(st[:, 11:12], st[:, 9:10], 1.0 / DIM, None, ALU.mult)
                v.tensor_tensor(st[:, 12:13], st[:, 10:11], st[:, 10:11], ALU.mult)
                v.tensor_scalar(xc1[:, t % 2, :], x1b[:, t % 2, :], st[:, 10:11],
                                None, ALU.subtract)
                v.drain()
                v.tensor_tensor(st[:, 13:14], st[:, 11:12], st[:, 12:13], ALU.subtract)
                cV.inc(v.drain())

        @block.scalar
        def _(a):
            for t in range(NT):
                a.wait_ge(cV.sem, t0["V"] + 2 * t + 1)
                if t >= 2:
                    a.wait_ge(dS.sem, t0["dS"] + 16 * (t - 1))
                st = stat[:, t % 2, :]
                a.activation(out=st[:, 6:7], in_=st[:, 5:6], func=AF.Ln,
                             bias=self.konst[:, 0:1])
                a.drain()
                a.activation(out=st[:, 7:8], in_=st[:, 6:7], func=AF.Exp, scale=-0.5)
                a.drain()
                a.activation(out=lno[:, t % 2, :], in_=xco[:, t % 2, :],
                             func=AF.Copy, scale=st[:, 7:8])
                cA.inc(a.drain())
                a.wait_ge(cV.sem, t0["V"] + 2 * t + 2)
                a.activation(out=st[:, 14:15], in_=st[:, 13:14], func=AF.Ln,
                             bias=self.konst[:, 0:1])
                a.drain()
                a.activation(out=st[:, 15:16], in_=st[:, 14:15], func=AF.Exp, scale=-0.5)
                a.drain()
                a.activation(out=ln1t[:, t % 2, :], in_=xc1[:, t % 2, :],
                             func=AF.Copy, scale=st[:, 15:16])
                cA.inc(a.drain())
                ins = a.dma_start(out=self.tm("ln1_tok")[:, t, :], in_=ln1t[:, t % 2, :])
                dS.inc(ins, 16)
        self.expect(cT, t0["T"] + NT, "O T")
        self.expect(cV, t0["V"] + 2 * NT, "O V")
        self.expect(cA, t0["A"] + 2 * NT, "O A")

    def ttt_L(self, block, ps):
        dY, dS, dG = self.dY, self.dS, self.dG
        cT, cV, cA, cC = self.cT, self.cV, self.cA, self.cC

        ln1T = self.sb("l_ln1T", (128, 8, NTOK), BF16, stack=ps)
        gq = self.sb("l_gq", (128, 8, 16), BF16, stack=ps)
        lt = self.sb("l_lt", (1, 1), F32, stack=ps)
        ltb = self.sb("l_ltb", (16, 1), F32, stack=ps)
        Lsb = self.sb("l_L", (16, NTOK), F32, stack=ps)
        pL = self.psum("l_pL", (16, 2, 512), stack=ps)
        pB = self.psum("l_pB", (16, 1), stack=ps)
        onesr = self.sb("l_onesr", (1, 16), BF16, stack=ps)
        ltb_b = self.sb("l_ltbb", (1, 1), BF16, stack=ps)

        t0 = dict(T=cT.n, V=cV.n, A=cA.n, dY=dY.n, dS=dS.n, dG=dG.n)
        ld = []

        @block.sync
        def _(y):
            for k in range(8):
                ins = y.dma_start(out=ln1T[:, k, :],
                                  in_=self.ap("ln1_tok")[:, k * 128:(k + 1) * 128],
                                  transpose=True)
                ld.append(dY.inc(ins, 16))

        @block.gpsimd
        def _(g):
            ins = g.dma_start(out=gq[:],
                              in_=self.ap("gate_queryT").rearrange("(k p) n -> p k n", p=128))
            dG.inc(ins, 16)
            ins = g.dma_start(out=lt[:], in_=self.ap("log_temp"))
            ld.append(dG.inc(ins, 16))

        @block.scalar
        def _(a):
            a.wait_ge(dG.sem, ld[-1])
            a.activation(out=lt[:], in_=lt[:], func=AF.Exp, scale=-1.0)
            cA.inc(a.drain())

        @block.vector
        def _(v):
            v.memset(onesr[:], 1.0)
            v.wait_ge(cA.sem, t0["A"] + 1)
            v.tensor_copy(ltb_b[:], lt[:])
            cV.inc(v.drain())

        @block.tensor
        def _(t_):
            v0 = t0["V"] + 1
            t_.wait_ge(cV.sem, v0)
            ins = t_.matmul(pB[:], onesr[0:1, :], ltb_b[0:1, :], start=True, stop=True)
            cT.inc(ins)
            t_.wait_ge(dY.sem, t0["dY"] + 128)
            t_.wait_ge(dG.sem, t0["dG"] + 16)
            for n in range(8):
                if n >= 2:
                    t_.wait_ge(cV.sem, v0 + n - 1)
                for k in range(8):
                    ins = t_.matmul(pL[:, n % 2, :], gq[:, k, :],
                                    ln1T[:, k, n * 512:(n + 1) * 512],
                                    start=(k == 0), stop=(k == 7))
                cT.inc(ins)

        @block.vector
        def _(v):
            v.wait_ge(cT.sem, t0["T"] + 1)
            v.tensor_copy(ltb[:], pB[:])
            cV.inc(v.drain())
            for n in range(8):
                v.wait_ge(cT.sem, t0["T"] + n + 2)
                v.tensor_scalar(Lsb[:, n * 512:(n + 1) * 512], pL[:, n % 2, :],
                                ltb[:, 0:1], None, ALU.mult)
                cV.inc(v.drain())

        @block.gpsimd
        def _(g):
            g.wait_ge(cV.sem, cV.n)
            ins = g.dma_start(out=self.ap("logitsT_own"), in_=Lsb[:])
            dG.inc(ins, 16)
            g.wait_ge(dG.sem, dG.n)
            ins = g.collective_compute(
                "AllGather", ALU.bypass, replica_groups=RG,
                ins=[self.ap("ln1_tok").opt()], outs=[self.ap("ln1_all").opt()])
            cC.inc(ins)
            ins = g.collective_compute(
                "AllGather", ALU.bypass, replica_groups=RG,
                ins=[self.ap("logitsT_own").opt()], outs=[self.ap("logitsT_all").opt()])
            cC.inc(ins)
            g.wait_ge(cC.sem, cC.n)

    def seq(self, block, engname, ch, fn):
        chans = [self.cT, self.cV, self.cA, self.cG, self.dY, self.dS, self.dG, self.cC]
        vals = [c.n for c in chans]

        def body(eng):
            for c, v in zip(chans, vals):
                if v > 0:
                    eng.wait_ge(c.sem, v)
            fn(eng)
            if ch is not None:
                ch.inc(eng.drain())
        getattr(block, engname)(body)

    def ffn_R(self, block, ps):
        dY, dS, dG = self.dY, self.dS, self.dG
        cT, cV, cA, cG, cC = self.cT, self.cV, self.cA, self.cG, self.cC
        N2 = 8192

        sink = self.sb("r_sink", (64, N2), F32, stack=ps)
        rank = self.sb("r_rank", (64, N2), F32, stack=ps)
        sm = self.sb("r_sm", (64, 24), F32, stack=ps)
        bmap = self.sb("r_bmap", (64, 4), F32, stack=ps)
        bmapT = self.sb("r_bmapT", (4, 64), F32, stack=ps)
        onesel = self.sb("r_onesel", (64, 8), F32, stack=ps)
        i8t = self.sb("r_i8t", (64, 8), F32, stack=ps)
        tstg = self.sb("r_tstg", (128, 64, 16), F32, stack=ps)
        tbhi = self.sb("r_tbhi", (128, 64, 8), BF16, stack=ps)
        tblo = self.sb("r_tblo", (128, 64, 8), BF16, stack=ps)
        tres = self.sb("r_tres", (128, 64, 8), F32, stack=ps)
        iota_blk = self.sb("r_iblk", (128, 64), BF16, stack=ps)
        iota_pc = self.sb("r_ipc", (128, 64), BF16, stack=ps)
        excol = self.sb("r_excol", (128, 16), F32, stack=ps)
        gcol = self.sb("r_gcol", (128, 16), F32, stack=ps)
        excb = self.sb("r_excb", (128, 16), U32, stack=ps)
        cstt = self.sb("r_cst", (1, 8), F32, stack=ps)
        ccb = self.sb("r_ccb", (1, 1), BF16, stack=ps)
        smu = self.sb("r_smu", (64, 2), U32, stack=ps)
        xev = self.sb("r_xev", (128, 4), F32, stack=ps)
        onesr64 = self.sb("r_o64", (1, 64), BF16, stack=ps)
        ps8 = self.psum("r_ps8", (64, 4, 512), stack=ps)
        psT = self.psum("r_psT", (128, 8, 16), stack=ps)
        psX = self.psum("r_psX", (128, 2, 4), stack=ps)
        scope1 = ExitStack()
        exps = self.sb("r_exps", (64, N2), F32, stack=scope1)
        lse4 = self.sb("r_lse4", (4, N2), F32, stack=scope1)

        S = lambda e, c, f: self.seq(block, e, c, f)

        def st_(y):
            ag3 = self.ap("logitsT_all").rearrange("(c e) t -> c e t", e=16)
            for e in range(16):
                for half in range(2):
                    dY.inc(y.dma_start(
                        out=sink[e * 4:(e + 1) * 4, half * 4096:(half + 1) * 4096],
                        in_=ag3[half::2, e]), 16)
            dY.inc(y.dma_start(out=cstt[:], in_=self.ap("cconst")), 16)
        S("sync", None, st_)

        def g_iota(g):
            g.iota(sm[:, 0:1], pattern=[[0, 1]], base=0, channel_multiplier=1,
                   allow_small_or_imprecise_dtypes=True)
            g.iota(i8t[:], pattern=[[1, 8]], base=0, channel_multiplier=0,
                   allow_small_or_imprecise_dtypes=True)
            g.iota(iota_blk[:], pattern=[[1, 64]], base=0, channel_multiplier=0,
                   allow_small_or_imprecise_dtypes=True)
            g.iota(iota_pc[:], pattern=[[0, 64]], base=0, channel_multiplier=1,
                   allow_small_or_imprecise_dtypes=True)
        S("gpsimd", cG, g_iota)

        def g_bT(g):
            g.iota(lse4[0:4, 0:64], pattern=[[0, 16], [1, 4]], base=0,
                   channel_multiplier=0, allow_small_or_imprecise_dtypes=True)
            g.iota(lse4[0:4, 64:65], pattern=[[0, 1]], base=0,
                   channel_multiplier=1, allow_small_or_imprecise_dtypes=True)
        S("gpsimd", cG, g_bT)

        def v_bT(v):
            v.memset(onesr64[:], 1.0)
            v.tensor_scalar(bmapT[:], lse4[0:4, 0:64], lse4[0:4, 64:65], None,
                            ALU.is_equal)
        S("vector", cV, v_bT)
        S("tensor", cT, lambda t_: t_.transpose(
            ps8[0:64, 0, 0:4], bmapT[:], self.identf[0:4, 0:4]))
        S("vector", cV, lambda v: v.tensor_copy(bmap[:], ps8[0:64, 0, 0:4]))

        for it in range(SINK_IT):
            S("scalar", cA, lambda a: a.activation(out=exps[:], in_=sink[:], func=AF.Exp))
            for qq in range(4):
                def t_es(t_, qq=qq):
                    for c in range(4):
                        t_.matmul(ps8[0:4, c, :], bmap[:],
                                  exps[:, (qq * 4 + c) * 512:(qq * 4 + c + 1) * 512],
                                  start=True, stop=True)
                S("tensor", cT, t_es)

                def v_es(v, qq=qq):
                    v.tensor_copy(lse4[:, qq * 2048:(qq + 1) * 2048],
                                  ps8[0:4, :, :].rearrange("p a b -> p (a b)"))
                S("vector", cV, v_es)
            S("scalar", cA, lambda a: a.activation(out=lse4[:], in_=lse4[:], func=AF.Ln))
            for qq in range(4):
                def t_bc(t_, qq=qq):
                    for c in range(4):
                        t_.matmul(ps8[:, c, :], bmapT[:],
                                  lse4[:, (qq * 4 + c) * 512:(qq * 4 + c + 1) * 512],
                                  start=True, stop=True)
                S("tensor", cT, t_bc)

                def v_bc(v, qq=qq):
                    v.tensor_tensor(sink[:, qq * 2048:(qq + 1) * 2048],
                                    sink[:, qq * 2048:(qq + 1) * 2048],
                                    ps8[:, :, :].rearrange("p a b -> p (a b)"),
                                    ALU.subtract)
                S("vector", cV, v_bc)

            def v_mx(v):
                v.reduce_max(sm[:, 2:3], sink[:], axis=AX)
                v.drain()
                v.tensor_scalar(sm[:, 3:4], sm[:, 2:3], -1.0, None, ALU.mult)
            S("vector", cV, v_mx)

            def a_nl(a):
                a.activation(out=exps[:], in_=sink[:], func=AF.Exp,
                             bias=sm[:, 3:4], accum_out=sm[:, 4:5])
                a.drain()
                a.activation(out=sm[:, 5:6], in_=sm[:, 4:5], func=AF.Ln)
            S("scalar", cA, a_nl)

            def v_sub(v):
                v.tensor_tensor(sm[:, 6:7], sm[:, 5:6], sm[:, 2:3], ALU.add)
                v.drain()
                v.tensor_scalar(sink[:], sink[:], sm[:, 6:7], None, ALU.subtract)
            S("vector", cV, v_sub)

        def v_bis(v):
            v.memset(sm[:, 8:9], -40.0)
            v.memset(sm[:, 9:10], 1e-3)
            v.drain()
            for _ in range(BIS_IT):
                v.tensor_tensor(sm[:, 10:11], sm[:, 8:9], sm[:, 9:10], ALU.add)
                v.drain()
                v.tensor_scalar(sm[:, 11:12], sm[:, 10:11], 0.5, None, ALU.mult)
                v.drain()
                v.tensor_scalar(exps[:], sink[:], sm[:, 11:12], 0.0,
                                ALU.is_gt, ALU.add, accum_out=sm[:, 12:13])
                v.drain()
                v.tensor_scalar(smu[:, 0:1], sm[:, 12:13], float(CAP), None, ALU.is_ge)
                v.tensor_scalar(smu[:, 1:2], sm[:, 12:13], float(CAP), None, ALU.is_lt)
                v.drain()
                v.select(sm[:, 8:9], smu[:, 0:1], sm[:, 11:12], sm[:, 8:9])
                v.select(sm[:, 9:10], smu[:, 1:2], sm[:, 11:12], sm[:, 9:10])
                v.drain()
        S("vector", cV, v_bis)

        def v_rank(v):
            v.tensor_scalar(exps[:], sink[:], sm[:, 8:9], None, ALU.is_gt)
            v.drain()
            v.tensor_tensor_scan(rank[:], exps[:], exps[:], 0.0, ALU.add, ALU.bypass)
            v.drain()
            v.tensor_tensor(rank[:], rank[:], exps[:], ALU.mult)
        S("vector", cV, v_rank)

        scope1.close()
        scope2 = ExitStack()
        town = self.sb("r_town", (8, 2 * N2), F32, stack=scope2)
        def v_csel(v):
            v.tensor_copy(ccb[:], cstt[0:1, 2:3])
        S("vector", cV, v_csel)
        S("tensor", cT, lambda t_: t_.matmul(ps8[:, 0, 0:1], onesr64[0:1, 0:64],
                                             ccb[0:1, :], start=True, stop=True))

        def v_oh(v):
            v.tensor_copy(sm[:, 14:15], ps8[0:64, 0, 0:1])
            v.drain()
            v.scalar_tensor_tensor(sm[:, 15:16], sm[:, 14:15], -8.0, sm[:, 0:1],
                                   ALU.mult, ALU.add)
            v.drain()
            v.tensor_scalar(onesel[:], i8t[:], sm[:, 15:16], None, ALU.is_equal)
        S("vector", cV, v_oh)

        for src_t, off in ((sink, 0), (rank, N2)):
            for qq in range(4):
                def t_ex(t_, src_t=src_t, qq=qq):
                    for c in range(4):
                        t_.matmul(ps8[0:8, c, :], onesel[:],
                                  src_t[:, (qq * 4 + c) * 512:(qq * 4 + c + 1) * 512],
                                  start=True, stop=True)
                S("tensor", cT, t_ex)

                def v_ex(v, off=off, qq=qq):
                    v.tensor_copy(town[0:8, off + qq * 2048:off + (qq + 1) * 2048],
                                  ps8[0:8, :, :].rearrange("p a b -> p (a b)"))
                S("vector", cV, v_ex)

        for grp in range(8):
            def t_tr(t_, grp=grp):
                for j in range(8):
                    blk = grp * 8 + j
                    t_.transpose(psT[:, j, 0:8], town[0:8, blk * 128:(blk + 1) * 128],
                                 self.identf[0:8, 0:8])
                    t_.transpose(psT[:, j, 8:16],
                                 town[0:8, N2 + blk * 128:N2 + (blk + 1) * 128],
                                 self.identf[0:8, 0:8])
            S("tensor", cT, t_tr)

            def v_tr(v, grp=grp):
                v.tensor_copy(tstg[:, grp * 8:(grp + 1) * 8, :], psT[:])
            S("vector", cV, v_tr)

        scope2.close()
        selT = self.sb("r_selT", (128, 64, 256), BF16, stack=ps)
        rhs2 = self.sb("r_rhs2", (128, 64, 4), BF16, stack=ps)

        def v_split(v):
            t8 = tstg[:, :, 0:8]
            v.tensor_copy(tbhi[:], t8)
            v.drain()
            v.tensor_tensor(tres[:], t8, tbhi[:], ALU.subtract)
            v.drain()
            v.tensor_copy(tblo[:], tres[:])
            v.drain()
            v.tensor_copy(rhs2[:, :, 0:1], iota_blk[:].rearrange("p (a o) -> p a o", o=1))
            v.tensor_copy(rhs2[:, :, 1:2], iota_pc[:].rearrange("p (a o) -> p a o", o=1))
        S("vector", cV, v_split)

        for r in range(8):
            def v_sel(v, r=r):
                rk = tstg[:, :, 8 + r:9 + r]
                for blk in range(64):
                    v.tensor_scalar(selT[:, blk, :], self.iota_cap[:],
                                    rk[:, blk, :], None, ALU.is_equal)
                v.drain()
                v.tensor_copy(rhs2[:, :, 2:3], tbhi[:, :, r:r + 1])
                v.tensor_copy(rhs2[:, :, 3:4], tblo[:, :, r:r + 1])
            S("vector", cV, v_sel)

            def t_idx(t_, r=r):
                for hh in range(2):
                    for blk in range(64):
                        t_.matmul(psX[:, hh, :], selT[:, blk, hh * 128:(hh + 1) * 128],
                                  rhs2[:, blk, :], start=(blk == 0), stop=(blk == 63))
            S("tensor", cT, t_idx)

            def v_fin(v, r=r):
                for hh in range(2):
                    c_ = r * 2 + hh
                    v.tensor_copy(xev[:], psX[:, hh, 0:4])
                    v.drain()
                    q_ = xev[:]
                    v.scalar_tensor_tensor(excol[:, c_:c_ + 1], q_[:, 0:1], 128.0,
                                           q_[:, 1:2], ALU.mult, ALU.add)
                    v.tensor_tensor(gcol[:, c_:c_ + 1], q_[:, 2:3],
                                    q_[:, 3:4], ALU.add)
                    v.drain()
                v.tensor_copy(excb[:, r * 2:r * 2 + 2], excol[:, r * 2:r * 2 + 2])
            S("vector", cV, v_fin)

        S("scalar", cA, lambda a: a.activation(out=gcol[:], in_=gcol[:], func=AF.Exp))

        def g_st(g):
            for r in range(8):
                for hh in range(2):
                    c_ = r * 2 + hh
                    dG.inc(g.dma_start(out=self.ap("idx_own")[r, hh * 128:(hh + 1) * 128],
                                       in_=excb[:, c_:c_ + 1]), 16)
                    dG.inc(g.dma_start(out=self.ap("gates_own")[r, hh * 128:(hh + 1) * 128],
                                       in_=gcol[:, c_:c_ + 1]), 16)
        S("gpsimd", None, g_st)

        if "sink_dump" in self.dbg:
            def g_dmp(g):
                dG.inc(g.dma_start(out=self.ap("sink_dump"), in_=sink[:]), 16)
                dG.inc(g.dma_start(out=self.ap("rank_dump"), in_=rank[:]), 16)
            S("gpsimd", None, g_dmp)

        def g_ag(g):
            cC.inc(g.collective_compute(
                "AllGather", ALU.bypass, replica_groups=RG,
                ins=[self.ap("idx_own").opt()],
                outs=[self.ap("idx_all")[0:64, :].opt()]))
        S("gpsimd", None, g_ag)
        S("gpsimd", None, lambda g: g.wait_ge(cC.sem, cC.n))

    def ffn_E(self, block, ps):
        dY, dS, dG = self.dY, self.dS, self.dG
        cT, cV, cA, cG, cC = self.cT, self.cV, self.cA, self.cG, self.cC

        w1b = self.sb("e_w1", (128, 8, HID), BF16, stack=ps)
        w2b = self.sb("e_w2", (128, 32, DIM), BF16, stack=ps)
        b1s = self.sb("e_b1", (128, HID // 128), F32, stack=ps)
        b2s = self.sb("e_b2", (128, DIM), F32, stack=ps)
        idxs = self.sb("e_idxs", (128, 2), U32, stack=ps)
        idxf = self.sb("e_idxf", (128, 2), F32, stack=ps)
        gts = self.sb("e_gts", (128, 2), F32, stack=ps)
        ei = self.sb("e_ei", (128, 2, DIM), BF16, stack=ps)
        eiT = self.sb("e_eiT", (128, 8, 256), BF16, stack=ps)
        h1T = self.sb("e_h1T", (128, 32, 256), BF16, stack=ps)
        eos = self.sb("e_eos", (128, DIM), F32, stack=ps)
        eob = self.sb("e_eob", (128, 2, DIM), BF16, stack=ps)
        psE1 = self.psum("e_ps1", (128, 4, 256), stack=ps)
        psE2 = self.psum("e_ps2", (128, 2, DIM), stack=ps)
        psTr = self.psum("e_psT", (128, 8, 128), BF16, stack=ps)

        S = lambda e, c, f: self.seq(block, e, c, f)

        for e_l in range(E_OWN):
            def g_w(g, e_l=e_l):
                dG.inc(g.dma_start(out=w1b[:], in_=self.ap("w1")[e_l].rearrange(
                    "(k p) n -> p k n", p=128)), 16)
                dG.inc(g.dma_start(out=w2b[:], in_=self.ap("w2")[e_l].rearrange(
                    "(k p) n -> p k n", p=128)), 16)
                dG.inc(g.dma_start(out=b1s[:], in_=self.ap("b1c")[e_l]), 16)
                dG.inc(g.dma_start(out=b2s[:], in_=self.ap("b2t")[e_l]), 16)
            S("gpsimd", None, g_w)
            for b_ in range(4):
                r = e_l * 4 + b_

                def g_ix(g, r=r):
                    for hh in range(2):
                        dG.inc(g.dma_start(out=idxs[:, hh:hh + 1],
                                           in_=self.ap("idx_own")[r, hh * 128:(hh + 1) * 128]), 16)
                        dG.inc(g.dma_start(out=gts[:, hh:hh + 1],
                                           in_=self.ap("gates_own")[r, hh * 128:(hh + 1) * 128]), 16)
                S("gpsimd", None, g_ix)

                def v_ix(v, b_=b_):
                    v.tensor_copy(idxf[:], idxs[:])
                    v.drain()
                    v.tensor_scalar(idxf[:], idxf[:], float(8192 * b_), None, ALU.add)
                    v.drain()
                    v.tensor_copy(idxs[:], idxf[:])
                S("vector", cV, v_ix)

                def g_ga(g):
                    for hh in range(2):
                        dG.inc(g.indirect_dma_start(
                            out=ei[:, hh, :], out_offset=None,
                            in_=self.ap("ln1_all")[0:128, :],
                            in_offset=IndirectOffsetOnAxis(idxs[:, hh:hh + 1], 0)), 16)
                S("gpsimd", None, g_ga)

                for hh in range(2):
                    def t_tr(t_, hh=hh):
                        for k in range(8):
                            t_.transpose(psTr[:, k, :], ei[:, hh, k * 128:(k + 1) * 128],
                                         self.ident[:])
                    S("tensor", cT, t_tr)

                    def v_tr(v, hh=hh):
                        v.tensor_copy(eiT[:, :, hh * 128:(hh + 1) * 128], psTr[:])
                    S("vector", cV, v_tr)

                t0T, t0A = cT.n, cA.n
                h1chans = [cT, cV, cA, cG, dY, dS, dG, cC]
                h1vals = [c.n for c in h1chans]

                def t_h1(t_):
                    for c, v in zip(h1chans, h1vals):
                        if v > 0:
                            t_.wait_ge(c.sem, v)
                    for g_ in range(16):
                        if g_ >= 2:
                            t_.wait_ge(cA.sem, t0A + g_ - 1)
                        half = (g_ % 2) * 2
                        for mm_ in range(2):
                            m = g_ * 2 + mm_
                            for k in range(8):
                                ins = t_.matmul(psE1[:, half + mm_, :],
                                                w1b[:, k, m * 128:(m + 1) * 128],
                                                eiT[:, k, :],
                                                start=(k == 0), stop=(k == 7))
                        cT.inc(ins)
                block.tensor(t_h1)

                def a_h1(a):
                    for c, v in zip(h1chans, h1vals):
                        if v > 0:
                            a.wait_ge(c.sem, v)
                    for g_ in range(16):
                        a.wait_ge(cT.sem, t0T + g_ + 1)
                        half = (g_ % 2) * 2
                        for mm_ in range(2):
                            m = g_ * 2 + mm_
                            a.activation(out=h1T[:, m, :], in_=psE1[:, half + mm_, :],
                                         func=AF.Gelu_apprx_tanh, bias=b1s[:, m:m + 1])
                        cA.inc(a.drain())
                block.scalar(a_h1)

                def t_eo(t_):
                    for sh in range(2):
                        for k2 in range(32):
                            t_.matmul(psE2[:, sh, 0:512],
                                      h1T[:, k2, sh * 128:(sh + 1) * 128],
                                      w2b[:, k2, 0:512], start=(k2 == 0), stop=(k2 == 31))
                            t_.matmul(psE2[:, sh, 512:1024],
                                      h1T[:, k2, sh * 128:(sh + 1) * 128],
                                      w2b[:, k2, 512:1024], start=(k2 == 0), stop=(k2 == 31))
                S("tensor", cT, t_eo)

                def v_eo(v):
                    for sh in range(2):
                        v.tensor_tensor(eos[:], psE2[:, sh, :], b2s[:], ALU.add)
                        v.drain()
                        v.tensor_scalar(eob[:, sh, :], eos[:], gts[:, sh:sh + 1],
                                        None, ALU.mult)
                        v.drain()
                S("vector", cV, v_eo)

                def g_eo(g, r=r):
                    for sh in range(2):
                        dG.inc(g.dma_start(
                            out=self.ap("eo_own")[r, sh * 128:(sh + 1) * 128, :],
                            in_=eob[:, sh, :]), 16)
                S("gpsimd", None, g_eo)

        def g_ag(g):
            cC.inc(g.collective_compute(
                "AllGather", ALU.bypass, replica_groups=RG,
                ins=[self.ap("eo_own").rearrange("a b c -> (a b) c").opt()],
                outs=[self.ap("eo_all").rearrange("a b c -> (a b) c").opt()]))
        S("gpsimd", None, g_ag)
        S("gpsimd", None, lambda g: g.wait_ge(cC.sem, cC.n))

    def ffn_C(self, block, ps):
        dY, dS, dG = self.dY, self.dS, self.dG
        cT, cV, cA, cG, cC = self.cT, self.cV, self.cA, self.cG, self.cC

        zt = self.sb("c_zt", (128, DIM), F32, stack=ps)
        cstt = self.sb("c_cst", (1, 8), F32, stack=ps)
        ccb2 = self.sb("c_ccb", (1, 2), BF16, stack=ps)
        onesr = self.sb("c_onesr", (1, 128), BF16, stack=ps)
        smc = self.sb("c_smc", (128, 8), F32, stack=ps)
        ipcol = self.sb("c_ip", (128, 1), F32, stack=ps)
        rowu = self.sb("c_rowu", (128, 1), U32, stack=ps)
        idg = self.sb("c_idg", (128, 256), F32, stack=ps)
        idc = self.sb("c_idc", (128, 2, 16), F32, stack=ps)
        eidx = self.sb("c_eidx", (128, 2), U32, stack=ps)
        eidf = self.sb("c_eidf", (128, 2), F32, stack=ps)
        eosb = self.sb("c_eosb", (128, 2, DIM), BF16, stack=ps)
        scol = self.sb("c_scol", (128, 2), U32, stack=ps)
        scu = self.sb("c_scu", (128, 1), U32, stack=ps)
        scf = self.sb("c_scf", (128, 8), F32, stack=ps)
        psC = self.psum("c_psC", (128, 2, 16), stack=ps)

        S = lambda e, c, f: self.seq(block, e, c, f)

        S("vector", cV, lambda v: (v.memset(zt[:], 0.0), v.memset(onesr[:], 1.0)))

        def g_zero(g):
            accv = self.ap("acc")
            for r0 in range(0, NTOK + 256, 128):
                dG.inc(g.dma_start(out=accv[r0:r0 + 128, :], in_=zt[:]), 16)
            dG.inc(g.dma_start(out=cstt[:], in_=self.ap("cconst")), 16)
        S("gpsimd", None, g_zero)

        def g_iotap(g):
            g.iota(ipcol[:], pattern=[[0, 1]], base=0, channel_multiplier=1,
                   allow_small_or_imprecise_dtypes=True)
        S("gpsimd", cG, g_iotap)

        S("vector", cV, lambda v: v.tensor_copy(ccb2[:], cstt[0:1, 0:2]))
        S("tensor", cT, lambda t_: t_.matmul(psC[:, 0, 0:2], onesr[0:1, :],
                                             ccb2[0:1, :], start=True, stop=True))

        def v_bc(v):
            v.tensor_copy(smc[:, 0:2], psC[:, 0, 0:2])
            v.drain()
            v.scalar_tensor_tensor(smc[:, 2:3], ipcol[:], 4.0, smc[:, 1:2],
                                   ALU.mult, ALU.add)
            v.drain()
            v.tensor_copy(rowu[:], smc[:, 2:3])
        S("vector", cV, v_bc)

        def g_gidx(g):
            dG.inc(g.indirect_dma_start(
                out=idg[:], out_offset=None,
                in_=self.ap("idx_all")[0:128, :],
                in_offset=IndirectOffsetOnAxis(rowu[:], 0),
                bounds_check=63, oob_is_err=False), 16)
        S("gpsimd", None, g_gidx)

        for hh in range(2):
            def t_tri(t_, hh=hh):
                t_.transpose(psC[:, hh, :], idg[0:16, hh * 128:(hh + 1) * 128],
                             self.identf[0:16, 0:16])
            S("tensor", cT, t_tri)
        S("vector", cV, lambda v: v.tensor_copy(idc[:], psC[:]))

        for e in range(16):
            def v_ec(v, e=e):
                for hh in range(2):
                    v.tensor_scalar(scf[:, hh:hh + 1], ipcol[:],
                                    float(128 * hh + 1024 * e), None, ALU.add)
                v.drain()
                for hh in range(2):
                    v.scalar_tensor_tensor(eidf[:, hh:hh + 1], smc[:, 1:2], 256.0,
                                           scf[:, hh:hh + 1], ALU.mult, ALU.add)
                v.drain()
                v.tensor_copy(eidx[:], eidf[:])
            S("vector", cV, v_ec)

            def g_geo(g, e=e):
                eof = self.ap("eo_all").rearrange("a b c -> (a b) c")
                for hh in range(2):
                    dG.inc(g.indirect_dma_start(
                        out=eosb[:, hh, :], out_offset=None,
                        in_=eof[0:128, :],
                        in_offset=IndirectOffsetOnAxis(eidx[:, hh:hh + 1], 0)), 16)
            S("gpsimd", None, g_geo)

            def v_sc(v, e=e):
                for hh in range(2):
                    d_ = scf[:, 2 + hh:3 + hh]
                    v.tensor_tensor(d_, idc[:, hh, e:e + 1], smc[:, 0:1], ALU.subtract)
                    v.drain()
                    v.tensor_scalar(scf[:, 4:5], d_, 0.0, None, ALU.is_ge)
                    v.tensor_scalar(scf[:, 5:6], d_, float(NTOK), None, ALU.is_lt)
                    v.drain()
                    v.tensor_tensor(scf[:, 6:7], scf[:, 4:5], scf[:, 5:6], ALU.mult)
                    v.tensor_scalar(scf[:, 7:8], ipcol[:], float(NTOK + 128 * hh),
                                    None, ALU.add)
                    v.drain()
                    v.tensor_scalar(scu[:, 0:1], scf[:, 6:7], 0.0, None, ALU.is_equal)
                    v.drain()
                    v.select(scf[:, 2 + hh:3 + hh], scu[:, 0:1], scf[:, 7:8], d_)
                    v.drain()
                v.tensor_copy(scol[:], scf[:, 2:4])
            S("vector", cV, v_sc)

            for hh in range(2):
                def g_sc(g, hh=hh):
                    dG.inc(g.indirect_dma_start(
                        out=self.ap("acc"),
                        out_offset=IndirectOffsetOnAxis(scol[:, hh:hh + 1], 0),
                        in_=eosb[:, hh, :], in_offset=None,
                        compute_op=ALU.add, oob_is_err=True,
                        bounds_check=NTOK + 255), 16)
                S("gpsimd", None, g_sc)

        accin = self.sb("c_acc", (128, 2, DIM), F32, stack=ps)
        x1in = self.sb("c_x1", (128, 2, DIM), F32, stack=ps)
        sq = self.sb("c_sq", (128, DIM), F32, stack=ps)
        stat = self.sb("c_stat", (128, 2, 16), F32, stack=ps)
        xcf = self.sb("c_xcf", (128, 2, DIM), F32, stack=ps)
        lnf = self.sb("c_lnf", (128, 2, DIM), F32, stack=ps)
        outt = self.sb("c_out", (128, 2, DIM), F32, stack=ps)

        f0 = dict(T=cT.n, V=cV.n, A=cA.n, dY=dY.n, dS=dS.n, dG=dG.n)
        a_ld = []

        @block.sync
        def _(y):
            for c, v in zip((cV, cA, cG, dG), (f0["V"], f0["A"], cG.n, f0["dG"])):
                if v > 0:
                    y.wait_ge(c.sem, v)
            accv = self.tm("acc") if False else self.ap("acc").rearrange(
                "(t p) f -> p t f", p=128)
            x1v = self.tm("x1_tok")
            for t in range(NT):
                if t >= 2:
                    y.wait_ge(cV.sem, f0["V"] + 2 * (t - 1))
                dY.inc(y.dma_start(out=accin[:, t % 2, :], in_=accv[:, t, :]), 16)
                a_ld.append(dY.inc(y.dma_start(out=x1in[:, t % 2, :], in_=x1v[:, t, :]), 16))

        @block.vector
        def _(v):
            for t in range(NT):
                v.wait_ge(dY.sem, a_ld[t])
                if t >= 2:
                    v.wait_ge(cA.sem, f0["A"] + (t - 1))
                st = stat[:, t % 2, :]
                at = accin[:, t % 2, :]
                v.tensor_tensor(sq[:], at, at, ALU.mult)
                v.reduce_sum(st[:, 0:1], at, axis=AX)
                v.drain()
                v.reduce_sum(st[:, 1:2], sq[:], axis=AX)
                v.tensor_scalar(st[:, 2:3], st[:, 0:1], 1.0 / DIM, None, ALU.mult)
                v.drain()
                v.tensor_scalar(st[:, 3:4], st[:, 1:2], 1.0 / DIM, None, ALU.mult)
                v.tensor_tensor(st[:, 4:5], st[:, 2:3], st[:, 2:3], ALU.mult)
                v.tensor_scalar(xcf[:, t % 2, :], at, st[:, 2:3], None, ALU.subtract)
                v.drain()
                v.tensor_tensor(st[:, 5:6], st[:, 3:4], st[:, 4:5], ALU.subtract)
                cV.inc(v.drain())
                v.wait_ge(cA.sem, f0["A"] + t + 1)
                if t >= 2:
                    v.wait_ge(dG.sem, f0["dG"] + 16 * (t - 1))
                v.tensor_tensor(outt[:, t % 2, :], x1in[:, t % 2, :],
                                lnf[:, t % 2, :], ALU.add)
                cV.inc(v.drain())

        @block.scalar
        def _(a):
            for t in range(NT):
                a.wait_ge(cV.sem, f0["V"] + 2 * t + 1)
                st = stat[:, t % 2, :]
                a.activation(out=st[:, 6:7], in_=st[:, 5:6], func=AF.Ln,
                             bias=self.konst[:, 0:1])
                a.drain()
                a.activation(out=st[:, 7:8], in_=st[:, 6:7], func=AF.Exp, scale=-0.5)
                a.drain()
                a.activation(out=lnf[:, t % 2, :], in_=xcf[:, t % 2, :],
                             func=AF.Copy, scale=st[:, 7:8])
                cA.inc(a.drain())

        @block.gpsimd
        def _(g):
            for t in range(NT):
                g.wait_ge(cV.sem, f0["V"] + 2 * t + 2)
                dG.inc(g.dma_start(out=self.tm("out")[:, t, :], in_=outt[:, t % 2, :]), 16)
            g.wait_ge(dG.sem, dG.n)
        self.expect(cV, f0["V"] + 2 * NT, "C V")

    def emit_debug(self, block):
        self.dbg = {d for d in self.dbg if d in self.dram}
        if not self.dbg:
            return
        for nm in sorted(self.dbg):
            t = self.dram[nm]
            self.dout("dbg_" + nm, list(t.shape), t.dtype)

        @block.gpsimd
        def _(g):
            for nm in sorted(self.dbg):
                src, dst = self.ap(nm), self.ap("dbg_" + nm)
                ins = g.dma_start(out=dst, in_=src)
                self.dG.inc(ins, 16)
            g.wait_ge(self.dG.sem, self.dG.n)


def shard_inputs(inputs):
    f32 = np.float32
    x = np.ascontiguousarray(np.asarray(inputs["L0_attn_x"], f32).reshape(NC * NTOK, DIM))
    qkv_w = np.ascontiguousarray(np.asarray(inputs["L0_attn_qkv_w"], f32))
    qkv_b = np.ascontiguousarray(
        np.broadcast_to(np.asarray(inputs["L0_attn_qkv_b"], f32)[None, :], (128, 3 * DIM)))

    def tiled(v):
        return np.ascontiguousarray(
            np.broadcast_to(np.tile(np.asarray(v, f32), HEADS)[None, :], (128, DIM)))

    qn_s, qn_b = tiled(inputs["L0_attn_qn_scale"]), tiled(inputs["L0_attn_qn_bias"])
    kn_s, kn_b = tiled(inputs["L0_attn_kn_scale"]), tiled(inputs["L0_attn_kn_bias"])
    G0 = np.ascontiguousarray(np.asarray(inputs["L0_attn_ttt_gate"], f32))
    V0 = np.ascontiguousarray(np.asarray(inputs["L0_attn_ttt_val"], f32))
    PT0 = np.ascontiguousarray(np.asarray(inputs["L0_attn_ttt_proj"], f32).T)
    proj_w = np.ascontiguousarray(np.asarray(inputs["L0_attn_proj_w"], f32))
    proj_b = np.ascontiguousarray(
        np.broadcast_to(np.asarray(inputs["L0_attn_proj_b"], f32)[None, :], (128, DIM)))
    gqT = np.ascontiguousarray(np.asarray(inputs["L0_ffn_gate_query"], f32).T)
    log_temp = np.asarray(inputs["L0_ffn_log_temp"], f32).reshape(1, 1)
    w1 = np.asarray(inputs["L0_ffn_w1"], f32)
    b1 = np.asarray(inputs["L0_ffn_b1"], f32)
    w2 = np.asarray(inputs["L0_ffn_w2"], f32)
    b2 = np.asarray(inputs["L0_ffn_b2"], f32)

    maps = []
    for c in range(NC):
        es = slice(2 * c, 2 * c + 2)
        b1c = np.ascontiguousarray(
            b1[es].reshape(E_OWN, HID // 128, 128).transpose(0, 2, 1))
        b2t = np.ascontiguousarray(
            np.broadcast_to(b2[es][:, None, :], (E_OWN, 128, DIM)))
        cs_ = slice(512 * c, 512 * (c + 1))
        cconst = np.zeros((1, 8), f32)
        cconst[0, 0] = (c % 2) * NTOK
        cconst[0, 1] = c // 2
        cconst[0, 2] = c
        maps.append({
            "x": np.ascontiguousarray(x[c * NTOK:(c + 1) * NTOK]),
            "qkv_w": qkv_w, "qkv_b": qkv_b,
            "qn_scale": qn_s, "qn_bias": qn_b,
            "kn_scale": kn_s, "kn_bias": kn_b,
            "ttt_gate": G0, "ttt_val": V0, "ttt_projT": PT0,
            "proj_w": proj_w, "proj_b": proj_b,
            "gate_queryT": gqT, "log_temp": log_temp,
            "gate_sh": np.ascontiguousarray(G0[:, cs_]),
            "val_sh": np.ascontiguousarray(V0[:, cs_]),
            "proj_sh": np.ascontiguousarray(PT0[:, cs_]),
            "w1": np.ascontiguousarray(w1[es]), "b1c": b1c,
            "w2": np.ascontiguousarray(w2[es]), "b2t": b2t,
            "cconst": cconst,
        })
    return maps


def run(inputs, dbg=(), race=False):
    b = B(dbg=dbg, race=race)
    nc = b.build()
    maps = shard_inputs(inputs)
    res = run_bass_kernel_spmd(nc, maps, core_ids=list(range(NC)))
    return res


def kernel(**inputs):
    res = run(inputs)
    out = np.concatenate([res.results[c]["out"] for c in range(NC)], axis=0)
    return out.reshape(4, 8192, DIM)
